# version 03
# baseline (speedup 1.0000x reference)
import math
import numpy as np
import ml_dtypes

import concourse.bass as bass
import concourse.mybir as mybir
from concourse.bass_utils import run_bass_kernel_spmd

F32 = mybir.dt.float32
BF16 = mybir.dt.bfloat16
AF = mybir.ActivationFunctionType
AX = mybir.AxisListType
OP = mybir.AluOpType

NH = 32
DC = 32
DM = 4096
S = 2048
TOK = 512
NT = 4
NPASS = 2
H2 = 2816
HC = 88
CC = 22
HEAD_SCALE = 1.0 / math.sqrt(128.0)
EPS = 1e-5
NEGBIG = -1.0e9

QT_HALF = [[0, 2, 4, 6, 9, 11, 13, 15], [1, 3, 5, 7, 8, 10, 12, 14]]

_ALIBI_START = 2.0 ** (-(2.0 ** -(math.log2(NH) - 3)))
SLOPES_F32 = [float(np.float32(_ALIBI_START ** (i + 1))) for i in range(NH)]


def prog_T(p, l):
    return (2 * l + 2) if p == 0 else (10 + 2 * l)



class Em:
    COMPUTE = ("pe", "act", "dve")
    QUEUES = ("sy", "gp")

    def __init__(self, nc):
        self.nc = nc
        self.sems = {e: nc.alloc_semaphore(f"sem_{e}") for e in self.COMPUTE}
        self.cnt = {e: 0 for e in self.COMPUTE}
        self.dq = {}
        self.ops = {e: [] for e in self.COMPUTE + self.QUEUES}
        self.buf = {}

    def emit(self, e, fn, reads=(), writes=(), dsem=None):
        if e in self.QUEUES:
            assert dsem is not None
            if dsem not in self.sems:
                self.sems[dsem] = self.nc.alloc_semaphore(f"sem_{dsem}")
                self.cnt[dsem] = 0
                self.dq[dsem] = e
            assert self.dq[dsem] == e
            key, inc = dsem, 16
        else:
            assert dsem is None
            key, inc = e, 1

        waits = {}

        def add(tok):
            if tok is None:
                return
            tk, tv = tok
            if tk == e and e == "pe":
                return
            waits[tk] = max(waits.get(tk, 0), tv)

        for b in reads:
            st = self.buf.get(b)
            if st is not None:
                add(st["w"])
        for b in writes:
            st = self.buf.get(b)
            if st is not None:
                add(st["w"])
                for t in st["r"]:
                    add(t)
        self.cnt[key] += inc
        tok = (key, self.cnt[key])
        self.ops[e].append((waits, fn, key, inc))
        for b in reads:
            self.buf.setdefault(b, {"w": None, "r": []})["r"].append(tok)
        for b in writes:
            self.buf[b] = {"w": tok, "r": []}
        return tok

    def coalesce(self, names, tok):
        for b in names:
            self.buf.setdefault(b, {"w": None, "r": []})["w"] = tok

    def finalize(self):
        nc = self.nc
        blk_cm = nc.Block()
        blk = blk_cm.__enter__()
        eng_attr = {"pe": "tensor", "act": "scalar", "dve": "vector",
                    "sy": "sync", "gp": "gpsimd"}
        for ename in ("sy", "gp", "pe", "act", "dve"):
            ops = self.ops[ename]
            fin = [(d, self.cnt[d]) for d, q in self.dq.items() if q == ename]

            def body(eng, _ops=ops, _fin=fin, _sems=self.sems):
                for waits, fn, key, inc in _ops:
                    for se, v in waits.items():
                        eng.wait_ge(_sems[se], v)
                    inst = fn(eng)
                    inst.then_inc(_sems[key], inc)
                for (d, v) in _fin:
                    eng.wait_ge(_sems[d], v)

            getattr(blk, eng_attr[ename])(body)
        blk_cm.__exit__(None, None, None)



def build_program(debug=False):
    nc = bass.Bass("TRN2", target_bir_lowering=False, debug=False)

    def din(name, shape, dt=F32):
        return nc.dram_tensor(name, list(shape), dt, kind="ExternalInput").ap()

    x_fm = din("x_fm", (DM, S))
    x_own = din("x_own", (DM, 1024))
    wq = din("wq", (NH, 128, DM))
    wkv = din("wkv", (2, 128, DM))
    wo = din("wo", (DC, 128, DM))
    w1 = din("w1", (HC, 128, 1024))
    w3 = din("w3", (HC, 128, 1024))
    w2 = din("w2", (DC, 128, H2))
    a1d = din("a1d", (128, 64))
    a3d = din("a3d", (128, 64))
    a2d = din("a2d", (128, 2 * HC))
    b1d = din("b1d", (2, 128 * HC))
    b3d = din("b3d", (2, 128 * HC))
    b2d = din("b2d", (2, DM))
    krows = din("krows", (8, S), BF16)
    bl2 = din("bl2", (NH, 8, 128), BF16)
    qpos = din("qpos", (128, 8))
    qmask = din("qmask", (128, 512), BF16)
    identd = din("identd", (128, 128), BF16)
    onesd = din("onesd", (128, 1), BF16)
    ones1d = din("ones1d", (1, 128))
    y = nc.dram_tensor("y", [DC, 128, 1024], F32, kind="ExternalOutput").ap()
    dbg = {}
    if debug:
        dbg["k"] = nc.dram_tensor("dbg_k", [128, S], F32, kind="ExternalOutput").ap()
        dbg["q"] = nc.dram_tensor("dbg_q", [128, 6144], F32, kind="ExternalOutput").ap()
        dbg["at"] = nc.dram_tensor("dbg_at", [128, 16384], F32, kind="ExternalOutput").ap()
        dbg["x1"] = nc.dram_tensor("dbg_x1", [128, 16384], F32, kind="ExternalOutput").ap()
        dbg["at2"] = nc.dram_tensor("dbg_at2", [128, 16384], F32, kind="ExternalOutput").ap()
        dbg["x12"] = nc.dram_tensor("dbg_x12", [128, 16384], F32, kind="ExternalOutput").ap()
        dbg["q2"] = nc.dram_tensor("dbg_q2", [128, 6144], F32, kind="ExternalOutput").ap()

    sb = nc.alloc_sbuf_tensor
    arena_a = sb("arena_a", [128, 32 * TOK], BF16)
    arena_b = sb("arena_b", [128, 32 * TOK], BF16)
    arena_q = sb("arena_q", [128, 12 * TOK], BF16)
    P_sb = sb("P_sb", [128, S], BF16)
    P2_sb = sb("P2_sb", [128, S], BF16)
    PTS = sb("PTS", [128, S], BF16)
    K_sb = sb("K_sb", [128, S], BF16)
    VT_sb = sb("VT_sb", [128, S], BF16)
    xprime = sb("xprime", [128, 32 * TOK], F32)
    XR = [sb(f"XR{i}", [128, TOK], F32) for i in range(2)]
    XB = [sb(f"XB{i}", [128, TOK], BF16) for i in range(2)]
    WA = [sb(f"WA{i}", [128, 2048], BF16) for i in range(4)]
    WB = [sb(f"WB{i}", [128, 1152], BF16) for i in range(4)]
    KR = sb("KR", [8, S], BF16)
    BL = [sb(f"BL{i}", [8, 128], BF16) for i in range(2)]
    QPOS = sb("QPOS", [128, 8], F32)
    QSQ = [sb(f"QSQ{i}", [128, TOK], BF16) for i in range(2)]
    QM = sb("QM", [128, 512], BF16)
    ID = sb("ID", [128, 128], BF16)
    ONEB = sb("ONEB", [128, 1], BF16)
    ONE1 = sb("ONE1", [1, 128], F32)
    A1 = sb("A1", [128, 64], BF16)
    A3 = sb("A3", [128, 64], BF16)
    A2 = sb("A2", [128, 2 * HC], BF16)
    T1 = sb("T1", [2, TOK], BF16)
    T3 = sb("T3", [2, TOK], BF16)
    T2 = sb("T2", [2, TOK], BF16)
    NB = sb("NB", [128, 4 * NH], F32)
    KN = sb("KN", [1, 1], F32)
    SR2 = sb("SR2", [1, TOK], F32)
    RSP = sb("RSP", [128, 4], F32)
    RS = sb("RS", [128, 1], F32)
    RR = sb("RR", [128, 1], F32)
    GSC = sb("GSC", [128, TOK], BF16)

    ps_big = nc.alloc_psum_tensor("ps_big", [128, 2048], F32)
    P4 = nc.alloc_psum_tensor("P4", [128, 512], F32)
    P5 = nc.alloc_psum_tensor("P5", [128, 512], F32)
    PT0 = nc.alloc_psum_tensor("PT0", [128, 1024], BF16)
    PT1 = nc.alloc_psum_tensor("PT1", [128, 1024], BF16)

    em = Em(nc)
    E = em.emit

    def G(i):
        return ps_big[:, 512 * i:512 * (i + 1)]

    cur = {}

    def XAr(i):
        return cur["XA"][:, TOK * i:TOK * (i + 1)]

    def ABr(i):
        return cur["AB"][:, TOK * i:TOK * (i + 1)]

    def xan(i):
        return f'{cur["xp"]}{i}'

    def abn(i):
        return f'{cur["bp"]}{i}'

    def Qr(i):
        return arena_q[:, TOK * i:TOK * (i + 1)]

    E("sy", lambda e: e.dma_start(out=ID[:], in_=identd), writes=["ID"], dsem="CS")
    E("sy", lambda e: e.dma_start(out=ONEB[:], in_=onesd), writes=["ONEB"], dsem="CS")
    E("sy", lambda e: e.dma_start(out=ONE1[:], in_=ones1d), writes=["ONE1"], dsem="CS")
    E("sy", lambda e: e.dma_start(out=QM[:], in_=qmask), writes=["QM"], dsem="CS")
    E("sy", lambda e: e.dma_start(out=KR[:], in_=krows), writes=["KR"], dsem="CS")
    tk = E("sy", lambda e: e.dma_start(out=QPOS[:], in_=qpos), writes=["QPOS"], dsem="CS")
    em.coalesce(["ID", "ONEB", "ONE1", "QM", "KR", "QPOS"], tk)
    E("gp", lambda e: e.dma_start(out=A1[:], in_=a1d), writes=["LA1"], dsem="LC")
    E("gp", lambda e: e.dma_start(out=A3[:], in_=a3d), writes=["LA3"], dsem="LC")
    tk = E("gp", lambda e: e.dma_start(out=A2[:], in_=a2d), writes=["LA2"], dsem="LC")
    em.coalesce(["LA1", "LA3", "LA2"], tk)

    wa_names = [f"WA{i}" for i in range(4)]
    wb_names = [f"WB{i}" for i in range(4)]

    def load_wa_block(src_ap, slots):
        ncols = src_ap.shape[1]
        half = ncols // 2
        for j, sl in enumerate(slots):
            E("gp",
              lambda e, s=sl, j=j, src=src_ap, h=half:
                  e.dma_start(out=WA[s][:, 0:h], in_=src[:, j * h:(j + 1) * h]),
              writes=[wa_names[sl]], dsem=f"WA{sl}")

    def q_proj_mms(h, p, wslots, part):
        xa = cur["XA"]

        def fn(e, h=h, p=p, ws=wslots, part=part, xa=xa):
            mm = None
            for c in range(8 * part, 8 * part + 8):
                sl = ws[c // 16]
                tile = WA[sl][:, 128 * (c % 16):128 * (c % 16) + 128]
                mm = e.matmul(P4[:], tile, xa[:, TOK * c:TOK * (c + 1)],
                              start=(c == 0), stop=(c == DC - 1))
            return mm
        reads = [wa_names[s] for s in set(wslots)] + [xan(c) for c in range(8 * part, 8 * part + 8)]
        E("pe", fn, reads=reads, writes=(["P4", "P4k"] if part == 0 else ["P4k"]))

    def q_evac(h):
        s = h % 12
        E("act",
          lambda e, s=s: e.activation(Qr(s), P4[:], AF.Copy, bias=0.0, scale=HEAD_SCALE),
          reads=["P4", "P4k"], writes=[f"Q{s}"])

    def negb_a(h):
        s = h % 12
        r = h % 2
        E("act", lambda e, s=s, r=r: e.activation(QSQ[r][:], Qr(s), AF.Square, bias=0.0, scale=1.0),
          reads=[f"Q{s}"], writes=[f"QSQ{r}"])

    def negb_b12(h):
        r = h % 2
        E("pe", lambda e, r=r: e.matmul(P5[0:1, :], ONEB[:], QSQ[r][:], start=True, stop=True),
          reads=[f"QSQ{r}", "ONEB"], writes=["P5", "P5k"])
        E("dve", lambda e: e.tensor_reduce(
            SR2[0:1, 0:4], P5[0:1, 0:512].rearrange("p (g k) -> p g k", k=128),
            axis=AX.X, op=OP.max),
          reads=["P5", "P5k"], writes=["SR2"])
        E("act", lambda e: e.activation(SR2[0:1, 0:4], SR2[0:1, 0:4], AF.Sqrt,
                                        bias=0.0, scale=1.0),
          reads=["SR2"], writes=["SR2"])
        E("dve", lambda e: e.tensor_scalar(SR2[0:1, 0:4], SR2[0:1, 0:4],
                                           KN[0:1, 0:1], -1.0, op0=OP.mult, op1=OP.mult),
          reads=["SR2", "KN"], writes=["SR2"])

    def negb_b3(h, p):
        E("pe", lambda e: e.matmul(P5[:, 0:4], ONE1[:], SR2[0:1, 0:4],
                                   start=True, stop=True),
          reads=["SR2", "ONE1"], writes=["P5", "P5k"])
        E("dve", lambda e, h=h, p=p: e.scalar_tensor_tensor(
            NB[:, 4 * h:4 * h + 4], QPOS[:, 4 * p:4 * p + 4], -SLOPES_F32[h],
            P5[:, 0:4], op0=OP.mult, op1=OP.add),
          reads=["QPOS", "P5", "P5k"], writes=[f"NB{h}"])

    wkv_slots = [0, 1, 2, 3]

    wa_ctr = 0
    for p in range(NPASS):
        if p == 0:
            cur.update(XA=arena_a, xp="A", AB=arena_b, bp="B")
        else:
            cur.update(XA=arena_b, xp="B", AB=arena_a, bp="A")
        for c in range(DC):
            tk = E("gp",
                   lambda e, c=c, p=p, xa=cur["XA"]: e.dma_start(
                       out=xa[:, TOK * c:TOK * (c + 1)],
                       in_=x_own[128 * c:128 * (c + 1), TOK * p:TOK * (p + 1)]),
                   writes=[xan(c)], dsem="XBL")
        em.coalesce([xan(c) for c in range(DC)], tk)

        if p == 0:
            load_wa_block(wkv[0], [0, 1])
            load_wa_block(wkv[1], [2, 3])
            wa_ctr = 2
            for half in range(2):
                for c in range(DC):
                    i = c % 2
                    kb = [abn(2 * i), abn(2 * i + 1)]
                    kvbuf = cur["AB"][:, 1024 * i:1024 * (i + 1)]
                    E("gp",
                      lambda e, c=c, half=half, kvbuf=kvbuf: e.dma_start(
                          out=kvbuf, in_=x_fm[128 * c:128 * (c + 1),
                                              1024 * half:1024 * (half + 1)]),
                      writes=kb, dsem=f"KV{i}")

                    def kv_fn(e, c=c, kvbuf=kvbuf):
                        mm = None
                        kt = WA[c // 16][:, 128 * (c % 16):128 * (c % 16) + 128]
                        vt = WA[2 + c // 16][:, 128 * (c % 16):128 * (c % 16) + 128]
                        for j in range(2):
                            mm = e.matmul(G(j), kt, kvbuf[:, 512 * j:512 * (j + 1)],
                                          start=(c == 0), stop=(c == DC - 1))
                            mm = e.matmul(G(2 + j), vt, kvbuf[:, 512 * j:512 * (j + 1)],
                                          start=(c == 0), stop=(c == DC - 1))
                        return mm
                    E("pe", kv_fn,
                      reads=kb + wa_names,
                      writes=(["G0", "G1", "G2", "G3", "Gk"] if c == 0 else ["Gk"]))
                E("act",
                  lambda e, half=half: e.activation(
                      K_sb[:, 1024 * half:1024 * (half + 1)], ps_big[:, 0:1024],
                      AF.Copy, bias=0.0, scale=1.0),
                  reads=["G0", "G1", "Gk"], writes=["K"])
                E("act",
                  lambda e: e.activation(P_sb[:, 0:1024], ps_big[:, 1024:2048],
                                         AF.Copy, bias=0.0, scale=1.0),
                  reads=["G2", "G3", "Gk"], writes=["P0"])

                def vtr_fn(e, half=half):
                    mm = None
                    for k in range(8):
                        mm = e.transpose(PT0[:, 128 * k:128 * (k + 1)],
                                         P_sb[:, 128 * k:128 * (k + 1)], ID[:])
                    return mm
                E("pe", vtr_fn, reads=["P0", "ID"], writes=["PT0"])
                E("dve",
                  lambda e, half=half: e.tensor_copy(
                      VT_sb[:, 1024 * half:1024 * (half + 1)], PT0[:]),
                  reads=["PT0"], writes=["VT"])

            for c in range(4):
                E("act", lambda e, c=c: e.activation(QSQ[c % 2][:], K_sb[:, 512 * c:512 * (c + 1)],
                                                     AF.Square, bias=0.0, scale=1.0),
                  reads=["K"], writes=[f"QSQ{c % 2}"])
                E("pe", lambda e, c=c: e.matmul(ps_big[0:1, 512 * c:512 * (c + 1)],
                                                ONEB[:], QSQ[c % 2][:], start=True, stop=True),
                  reads=[f"QSQ{c % 2}", "ONEB"], writes=[f"G{c}"])
            E("dve", lambda e: e.reduce_max(KN[0:1, 0:1], ps_big[0:1, :], axis=AX.X),
              reads=["G0", "G1", "G2", "G3"], writes=["KN"])
            E("act", lambda e: e.activation(KN[0:1, 0:1], KN[0:1, 0:1], AF.Sqrt,
                                            bias=0.0, scale=1.0),
              reads=["KN"], writes=["KN"])
            E("dve", lambda e: e.tensor_scalar_mul(KN[0:1, 0:1], KN[0:1, 0:1], 1.0005),
              reads=["KN"], writes=["KN"])
            if debug:
                E("gp", lambda e: e.dma_start(out=dbg["k"], in_=K_sb[:]), reads=["K"], dsem="DBGK")

        nbq = []
        for h in range(12):
            blk_src = wq[h]
            slots = [(2 * wa_ctr) % 4, (2 * wa_ctr + 1) % 4]
            wa_ctr += 1
            load_wa_block(blk_src, slots)
            for part in range(4):
                q_proj_mms(h, p, slots, part)
            if nbq:
                negb_b3(nbq.pop(0), p)
            q_evac(h)
            negb_a(h)
            negb_b12(h)
            nbq.append(h)

        for h in range(NH):
            s = h % 12
            bl = h % 2
            E("sy", lambda e, h=h, bl=bl: e.dma_start(out=BL[bl][:], in_=bl2[h]),
              writes=[f"BL{bl}"], dsem=f"BLD{bl}")

            fill = (1 <= h <= 20)
            if fill:
                fh = h + 11
                fslots = [(2 * wa_ctr) % 4, (2 * wa_ctr + 1) % 4]
                wa_ctr += 1
                load_wa_block(wq[fh], fslots)

            pt_ctr = 0

            def geom(l):
                T = prog_T(p, l)
                L = 128 * T
                nsl = (L + 511) // 512
                return T, L, nsl

            def bank(l, sl):
                return 2 * (l % 2) + sl % 2

            def pbuf(l):
                return (P_sb, "P0") if l % 2 == 0 else (P2_sb, "P1")

            def emit_S_slice(l, sl, s=s, bl=bl):
                T, L, nsl = geom(l)
                c0 = 512 * sl
                c1 = min(512 * (sl + 1), L)
                bk = bank(l, sl)

                def s_fn(e, s=s, l=l, c0=c0, c1=c1, bk=bk, bl=bl):
                    e.matmul(G(bk)[:, 0:c1 - c0],
                             Qr(s)[:, 128 * l:128 * (l + 1)],
                             K_sb[:, c0:c1], start=True, stop=False)
                    return e.matmul(G(bk)[:, 0:c1 - c0],
                                    BL[bl][:], KR[0:8, c0:c1], start=False, stop=True)
                E("pe", s_fn, reads=[f"Q{s}", "K", f"BL{bl}"], writes=[f"G{bk}"])
                if sl == nsl - 1:
                    w = c1 - c0
                    E("dve",
                      lambda e, w=w, p=p, bk=bk: e.tensor_tensor(
                          G(bk)[:, w - 256:w], G(bk)[:, w - 256:w],
                          QM[:, 256 * p:256 * (p + 1)], op=OP.add),
                      reads=["QM", f"G{bk}"], writes=[f"G{bk}"])

            def emit_exp(l, sl, h=h):
                T, L, nsl = geom(l)
                pb, pn = pbuf(l)
                c0 = 512 * sl
                c1 = min(512 * (sl + 1), L)
                bk = bank(l, sl)
                E("act",
                  lambda e, c0=c0, c1=c1, h=h, l=l, sl=sl, bk=bk, pb=pb: e.activation(
                      pb[:, c0:c1], G(bk)[:, 0:c1 - c0], AF.Exp,
                      bias=NB[:, 4 * h + l:4 * h + l + 1], scale=1.0,
                      accum_out=RSP[:, sl:sl + 1]),
                  reads=[f"G{bk}", f"NB{h}"], writes=[pn, f"RSP{sl}"])

            def emit_tile_chain(l):
                T, L, nsl = geom(l)
                pb, pn = pbuf(l)
                E("dve", lambda e, nsl=nsl: e.reduce_sum(RS[:], RSP[:, 0:nsl], axis=AX.X),
                  reads=[f"RSP{i}" for i in range(nsl)], writes=["RS"])
                E("dve", lambda e: e.reciprocal(RR[:], RS[:]), reads=["RS"], writes=["RR"])
                E("dve", lambda e, L=L, pb=pb: e.tensor_scalar_mul(pb[:, 0:L], pb[:, 0:L], RR[:]),
                  reads=[pn, "RR"], writes=[pn])

            def emit_tile_full(l):
                T, L, nsl = geom(l)
                for sl in range(nsl):
                    emit_S_slice(l, sl)
                    emit_exp(l, sl)
                emit_tile_chain(l)

            def emit_tile_start(l):
                T, L, nsl = geom(l)
                for sl in range(min(2, nsl)):
                    emit_S_slice(l, sl)

            def emit_tile_rest(l):
                T, L, nsl = geom(l)
                for sl in range(min(2, nsl)):
                    emit_exp(l, sl)
                for sl in range(2, nsl):
                    emit_S_slice(l, sl)
                    emit_exp(l, sl)
                emit_tile_chain(l)

            def emit_TAV(l):
                nonlocal pt_ctr
                T, L, nsl = geom(l)
                pb, pn = pbuf(l)
                for g in range((T + 7) // 8):
                    k0 = 8 * g
                    k1 = min(8 * g + 8, T)
                    pt = pt_ctr % 2
                    pt_ctr += 1

                    def tr_fn(e, k0=k0, k1=k1, pt=pt, pb=pb):
                        mm = None
                        for k in range(k0, k1):
                            mm = e.transpose((PT0 if pt == 0 else PT1)[:, 128 * (k - k0):128 * (k - k0 + 1)],
                                             pb[:, 128 * k:128 * (k + 1)], ID[:])
                        return mm
                    E("pe", tr_fn, reads=[pn, "ID"], writes=[f"PT{pt}"])
                    E("dve",
                      lambda e, k0=k0, k1=k1, pt=pt: e.tensor_copy(
                          PTS[:, 128 * k0:128 * k1],
                          (PT0 if pt == 0 else PT1)[:, 0:128 * (k1 - k0)]),
                      reads=[f"PT{pt}"], writes=["PTS"])

                def av_fn(e, T=T, l=l):
                    mm = None
                    for kt in range(T):
                        mm = e.matmul(P5[:, 128 * l:128 * (l + 1)],
                                      VT_sb[:, 128 * kt:128 * (kt + 1)],
                                      PTS[:, 128 * kt:128 * (kt + 1)],
                                      start=(kt == 0), stop=(kt == T - 1))
                    return mm
                E("pe", av_fn, reads=["PTS", "VT"],
                  writes=(["P5", "P5k"] if l == 0 else ["P5k"]))

            emit_tile_full(0)
            if nbq:
                negb_b3(nbq.pop(0), p)
            emit_tile_full(1)
            emit_tile_start(2)
            emit_TAV(0)
            emit_tile_rest(2)
            if fill:
                q_proj_mms(fh, p, fslots, 0)
            emit_tile_start(3)
            emit_TAV(1)
            emit_tile_rest(3)
            if fill:
                q_proj_mms(fh, p, fslots, 1)
            emit_TAV(2)
            if fill:
                q_proj_mms(fh, p, fslots, 2)
            emit_TAV(3)
            if fill:
                q_proj_mms(fh, p, fslots, 3)
                q_evac(fh)
                negb_a(fh)
            E("act", lambda e, h=h, ab=cur["AB"]: e.activation(
                ab[:, TOK * h:TOK * (h + 1)], P5[:], AF.Copy, bias=0.0, scale=1.0),
              reads=["P5", "P5k"], writes=[abn(h)])
            if fill:
                negb_b12(fh)
                nbq.append(fh)

        if debug:
            E("gp", lambda e, p=p: e.dma_start(out=dbg["q" if p == 0 else "q2"], in_=arena_q[:]),
              reads=[f"Q{i}" for i in range(12)], dsem="DBGQ")
            E("gp", lambda e, p=p, ab=cur["AB"]: e.dma_start(
                out=dbg["at" if p == 0 else "at2"], in_=ab[:]),
              reads=[abn(i) for i in range(NH)], dsem="DBGA")

        stq = []
        def stats_a(c):
            E("gp", lambda e, c=c: e.dma_start(out=XB[c % 2][:],
                                               in_=xprime[:, TOK * c:TOK * (c + 1)]),
              reads=[f"X{c}"], writes=[f"XB{c % 2}"], dsem=f"DXB{c % 2}")
            E("act", lambda e, c=c: e.activation(QSQ[c % 2][:], xprime[:, TOK * c:TOK * (c + 1)],
                                                 AF.Square, bias=0.0, scale=1.0),
              reads=[f"X{c}"], writes=[f"QSQ{c % 2}"])

        def stats_b(c):
            E("pe", lambda e, c=c: e.matmul(ps_big[0:1, 0:512], ONEB[:], XB[c % 2][:],
                                            start=(c == 0), stop=(c == DC - 1)),
              reads=[f"XB{c % 2}", "ONEB"], writes=(["G0", "Gk"] if c == 0 else ["Gk"]))
            E("pe", lambda e, c=c: e.matmul(ps_big[0:1, 512:1024], ONEB[:], QSQ[c % 2][:],
                                            start=(c == 0), stop=(c == DC - 1)),
              reads=[f"QSQ{c % 2}", "ONEB"], writes=(["G1"] if c == 0 else ["Gk"]))

        def ln_rows_and_apply(out_to_ob):
            r0 = XR[0][0:1, :]
            r1 = XR[1][0:1, :]
            E("act", lambda e, r0=r0: e.activation(r0, ps_big[0:1, 0:512], AF.Copy,
                                                   bias=0.0, scale=1.0 / DM),
              reads=["G0", "Gk"], writes=["XR0"])
            E("act", lambda e, r1=r1: e.activation(r1, ps_big[0:1, 512:1024], AF.Copy,
                                                   bias=0.0, scale=1.0 / DM),
              reads=["G1", "Gk"], writes=["XR1"])
            E("act", lambda e, r0=r0: e.activation(SR2[:], r0, AF.Square, bias=0.0, scale=1.0),
              reads=["XR0"], writes=["SR2"])
            E("dve", lambda e, r1=r1: e.tensor_tensor(r1, r1, SR2[:], op=OP.subtract),
              reads=["XR1", "SR2"], writes=["XR1"])
            E("dve", lambda e, r1=r1: e.tensor_scalar_add(r1, r1, EPS),
              reads=["XR1"], writes=["XR1"])
            E("act", lambda e, r1=r1: e.activation(r1, r1, AF.Sqrt, bias=0.0, scale=1.0),
              reads=["XR1"], writes=["XR1"])
            E("dve", lambda e, r1=r1: e.reciprocal(r1, r1), reads=["XR1"], writes=["XR1"])
            E("dve", lambda e, r0=r0, r1=r1: e.tensor_tensor(r0, r0, r1, op=OP.mult),
              reads=["XR0", "XR1"], writes=["XR0"])
            E("pe", lambda e, r1=r1: e.matmul(ps_big[:, 1024:1536], ONE1[:], r1,
                                              start=True, stop=True),
              reads=["XR1", "ONE1"], writes=["G2"])
            E("pe", lambda e, r0=r0: e.matmul(ps_big[:, 1536:2048], ONE1[:], r0,
                                              start=True, stop=True),
              reads=["XR0", "ONE1"], writes=["G3"])
            for c in range(DC):
                if not out_to_ob:
                    E("dve", lambda e, c=c: e.tensor_tensor(
                        xprime[:, TOK * c:TOK * (c + 1)], xprime[:, TOK * c:TOK * (c + 1)],
                        ps_big[:, 1024:1536], op=OP.mult),
                      reads=[f"X{c}", "G2"], writes=[f"X{c}"])
                    E("dve", lambda e, c=c: e.tensor_tensor(
                        xprime[:, TOK * c:TOK * (c + 1)], xprime[:, TOK * c:TOK * (c + 1)],
                        ps_big[:, 1536:2048], op=OP.subtract),
                      reads=[f"X{c}", "G3"], writes=[f"X{c}"])
                else:
                    ob = c % 2
                    E("dve", lambda e, c=c, ob=ob: e.tensor_tensor(
                        XR[ob][:], xprime[:, TOK * c:TOK * (c + 1)],
                        ps_big[:, 1024:1536], op=OP.mult),
                      reads=[f"X{c}", "G2"], writes=[f"XR{ob}"])
                    E("dve", lambda e, c=c, ob=ob: e.tensor_tensor(
                        XR[ob][:], XR[ob][:], ps_big[:, 1536:2048], op=OP.subtract),
                      reads=[f"XR{ob}", "G3"], writes=[f"XR{ob}"])
                    E("sy", lambda e, c=c, ob=ob, p=p: e.dma_start(
                        out=y[c][:, TOK * p:TOK * (p + 1)], in_=XR[ob][:]),
                      reads=[f"XR{ob}"], dsem=f"OUT{ob}")

        for do in range(DC):
            slots = [(2 * wa_ctr) % 4, (2 * wa_ctr + 1) % 4]
            wa_ctr += 1
            load_wa_block(wo[do], slots)
            pso = P4 if do % 2 == 0 else P5
            pn = "P4" if do % 2 == 0 else "P5"

            def wo_fn(e, slots=slots, pso=pso, ab=cur["AB"]):
                mm = None
                for c in range(DC):
                    sl = slots[c // 16]
                    mm = e.matmul(pso[:], WA[sl][:, 128 * (c % 16):128 * (c % 16) + 128],
                                  ab[:, TOK * c:TOK * (c + 1)], start=(c == 0), stop=(c == DC - 1))
                return mm
            E("pe", wo_fn,
              reads=[wa_names[s] for s in set(slots)] + [abn(c) for c in range(DC)],
              writes=[pn])
            E("sy", lambda e, do=do, p=p: e.dma_start(
                out=XR[do % 2][:], in_=x_own[128 * do:128 * (do + 1), TOK * p:TOK * (p + 1)]),
              writes=[f"XR{do % 2}"], dsem=f"RXR{do % 2}")
            E("dve", lambda e, do=do, pso=pso: e.tensor_tensor(
                xprime[:, TOK * do:TOK * (do + 1)], XR[do % 2][:], pso[:], op=OP.add),
              reads=[pn, f"XR{do % 2}"], writes=[f"X{do}"])
            if len(stq) >= 2:
                stats_b(stq.pop(0))
            stats_a(do)
            stq.append(do)

        while stq:
            stats_b(stq.pop(0))
        ln_rows_and_apply(out_to_ob=False)
        if debug:
            E("gp", lambda e, p=p: e.dma_start(out=dbg["x1" if p == 0 else "x12"], in_=xprime[:]),
              reads=[f"X{c}" for c in range(DC)], dsem="DBGX")

        for jc in range(8):
            E("dve", lambda e, jc=jc: e.tensor_tensor(
                XR[0][:], xprime[:, TOK * jc:TOK * (jc + 1)],
                xprime[:, TOK * (8 + jc):TOK * (9 + jc)], op=OP.add),
              reads=[f"X{jc}", f"X{8 + jc}"], writes=["XR0"])
            E("dve", lambda e, jc=jc: e.tensor_tensor(
                XR[0][:], XR[0][:], xprime[:, TOK * (16 + jc):TOK * (17 + jc)], op=OP.add),
              reads=["XR0", f"X{16 + jc}"], writes=["XR0"])
            E("dve", lambda e, jc=jc, xa=cur["XA"]: e.tensor_tensor(
                xa[:, TOK * jc:TOK * (jc + 1)], XR[0][:],
                xprime[:, TOK * (24 + jc):TOK * (25 + jc)], op=OP.add),
              reads=["XR0", f"X{24 + jc}"], writes=[xan(jc)])
        for c in range(DC):
            E("gp", lambda e, c=c: e.dma_start(out=XB[c % 2][:],
                                               in_=xprime[:, TOK * c:TOK * (c + 1)]),
              reads=[f"X{c}"], writes=[f"XB{c % 2}"], dsem=f"DXB{c % 2}")
            E("pe", lambda e, c=c: e.matmul(P4[0:2, :], A1[:, 2 * c:2 * c + 2], XB[c % 2][:],
                                            start=(c == 0), stop=(c == DC - 1)),
              reads=[f"XB{c % 2}", "LA1"], writes=(["P4", "P4k"] if c == 0 else ["P4k"]))
            E("pe", lambda e, c=c: e.matmul(P5[0:2, :], A3[:, 2 * c:2 * c + 2], XB[c % 2][:],
                                            start=(c == 0), stop=(c == DC - 1)),
              reads=[f"XB{c % 2}", "LA3"], writes=(["P5", "P5k"] if c == 0 else ["P5k"]))
        E("act", lambda e: e.activation(T1[:], P4[0:2, :], AF.Copy, bias=0.0, scale=0.5),
          reads=["P4", "P4k"], writes=["T1"])
        E("act", lambda e: e.activation(T3[:], P5[0:2, :], AF.Copy, bias=0.0, scale=0.5),
          reads=["P5", "P5k"], writes=["T3"])

        wb_ctr = 0
        t2q = []

        def emit_t2(hc, grd, gap):
            E("pe", lambda e, hc=hc, gap=gap: e.matmul(
                P4[0:2, :], A2[:, 2 * hc:2 * hc + 2], gap,
                start=(hc == 0), stop=(hc == HC - 1)),
              reads=[grd, "LA2"], writes=(["P4", "P4k"] if hc == 0 else ["P4k"]))

        for hc in range(HC):
            m, cc = hc // CC, hc % CC
            s1 = wb_ctr % 4
            s3 = (wb_ctr + 1) % 4
            wb_ctr += 2
            E("gp", lambda e, hc=hc, s1=s1: e.dma_start(out=WB[s1][:, 0:1024], in_=w1[hc]),
              writes=[wb_names[s1]], dsem=f"WB{s1}")
            E("gp", lambda e, hc=hc, s1=s1: e.dma_start(
                out=WB[s1][0:2, 1024:1152], in_=b1d[:, 128 * hc:128 * (hc + 1)]),
              writes=[wb_names[s1]], dsem=f"WB{s1}")
            E("gp", lambda e, hc=hc, s3=s3: e.dma_start(out=WB[s3][:, 0:1024], in_=w3[hc]),
              writes=[wb_names[s3]], dsem=f"WB{s3}")
            E("gp", lambda e, hc=hc, s3=s3: e.dma_start(
                out=WB[s3][0:2, 1024:1152], in_=b3d[:, 128 * hc:128 * (hc + 1)]),
              writes=[wb_names[s3]], dsem=f"WB{s3}")
            gb = 2 * (hc % 2)
            ub = gb + 1

            def gate_fn(e, hc=hc, s1=s1, gb=gb, xa=cur["XA"]):
                for jc in range(8):
                    e.matmul(G(gb), WB[s1][:, 128 * jc:128 * (jc + 1)],
                             xa[:, TOK * jc:TOK * (jc + 1)],
                             start=(jc == 0), stop=False)
                return e.matmul(G(gb), WB[s1][0:2, 1024:1152], T1[:],
                                start=False, stop=True)
            E("pe", gate_fn,
              reads=[wb_names[s1], "T1"] + [xan(j) for j in range(8)],
              writes=[f"G{gb}"])

            def up_fn(e, hc=hc, s3=s3, ub=ub, xa=cur["XA"]):
                for jc in range(8):
                    e.matmul(G(ub), WB[s3][:, 128 * jc:128 * (jc + 1)],
                             xa[:, TOK * jc:TOK * (jc + 1)],
                             start=(jc == 0), stop=False)
                return e.matmul(G(ub), WB[s3][0:2, 1024:1152], T3[:],
                                start=False, stop=True)
            E("pe", up_fn,
              reads=[wb_names[s3], "T3"] + [xan(j) for j in range(8)],
              writes=[f"G{ub}"])
            if t2q:
                emit_t2(*t2q.pop())
            g1r = 30 + (hc % 2)
            E("act", lambda e, gb=gb, g1r=g1r, xa=cur["XA"]: e.activation(
                xa[:, TOK * g1r:TOK * (g1r + 1)], G(gb), AF.Silu, bias=0.0, scale=1.0),
              reads=[f"G{gb}"], writes=[xan(g1r)])
            gname = xan(8 + cc)
            if m == 0:
                E("dve", lambda e, g1r=g1r, ub=ub, cc=cc, xa=cur["XA"]: e.tensor_tensor(
                    xa[:, TOK * (8 + cc):TOK * (9 + cc)],
                    xa[:, TOK * g1r:TOK * (g1r + 1)], G(ub), op=OP.mult),
                  reads=[xan(g1r), f"G{ub}"], writes=[gname])
                t2q.append((hc, gname, XAr(8 + cc)))
            else:
                E("dve", lambda e, g1r=g1r, ub=ub, xa=cur["XA"]: e.tensor_tensor(
                    GSC[:], xa[:, TOK * g1r:TOK * (g1r + 1)], G(ub), op=OP.mult),
                  reads=[xan(g1r), f"G{ub}"], writes=["GSC"])
                E("dve", lambda e, cc=cc, xa=cur["XA"]: e.tensor_tensor(
                    xa[:, TOK * (8 + cc):TOK * (9 + cc)],
                    xa[:, TOK * (8 + cc):TOK * (9 + cc)], GSC[:], op=OP.add),
                  reads=[gname, "GSC"], writes=[gname])
                t2q.append((hc, "GSC", GSC[:]))
        while t2q:
            emit_t2(*t2q.pop())
        E("act", lambda e: e.activation(T2[:], P4[0:2, :], AF.Copy, bias=0.0, scale=0.5),
          reads=["P4", "P4k"], writes=["T2"])

        for do in range(DC):
            slots = [(2 * wa_ctr) % 4, (2 * wa_ctr + 1) % 4]
            wa_ctr += 1
            for j, sl in enumerate(slots):
                E("gp", lambda e, do=do, j=j, sl=sl: e.dma_start(
                    out=WA[sl][:, 0:1408], in_=w2[do][:, 1408 * j:1408 * (j + 1)]),
                  writes=[wa_names[sl]], dsem=f"WA{sl}")
            E("gp", lambda e, do=do, sl=slots[0]: e.dma_start(
                out=WA[sl][0:2, 1408:1536], in_=b2d[:, 128 * do:128 * (do + 1)]),
              writes=[wa_names[slots[0]]], dsem=f"WA{slots[0]}")
            pso = P4 if do % 2 == 0 else P5
            pn = "P4" if do % 2 == 0 else "P5"

            def down_fn(e, slots=slots, pso=pso, do=do, xa=cur["XA"]):
                for cc in range(CC):
                    sl = slots[cc // 11]
                    e.matmul(pso[:], WA[sl][:, 128 * (cc % 11):128 * (cc % 11) + 128],
                             xa[:, TOK * (8 + cc):TOK * (9 + cc)], start=(cc == 0), stop=False)
                return e.matmul(pso[:], WA[slots[0]][0:2, 1408:1536], T2[:],
                                start=False, stop=True)
            E("pe", down_fn,
              reads=[wa_names[s] for s in set(slots)] + ["T2"] +
                    [xan(8 + cc) for cc in range(CC)],
              writes=[pn, "P4k" if do % 2 == 0 else "P5k"])
            E("dve", lambda e, do=do, pso=pso: e.tensor_tensor(
                xprime[:, TOK * do:TOK * (do + 1)], xprime[:, TOK * do:TOK * (do + 1)],
                pso[:], op=OP.add),
              reads=[pn, f"X{do}"], writes=[f"X{do}"])
            if len(stq) >= 2:
                stats_b(stq.pop(0))
            stats_a(do)
            stq.append(do)

        while stq:
            stats_b(stq.pop(0))
        ln_rows_and_apply(out_to_ob=True)

    em.finalize()
    nc.finalize()
    return nc



def _relayout_blocks(wT, n_out_chunks, K):
    blocks = wT.reshape(K // 128, 128, n_out_chunks, 128)
    blocks = blocks.transpose(2, 1, 0, 3).reshape(n_out_chunks, 128, K)
    return np.ascontiguousarray(blocks)


def _bf(x):
    return np.asarray(x, dtype=ml_dtypes.bfloat16)


def _prep(inputs):
    f32 = np.float32
    X = np.asarray(inputs["L0_attn_hidden_states"], f32)
    Wq = np.asarray(inputs["L0_attn_Wq"], f32)
    Wkv = np.asarray(inputs["L0_attn_Wkv"], f32)
    Wo = np.asarray(inputs["L0_attn_Wo"], f32)
    w1q = np.asarray(inputs["L0_ffn_w1q"], f32)
    a1 = np.asarray(inputs["L0_ffn_a1"], f32)
    b1 = np.asarray(inputs["L0_ffn_b1"], f32)
    w3q = np.asarray(inputs["L0_ffn_w3q"], f32)
    a3 = np.asarray(inputs["L0_ffn_a3"], f32)
    b3 = np.asarray(inputs["L0_ffn_b3"], f32)
    w2q = np.asarray(inputs["L0_ffn_w2q"], f32)
    a2 = np.asarray(inputs["L0_ffn_a2"], f32)
    b2 = np.asarray(inputs["L0_ffn_b2"], f32)

    x_fm = np.ascontiguousarray(X.transpose(0, 2, 1))

    wq_dev = _relayout_blocks(Wq, NH, DM)
    wkv_dev = _relayout_blocks(Wkv, 2, DM)
    wo_dev = _relayout_blocks(Wo, DC, DM)

    w1p = np.zeros((4 * H2, 1024), f32)
    w3p = np.zeros((4 * H2, 1024), f32)
    b1p = np.zeros((4 * H2, 2), f32)
    b3p = np.zeros((4 * H2, 2), f32)
    a2p = np.zeros((2, 4 * H2), f32)
    for m in range(4):
        w1p[m * H2:m * H2 + 2752] = w1q[m * 2752:(m + 1) * 2752]
        w3p[m * H2:m * H2 + 2752] = w3q[m * 2752:(m + 1) * 2752]
        b1p[m * H2:m * H2 + 2752] = b1[m * 2752:(m + 1) * 2752]
        b3p[m * H2:m * H2 + 2752] = b3[m * 2752:(m + 1) * 2752]
        a2p[:, m * H2:m * H2 + 2752] = a2[:, m * 2752:(m + 1) * 2752]
    w1_dev = _relayout_blocks(np.ascontiguousarray(w1p.T), HC, 1024)
    w3_dev = _relayout_blocks(np.ascontiguousarray(w3p.T), HC, 1024)
    w2p = np.zeros((H2, DM), f32)
    w2p[:2752] = w2q.T
    w2_dev = _relayout_blocks(w2p, DC, H2)

    a1_dev = np.ascontiguousarray(a1.T.reshape(DC, 128, 2).transpose(1, 0, 2).reshape(128, 64))
    a3_dev = np.ascontiguousarray(a3.T.reshape(DC, 128, 2).transpose(1, 0, 2).reshape(128, 64))
    a2_dev = np.ascontiguousarray(a2p.T.reshape(HC, 128, 2).transpose(1, 0, 2).reshape(128, 2 * HC))
    b1_dev = np.ascontiguousarray(b1p.T)
    b3_dev = np.ascontiguousarray(b3p.T)
    b2_dev = np.ascontiguousarray(b2.T)

    slopes32 = np.array(SLOPES_F32, np.float64)
    sa = np.zeros((4, NH), np.float64)
    rem = slopes32.copy()
    for a in range(4):
        part = _bf(rem).astype(np.float64)
        sa[a] = part
        rem = rem - part
    k = np.arange(S, dtype=np.float64)
    k_hi, k_lo = np.floor_divide(k, 8), np.mod(k, 8)
    krows_dev = _bf(np.stack([k_hi] * 4 + [k_lo] * 4))
    bl2_dev = np.zeros((NH, 8, 128), np.float64)
    for h in range(NH):
        for a in range(4):
            bl2_dev[h, a, :] = 8.0 * sa[a, h]
            bl2_dev[h, 4 + a, :] = sa[a, h]
    bl2_dev = _bf(bl2_dev)

    qmask_cores = []
    qpos_cores = []
    xown_cores = []
    for core in range(8):
        half = core % 2
        qts = QT_HALF[half]
        qp = np.zeros((128, 8), np.float32)
        for j, t in enumerate(qts):
            qp[:, j] = 128.0 * t + np.arange(128)
        qpos_cores.append(qp)
        qm = np.zeros((128, 512), np.float32)
        for pp in range(2):
            t = qts[4 * pp]
            T = prog_T(pp, 0)
            off = t - (T - 2)
            for j, tt in enumerate(qts[4 * pp:4 * pp + 4]):
                assert tt - (prog_T(pp, j) - 2) == off
            kk = np.arange(256)
            qq = 128 * off + np.arange(128)
            tile = np.where(kk[None, :] <= qq[:, None], 0.0, NEGBIG).astype(np.float32)
            qm[:, 256 * pp:256 * (pp + 1)] = tile
        qmask_cores.append(_bf(qm))
        b = core // 2
        tok_idx = np.concatenate([np.arange(128 * t, 128 * (t + 1)) for t in qts])
        xown_cores.append(np.ascontiguousarray(x_fm[b][:, tok_idx]))

    ident = _bf(np.eye(128, dtype=np.float32))
    onesb = _bf(np.ones((128, 1), np.float32))
    ones1 = np.ones((1, 128), np.float32)

    shared = {
        "wq": wq_dev, "wkv": wkv_dev, "wo": wo_dev,
        "w1": w1_dev, "w3": w3_dev, "w2": w2_dev,
        "a1d": a1_dev, "a3d": a3_dev, "a2d": a2_dev,
        "b1d": b1_dev, "b3d": b3_dev, "b2d": b2_dev,
        "krows": krows_dev, "bl2": bl2_dev,
        "identd": ident, "onesd": onesb, "ones1d": ones1,
    }
    in_maps = []
    for core in range(8):
        m = dict(shared)
        m["x_fm"] = x_fm[core // 2]
        m["x_own"] = xown_cores[core]
        m["qpos"] = qpos_cores[core]
        m["qmask"] = qmask_cores[core]
        in_maps.append(m)
    return in_maps


def _assemble(results):
    out = np.empty((4, S, DM), np.float32)
    for core in range(8):
        half = core % 2
        b = core // 2
        qts = QT_HALF[half]
        tok_idx = np.concatenate([np.arange(128 * t, 128 * (t + 1)) for t in qts])
        yc = np.asarray(results[core]["y"], np.float32).reshape(DM, 1024)
        out[b, tok_idx, :] = yc.T
    return out


_cached = {}


def kernel(**inputs):
    import os
    debug = bool(os.environ.get("KDBG"))
    key = ("nc", debug)
    if key not in _cached:
        _cached[key] = build_program(debug=debug)
    nc = _cached[key]
    in_maps = _prep(inputs)
    res = run_bass_kernel_spmd(nc, in_maps, core_ids=list(range(8)))
    if debug:
        kernel.last_results = res.results
    return _assemble(res.results)


# version 04
# speedup vs baseline: 1.2672x; 1.2672x over previous
import math
import numpy as np
import ml_dtypes

import concourse.bass as bass
import concourse.mybir as mybir
from concourse.bass_utils import run_bass_kernel_spmd

F32 = mybir.dt.float32
BF16 = mybir.dt.bfloat16
AF = mybir.ActivationFunctionType
AX = mybir.AxisListType
OP = mybir.AluOpType

NH = 32
DC = 32
DM = 4096
S = 2048
TOK = 512
NT = 4
NPASS = 2
H2 = 2816
HC = 88
CC = 22
HEAD_SCALE = 1.0 / math.sqrt(128.0)
EPS = 1e-5
NEGBIG = -1.0e9

QT_HALF = [[0, 2, 4, 6, 9, 11, 13, 15], [1, 3, 5, 7, 8, 10, 12, 14]]

_ALIBI_START = 2.0 ** (-(2.0 ** -(math.log2(NH) - 3)))
SLOPES_F32 = [float(np.float32(_ALIBI_START ** (i + 1))) for i in range(NH)]


def prog_T(p, l):
    return (2 * l + 2) if p == 0 else (10 + 2 * l)



class Em:
    COMPUTE = ("pe", "act", "dve")
    QUEUES = ("sy", "gp")

    def __init__(self, nc):
        self.nc = nc
        self.sems = {e: nc.alloc_semaphore(f"sem_{e}") for e in self.COMPUTE}
        self.cnt = {e: 0 for e in self.COMPUTE}
        self.dq = {}
        self.ops = {e: [] for e in self.COMPUTE + self.QUEUES}
        self.buf = {}

    def emit(self, e, fn, reads=(), writes=(), dsem=None):
        if e in self.QUEUES:
            assert dsem is not None
            if dsem not in self.sems:
                self.sems[dsem] = self.nc.alloc_semaphore(f"sem_{dsem}")
                self.cnt[dsem] = 0
                self.dq[dsem] = e
            assert self.dq[dsem] == e
            key, inc = dsem, 16
        else:
            assert dsem is None
            key, inc = e, 1

        waits = {}

        def add(tok):
            if tok is None:
                return
            tk, tv = tok
            if tk == e and e == "pe":
                return
            waits[tk] = max(waits.get(tk, 0), tv)

        for b in reads:
            st = self.buf.get(b)
            if st is not None:
                add(st["w"])
        for b in writes:
            st = self.buf.get(b)
            if st is not None:
                add(st["w"])
                for t in st["r"]:
                    add(t)
        self.cnt[key] += inc
        tok = (key, self.cnt[key])
        self.ops[e].append((waits, fn, key, inc))
        for b in reads:
            self.buf.setdefault(b, {"w": None, "r": []})["r"].append(tok)
        for b in writes:
            self.buf[b] = {"w": tok, "r": []}
        return tok

    def coalesce(self, names, tok):
        for b in names:
            self.buf.setdefault(b, {"w": None, "r": []})["w"] = tok

    def finalize(self):
        nc = self.nc
        blk_cm = nc.Block()
        blk = blk_cm.__enter__()
        eng_attr = {"pe": "tensor", "act": "scalar", "dve": "vector",
                    "sy": "sync", "gp": "gpsimd"}
        for ename in ("sy", "gp", "pe", "act", "dve"):
            ops = self.ops[ename]
            fin = [(d, self.cnt[d]) for d, q in self.dq.items() if q == ename]

            def body(eng, _ops=ops, _fin=fin, _sems=self.sems):
                for waits, fn, key, inc in _ops:
                    for se, v in waits.items():
                        eng.wait_ge(_sems[se], v)
                    inst = fn(eng)
                    inst.then_inc(_sems[key], inc)
                for (d, v) in _fin:
                    eng.wait_ge(_sems[d], v)

            getattr(blk, eng_attr[ename])(body)
        blk_cm.__exit__(None, None, None)



def build_program(debug=False):
    nc = bass.Bass("TRN2", target_bir_lowering=False, debug=False)

    def din(name, shape, dt=F32):
        return nc.dram_tensor(name, list(shape), dt, kind="ExternalInput").ap()

    x_fm = din("x_fm", (DM, S))
    x_own = din("x_own", (DM, 1024))
    wq = din("wq", (NH, 128, DM))
    wkv = din("wkv", (2, 128, DM))
    wo = din("wo", (DC, 128, DM))
    w1 = din("w1", (HC, 128, 1024))
    w3 = din("w3", (HC, 128, 1024))
    w2 = din("w2", (DC, 128, H2))
    a1d = din("a1d", (128, 64))
    a3d = din("a3d", (128, 64))
    a2d = din("a2d", (128, 2 * HC))
    b1d = din("b1d", (2, 128 * HC))
    b3d = din("b3d", (2, 128 * HC))
    b2d = din("b2d", (2, DM))
    krows = din("krows", (8, S), BF16)
    bl2 = din("bl2", (NH, 8, 128), BF16)
    qpos = din("qpos", (128, 8))
    qmask = din("qmask", (128, 512), BF16)
    identd = din("identd", (128, 128), BF16)
    onesd = din("onesd", (128, 1), BF16)
    ones1d = din("ones1d", (1, 128))
    y = nc.dram_tensor("y", [DC, 128, 1024], F32, kind="ExternalOutput").ap()
    dbg = {}
    if debug:
        dbg["k"] = nc.dram_tensor("dbg_k", [128, S], F32, kind="ExternalOutput").ap()
        dbg["q"] = nc.dram_tensor("dbg_q", [128, 6144], F32, kind="ExternalOutput").ap()
        dbg["at"] = nc.dram_tensor("dbg_at", [128, 16384], F32, kind="ExternalOutput").ap()
        dbg["x1"] = nc.dram_tensor("dbg_x1", [128, 16384], F32, kind="ExternalOutput").ap()
        dbg["at2"] = nc.dram_tensor("dbg_at2", [128, 16384], F32, kind="ExternalOutput").ap()
        dbg["x12"] = nc.dram_tensor("dbg_x12", [128, 16384], F32, kind="ExternalOutput").ap()
        dbg["q2"] = nc.dram_tensor("dbg_q2", [128, 6144], F32, kind="ExternalOutput").ap()

    sb = nc.alloc_sbuf_tensor
    arena_a = sb("arena_a", [128, 32 * TOK], BF16)
    arena_b = sb("arena_b", [128, 32 * TOK], BF16)
    arena_q = sb("arena_q", [128, 12 * TOK], BF16)
    P_sb = sb("P_sb", [128, S], BF16)
    P2_sb = sb("P2_sb", [128, S], BF16)
    PTS = sb("PTS", [128, S], BF16)
    K_sb = sb("K_sb", [128, S], BF16)
    VT_sb = sb("VT_sb", [128, S], BF16)
    xprime = sb("xprime", [128, 32 * TOK], F32)
    XR = [sb(f"XR{i}", [128, TOK], F32) for i in range(2)]
    XB = [sb(f"XB{i}", [128, TOK], BF16) for i in range(2)]
    WA = [sb(f"WA{i}", [128, 2048], BF16) for i in range(4)]
    WB = [sb(f"WB{i}", [128, 1152], BF16) for i in range(4)]
    KR = sb("KR", [8, S], BF16)
    BL = [sb(f"BL{i}", [8, 128], BF16) for i in range(2)]
    QPOS = sb("QPOS", [128, 8], F32)
    QSQ = [sb(f"QSQ{i}", [128, TOK], BF16) for i in range(2)]
    QM = sb("QM", [128, 512], BF16)
    ID = sb("ID", [128, 128], BF16)
    ONEB = sb("ONEB", [128, 1], BF16)
    ONE1 = sb("ONE1", [1, 128], F32)
    A1 = sb("A1", [128, 64], BF16)
    A3 = sb("A3", [128, 64], BF16)
    A2 = sb("A2", [128, 2 * HC], BF16)
    T1 = sb("T1", [2, TOK], BF16)
    T3 = sb("T3", [2, TOK], BF16)
    T2 = sb("T2", [2, TOK], BF16)
    NB = sb("NB", [128, 4 * NH], F32)
    KN = sb("KN", [1, 1], F32)
    SR2 = sb("SR2", [1, TOK], F32)
    RSP = sb("RSP", [128, 4], F32)
    RS = sb("RS", [128, 1], F32)
    RR = sb("RR", [128, 1], F32)
    GSC = sb("GSC", [128, TOK], BF16)

    ps_big = nc.alloc_psum_tensor("ps_big", [128, 2048], F32)
    P4 = nc.alloc_psum_tensor("P4", [128, 512], F32)
    P5 = nc.alloc_psum_tensor("P5", [128, 512], F32)
    PT0 = nc.alloc_psum_tensor("PT0", [128, 1024], BF16)
    PT1 = nc.alloc_psum_tensor("PT1", [128, 1024], BF16)

    em = Em(nc)
    E = em.emit

    def G(i):
        return ps_big[:, 512 * i:512 * (i + 1)]

    cur = {}

    def XAr(i):
        return cur["XA"][:, TOK * i:TOK * (i + 1)]

    def ABr(i):
        return cur["AB"][:, TOK * i:TOK * (i + 1)]

    def xan(i):
        return f'{cur["xp"]}{i}'

    def abn(i):
        return f'{cur["bp"]}{i}'

    def Qr(i):
        return arena_q[:, TOK * i:TOK * (i + 1)]

    E("sy", lambda e: e.dma_start(out=ID[:], in_=identd), writes=["ID"], dsem="CS")
    E("sy", lambda e: e.dma_start(out=ONEB[:], in_=onesd), writes=["ONEB"], dsem="CS")
    E("sy", lambda e: e.dma_start(out=ONE1[:], in_=ones1d), writes=["ONE1"], dsem="CS")
    E("sy", lambda e: e.dma_start(out=QM[:], in_=qmask), writes=["QM"], dsem="CS")
    E("sy", lambda e: e.dma_start(out=KR[:], in_=krows), writes=["KR"], dsem="CS")
    tk = E("sy", lambda e: e.dma_start(out=QPOS[:], in_=qpos), writes=["QPOS"], dsem="CS")
    em.coalesce(["ID", "ONEB", "ONE1", "QM", "KR", "QPOS"], tk)
    E("gp", lambda e: e.dma_start(out=A1[:], in_=a1d), writes=["LA1"], dsem="LC")
    E("gp", lambda e: e.dma_start(out=A3[:], in_=a3d), writes=["LA3"], dsem="LC")
    tk = E("gp", lambda e: e.dma_start(out=A2[:], in_=a2d), writes=["LA2"], dsem="LC")
    em.coalesce(["LA1", "LA3", "LA2"], tk)

    wa_names = [f"WA{i}" for i in range(4)]
    wb_names = [f"WB{i}" for i in range(4)]

    def load_wa_block(src_ap, slots):
        ncols = src_ap.shape[1]
        half = ncols // 2
        for j, sl in enumerate(slots):
            E("gp",
              lambda e, s=sl, j=j, src=src_ap, h=half:
                  e.dma_start(out=WA[s][:, 0:h], in_=src[:, j * h:(j + 1) * h]),
              writes=[wa_names[sl]], dsem=f"WA{sl}")

    def q_proj_mms(h, p, wslots, part):
        xa = cur["XA"]

        def fn(e, h=h, p=p, ws=wslots, part=part, xa=xa):
            mm = None
            for c in range(8 * part, 8 * part + 8):
                sl = ws[c // 16]
                tile = WA[sl][:, 128 * (c % 16):128 * (c % 16) + 128]
                mm = e.matmul(P4[:], tile, xa[:, TOK * c:TOK * (c + 1)],
                              start=(c == 0), stop=(c == DC - 1))
            return mm
        reads = [wa_names[s] for s in set(wslots)] + [xan(c) for c in range(8 * part, 8 * part + 8)]
        E("pe", fn, reads=reads, writes=(["P4", "P4k"] if part == 0 else ["P4k"]))

    def q_evac(h):
        s = h % 12
        E("act",
          lambda e, s=s: e.activation(Qr(s), P4[:], AF.Copy, bias=0.0, scale=HEAD_SCALE),
          reads=["P4", "P4k"], writes=[f"Q{s}"])

    def negb_a(h):
        s = h % 12
        r = h % 2
        E("act", lambda e, s=s, r=r: e.activation(QSQ[r][:], Qr(s), AF.Square, bias=0.0, scale=1.0),
          reads=[f"Q{s}"], writes=[f"QSQ{r}"])

    def negb_b12(h):
        r = h % 2
        E("pe", lambda e, r=r: e.matmul(P5[0:1, :], ONEB[:], QSQ[r][:], start=True, stop=True),
          reads=[f"QSQ{r}", "ONEB"], writes=["P5", "P5k"])
        E("dve", lambda e: e.tensor_reduce(
            SR2[0:1, 0:4], P5[0:1, 0:512].rearrange("p (g k) -> p g k", k=128),
            axis=AX.X, op=OP.max),
          reads=["P5", "P5k"], writes=["SR2"])
        E("act", lambda e: e.activation(SR2[0:1, 0:4], SR2[0:1, 0:4], AF.Sqrt,
                                        bias=0.0, scale=1.0),
          reads=["SR2"], writes=["SR2"])
        E("dve", lambda e: e.tensor_scalar(SR2[0:1, 0:4], SR2[0:1, 0:4],
                                           KN[0:1, 0:1], -1.0, op0=OP.mult, op1=OP.mult),
          reads=["SR2", "KN"], writes=["SR2"])

    def negb_b3(h, p):
        E("pe", lambda e: e.matmul(P5[:, 0:4], ONE1[:], SR2[0:1, 0:4],
                                   start=True, stop=True),
          reads=["SR2", "ONE1"], writes=["P5", "P5k"])
        E("dve", lambda e, h=h, p=p: e.scalar_tensor_tensor(
            NB[:, 4 * h:4 * h + 4], QPOS[:, 4 * p:4 * p + 4], -SLOPES_F32[h],
            P5[:, 0:4], op0=OP.mult, op1=OP.add),
          reads=["QPOS", "P5", "P5k"], writes=[f"NB{h}"])

    wkv_slots = [0, 1, 2, 3]

    wa_ctr = 0
    for p in range(NPASS):
        if p == 0:
            cur.update(XA=arena_a, xp="A", AB=arena_b, bp="B")
        else:
            cur.update(XA=arena_b, xp="B", AB=arena_a, bp="A")
        for c in range(DC):
            tk = E("gp",
                   lambda e, c=c, p=p, xa=cur["XA"]: e.dma_start(
                       out=xa[:, TOK * c:TOK * (c + 1)],
                       in_=x_own[128 * c:128 * (c + 1), TOK * p:TOK * (p + 1)]),
                   writes=[xan(c)], dsem="XBL")
        em.coalesce([xan(c) for c in range(DC)], tk)

        if p == 0:
            load_wa_block(wkv[0], [0, 1])
            load_wa_block(wkv[1], [2, 3])
            wa_ctr = 2
            for half in range(2):
                for c in range(DC):
                    i = c % 2
                    kb = [abn(2 * i), abn(2 * i + 1)]
                    kvbuf = cur["AB"][:, 1024 * i:1024 * (i + 1)]
                    E("gp",
                      lambda e, c=c, half=half, kvbuf=kvbuf: e.dma_start(
                          out=kvbuf, in_=x_fm[128 * c:128 * (c + 1),
                                              1024 * half:1024 * (half + 1)]),
                      writes=kb, dsem=f"KV{i}")

                    def kv_fn(e, c=c, kvbuf=kvbuf):
                        mm = None
                        kt = WA[c // 16][:, 128 * (c % 16):128 * (c % 16) + 128]
                        vt = WA[2 + c // 16][:, 128 * (c % 16):128 * (c % 16) + 128]
                        for j in range(2):
                            mm = e.matmul(G(j), kt, kvbuf[:, 512 * j:512 * (j + 1)],
                                          start=(c == 0), stop=(c == DC - 1))
                            mm = e.matmul(G(2 + j), vt, kvbuf[:, 512 * j:512 * (j + 1)],
                                          start=(c == 0), stop=(c == DC - 1))
                        return mm
                    E("pe", kv_fn,
                      reads=kb + wa_names,
                      writes=(["G0", "G1", "G2", "G3", "Gk"] if c == 0 else ["Gk"]))
                E("act",
                  lambda e, half=half: e.activation(
                      K_sb[:, 1024 * half:1024 * (half + 1)], ps_big[:, 0:1024],
                      AF.Copy, bias=0.0, scale=1.0),
                  reads=["G0", "G1", "Gk"], writes=["K"])
                E("act",
                  lambda e: e.activation(P_sb[:, 0:1024], ps_big[:, 1024:2048],
                                         AF.Copy, bias=0.0, scale=1.0),
                  reads=["G2", "G3", "Gk"], writes=["P0"])

                def vtr_fn(e, half=half):
                    mm = None
                    for k in range(8):
                        mm = e.transpose(PT0[:, 128 * k:128 * (k + 1)],
                                         P_sb[:, 128 * k:128 * (k + 1)], ID[:])
                    return mm
                E("pe", vtr_fn, reads=["P0", "ID"], writes=["PT0"])
                E("dve",
                  lambda e, half=half: e.tensor_copy(
                      VT_sb[:, 1024 * half:1024 * (half + 1)], PT0[:]),
                  reads=["PT0"], writes=["VT"])

            for c in range(4):
                E("act", lambda e, c=c: e.activation(QSQ[c % 2][:], K_sb[:, 512 * c:512 * (c + 1)],
                                                     AF.Square, bias=0.0, scale=1.0),
                  reads=["K"], writes=[f"QSQ{c % 2}"])
                E("pe", lambda e, c=c: e.matmul(ps_big[0:1, 512 * c:512 * (c + 1)],
                                                ONEB[:], QSQ[c % 2][:], start=True, stop=True),
                  reads=[f"QSQ{c % 2}", "ONEB"], writes=[f"G{c}"])
            E("dve", lambda e: e.reduce_max(KN[0:1, 0:1], ps_big[0:1, :], axis=AX.X),
              reads=["G0", "G1", "G2", "G3"], writes=["KN"])
            E("act", lambda e: e.activation(KN[0:1, 0:1], KN[0:1, 0:1], AF.Sqrt,
                                            bias=0.0, scale=1.0),
              reads=["KN"], writes=["KN"])
            E("dve", lambda e: e.tensor_scalar_mul(KN[0:1, 0:1], KN[0:1, 0:1], 1.0005),
              reads=["KN"], writes=["KN"])
            if debug:
                E("gp", lambda e: e.dma_start(out=dbg["k"], in_=K_sb[:]), reads=["K"], dsem="DBGK")

        nbq = []
        for h in range(12):
            blk_src = wq[h]
            slots = [(2 * wa_ctr) % 4, (2 * wa_ctr + 1) % 4]
            wa_ctr += 1
            load_wa_block(blk_src, slots)
            for part in range(4):
                q_proj_mms(h, p, slots, part)
            if nbq:
                negb_b3(nbq.pop(0), p)
            q_evac(h)
            negb_a(h)
            negb_b12(h)
            nbq.append(h)

        for h in range(NH):
            s = h % 12
            bl = h % 2
            E("sy", lambda e, h=h, bl=bl: e.dma_start(out=BL[bl][:], in_=bl2[h]),
              writes=[f"BL{bl}"], dsem=f"BLD{bl}")

            fill = (1 <= h <= 20)
            if fill:
                fh = h + 11
                fslots = [(2 * wa_ctr) % 4, (2 * wa_ctr + 1) % 4]
                wa_ctr += 1
                load_wa_block(wq[fh], fslots)

            pt_ctr = 0

            def geom(l):
                T = prog_T(p, l)
                L = 128 * T
                nsl = (L + 511) // 512
                return T, L, nsl

            def bank(l, sl):
                return 2 * (l % 2) + sl % 2

            def pbuf(l):
                return (P_sb, "P0") if l % 2 == 0 else (P2_sb, "P1")

            def emit_S_slice(l, sl, s=s, bl=bl):
                T, L, nsl = geom(l)
                c0 = 512 * sl
                c1 = min(512 * (sl + 1), L)
                bk = bank(l, sl)

                def s_fn(e, s=s, l=l, c0=c0, c1=c1, bk=bk, bl=bl):
                    e.matmul(G(bk)[:, 0:c1 - c0],
                             Qr(s)[:, 128 * l:128 * (l + 1)],
                             K_sb[:, c0:c1], start=True, stop=False)
                    return e.matmul(G(bk)[:, 0:c1 - c0],
                                    BL[bl][:], KR[0:8, c0:c1], start=False, stop=True)
                E("pe", s_fn, reads=[f"Q{s}", "K", f"BL{bl}"], writes=[f"G{bk}"])
                if sl == nsl - 1:
                    w = c1 - c0
                    E("dve",
                      lambda e, w=w, p=p, bk=bk: e.tensor_tensor(
                          G(bk)[:, w - 256:w], G(bk)[:, w - 256:w],
                          QM[:, 256 * p:256 * (p + 1)], op=OP.add),
                      reads=["QM", f"G{bk}"], writes=[f"G{bk}"])

            def emit_exp(l, sl, h=h):
                T, L, nsl = geom(l)
                pb, pn = pbuf(l)
                c0 = 512 * sl
                c1 = min(512 * (sl + 1), L)
                bk = bank(l, sl)
                E("act",
                  lambda e, c0=c0, c1=c1, h=h, l=l, sl=sl, bk=bk, pb=pb: e.activation(
                      pb[:, c0:c1], G(bk)[:, 0:c1 - c0], AF.Exp,
                      bias=NB[:, 4 * h + l:4 * h + l + 1], scale=1.0,
                      accum_out=RSP[:, sl:sl + 1]),
                  reads=[f"G{bk}", f"NB{h}"], writes=[pn, f"RSP{sl}"])

            def emit_tile_chain(l):
                T, L, nsl = geom(l)
                pb, pn = pbuf(l)
                E("dve", lambda e, nsl=nsl: e.reduce_sum(RS[:], RSP[:, 0:nsl], axis=AX.X),
                  reads=[f"RSP{i}" for i in range(nsl)], writes=["RS"])
                E("dve", lambda e: e.reciprocal(RR[:], RS[:]), reads=["RS"], writes=["RR"])
                E("dve", lambda e, L=L, pb=pb: e.tensor_scalar_mul(pb[:, 0:L], pb[:, 0:L], RR[:]),
                  reads=[pn, "RR"], writes=[pn])

            def emit_tile_full(l):
                T, L, nsl = geom(l)
                for sl in range(nsl):
                    emit_S_slice(l, sl)
                    emit_exp(l, sl)
                emit_tile_chain(l)

            def emit_tile_start(l):
                T, L, nsl = geom(l)
                for sl in range(min(2, nsl)):
                    emit_S_slice(l, sl)

            def emit_tile_rest(l):
                T, L, nsl = geom(l)
                for sl in range(min(2, nsl)):
                    emit_exp(l, sl)
                for sl in range(2, nsl):
                    emit_S_slice(l, sl)
                    emit_exp(l, sl)
                emit_tile_chain(l)

            def emit_TAV(l):
                nonlocal pt_ctr
                T, L, nsl = geom(l)
                pb, pn = pbuf(l)
                for g in range((T + 7) // 8):
                    k0 = 8 * g
                    k1 = min(8 * g + 8, T)
                    pt = pt_ctr % 2
                    pt_ctr += 1

                    def tr_fn(e, k0=k0, k1=k1, pt=pt, pb=pb):
                        mm = None
                        for k in range(k0, k1):
                            mm = e.transpose((PT0 if pt == 0 else PT1)[:, 128 * (k - k0):128 * (k - k0 + 1)],
                                             pb[:, 128 * k:128 * (k + 1)], ID[:])
                        return mm
                    E("pe", tr_fn, reads=[pn, "ID"], writes=[f"PT{pt}"])
                    E("dve",
                      lambda e, k0=k0, k1=k1, pt=pt: e.tensor_copy(
                          PTS[:, 128 * k0:128 * k1],
                          (PT0 if pt == 0 else PT1)[:, 0:128 * (k1 - k0)]),
                      reads=[f"PT{pt}"], writes=["PTS"])

                def av_fn(e, T=T, l=l):
                    mm = None
                    for kt in range(T):
                        mm = e.matmul(P5[:, 128 * l:128 * (l + 1)],
                                      VT_sb[:, 128 * kt:128 * (kt + 1)],
                                      PTS[:, 128 * kt:128 * (kt + 1)],
                                      start=(kt == 0), stop=(kt == T - 1))
                    return mm
                E("pe", av_fn, reads=["PTS", "VT"],
                  writes=(["P5", "P5k"] if l == 0 else ["P5k"]))

            emit_tile_full(0)
            if nbq:
                negb_b3(nbq.pop(0), p)
            emit_tile_full(1)
            emit_tile_start(2)
            emit_TAV(0)
            emit_tile_rest(2)
            if fill:
                q_proj_mms(fh, p, fslots, 0)
            emit_tile_start(3)
            emit_TAV(1)
            emit_tile_rest(3)
            if fill:
                q_proj_mms(fh, p, fslots, 1)
            emit_TAV(2)
            if fill:
                q_proj_mms(fh, p, fslots, 2)
            emit_TAV(3)
            if fill:
                q_proj_mms(fh, p, fslots, 3)
                q_evac(fh)
                negb_a(fh)
            E("act", lambda e, h=h, ab=cur["AB"]: e.activation(
                ab[:, TOK * h:TOK * (h + 1)], P5[:], AF.Copy, bias=0.0, scale=1.0),
              reads=["P5", "P5k"], writes=[abn(h)])
            if fill:
                negb_b12(fh)
                nbq.append(fh)

        if debug:
            E("gp", lambda e, p=p: e.dma_start(out=dbg["q" if p == 0 else "q2"], in_=arena_q[:]),
              reads=[f"Q{i}" for i in range(12)], dsem="DBGQ")
            E("gp", lambda e, p=p, ab=cur["AB"]: e.dma_start(
                out=dbg["at" if p == 0 else "at2"], in_=ab[:]),
              reads=[abn(i) for i in range(NH)], dsem="DBGA")

        stq = []
        def stats_a(c):
            E("act", lambda e, c=c: e.activation(XB[c % 2][:], xprime[:, TOK * c:TOK * (c + 1)],
                                                 AF.Copy, bias=0.0, scale=1.0),
              reads=[f"X{c}"], writes=[f"XB{c % 2}"])
            E("act", lambda e, c=c: e.activation(QSQ[c % 2][:], xprime[:, TOK * c:TOK * (c + 1)],
                                                 AF.Square, bias=0.0, scale=1.0),
              reads=[f"X{c}"], writes=[f"QSQ{c % 2}"])

        def stats_b(c):
            E("pe", lambda e, c=c: e.matmul(ps_big[0:1, 0:512], ONEB[:], XB[c % 2][:],
                                            start=(c == 0), stop=(c == DC - 1)),
              reads=[f"XB{c % 2}", "ONEB"], writes=(["G0", "Gk"] if c == 0 else ["Gk"]))
            E("pe", lambda e, c=c: e.matmul(ps_big[0:1, 512:1024], ONEB[:], QSQ[c % 2][:],
                                            start=(c == 0), stop=(c == DC - 1)),
              reads=[f"QSQ{c % 2}", "ONEB"], writes=(["G1"] if c == 0 else ["Gk"]))

        def ln_rows_and_apply(out_to_ob):
            r0 = XR[0][0:1, :]
            r1 = XR[1][0:1, :]
            E("act", lambda e, r0=r0: e.activation(r0, ps_big[0:1, 0:512], AF.Copy,
                                                   bias=0.0, scale=1.0 / DM),
              reads=["G0", "Gk"], writes=["XR0"])
            E("act", lambda e, r1=r1: e.activation(r1, ps_big[0:1, 512:1024], AF.Copy,
                                                   bias=0.0, scale=1.0 / DM),
              reads=["G1", "Gk"], writes=["XR1"])
            E("act", lambda e, r0=r0: e.activation(SR2[:], r0, AF.Square, bias=0.0, scale=1.0),
              reads=["XR0"], writes=["SR2"])
            E("dve", lambda e, r1=r1: e.tensor_tensor(r1, r1, SR2[:], op=OP.subtract),
              reads=["XR1", "SR2"], writes=["XR1"])
            E("dve", lambda e, r1=r1: e.tensor_scalar_add(r1, r1, EPS),
              reads=["XR1"], writes=["XR1"])
            E("act", lambda e, r1=r1: e.activation(r1, r1, AF.Sqrt, bias=0.0, scale=1.0),
              reads=["XR1"], writes=["XR1"])
            E("dve", lambda e, r1=r1: e.reciprocal(r1, r1), reads=["XR1"], writes=["XR1"])
            E("dve", lambda e, r0=r0, r1=r1: e.tensor_tensor(r0, r0, r1, op=OP.mult),
              reads=["XR0", "XR1"], writes=["XR0"])
            E("pe", lambda e, r1=r1: e.matmul(ps_big[:, 1024:1536], ONE1[:], r1,
                                              start=True, stop=True),
              reads=["XR1", "ONE1"], writes=["G2"])
            E("pe", lambda e, r0=r0: e.matmul(ps_big[:, 1536:2048], ONE1[:], r0,
                                              start=True, stop=True),
              reads=["XR0", "ONE1"], writes=["G3"])
            for c in range(DC):
                if not out_to_ob:
                    E("dve", lambda e, c=c: e.tensor_tensor(
                        xprime[:, TOK * c:TOK * (c + 1)], xprime[:, TOK * c:TOK * (c + 1)],
                        ps_big[:, 1024:1536], op=OP.mult),
                      reads=[f"X{c}", "G2"], writes=[f"X{c}"])
                    E("dve", lambda e, c=c: e.tensor_tensor(
                        xprime[:, TOK * c:TOK * (c + 1)], xprime[:, TOK * c:TOK * (c + 1)],
                        ps_big[:, 1536:2048], op=OP.subtract),
                      reads=[f"X{c}", "G3"], writes=[f"X{c}"])
                else:
                    ob = c % 2
                    E("dve", lambda e, c=c, ob=ob: e.tensor_tensor(
                        XR[ob][:], xprime[:, TOK * c:TOK * (c + 1)],
                        ps_big[:, 1024:1536], op=OP.mult),
                      reads=[f"X{c}", "G2"], writes=[f"XR{ob}"])
                    E("dve", lambda e, c=c, ob=ob: e.tensor_tensor(
                        XR[ob][:], XR[ob][:], ps_big[:, 1536:2048], op=OP.subtract),
                      reads=[f"XR{ob}", "G3"], writes=[f"XR{ob}"])
                    E("sy", lambda e, c=c, ob=ob, p=p: e.dma_start(
                        out=y[c][:, TOK * p:TOK * (p + 1)], in_=XR[ob][:]),
                      reads=[f"XR{ob}"], dsem=f"OUT{ob}")

        for do in range(DC):
            slots = [(2 * wa_ctr) % 4, (2 * wa_ctr + 1) % 4]
            wa_ctr += 1
            load_wa_block(wo[do], slots)
            pso = P4 if do % 2 == 0 else P5
            pn = "P4" if do % 2 == 0 else "P5"

            def wo_fn(e, slots=slots, pso=pso, ab=cur["AB"]):
                mm = None
                for c in range(DC):
                    sl = slots[c // 16]
                    mm = e.matmul(pso[:], WA[sl][:, 128 * (c % 16):128 * (c % 16) + 128],
                                  ab[:, TOK * c:TOK * (c + 1)], start=(c == 0), stop=(c == DC - 1))
                return mm
            E("pe", wo_fn,
              reads=[wa_names[s] for s in set(slots)] + [abn(c) for c in range(DC)],
              writes=[pn])
            E("sy", lambda e, do=do, p=p: e.dma_start(
                out=XR[do % 2][:], in_=x_own[128 * do:128 * (do + 1), TOK * p:TOK * (p + 1)]),
              writes=[f"XR{do % 2}"], dsem=f"RXR{do % 2}")
            E("dve", lambda e, do=do, pso=pso: e.tensor_tensor(
                xprime[:, TOK * do:TOK * (do + 1)], XR[do % 2][:], pso[:], op=OP.add),
              reads=[pn, f"XR{do % 2}"], writes=[f"X{do}"])
            if len(stq) >= 2:
                stats_b(stq.pop(0))
            stats_a(do)
            stq.append(do)

        while stq:
            stats_b(stq.pop(0))
        ln_rows_and_apply(out_to_ob=False)
        if debug:
            E("gp", lambda e, p=p: e.dma_start(out=dbg["x1" if p == 0 else "x12"], in_=xprime[:]),
              reads=[f"X{c}" for c in range(DC)], dsem="DBGX")

        for jc in range(8):
            E("dve", lambda e, jc=jc: e.tensor_tensor(
                XR[0][:], xprime[:, TOK * jc:TOK * (jc + 1)],
                xprime[:, TOK * (8 + jc):TOK * (9 + jc)], op=OP.add),
              reads=[f"X{jc}", f"X{8 + jc}"], writes=["XR0"])
            E("dve", lambda e, jc=jc: e.tensor_tensor(
                XR[0][:], XR[0][:], xprime[:, TOK * (16 + jc):TOK * (17 + jc)], op=OP.add),
              reads=["XR0", f"X{16 + jc}"], writes=["XR0"])
            E("dve", lambda e, jc=jc, xa=cur["XA"]: e.tensor_tensor(
                xa[:, TOK * jc:TOK * (jc + 1)], XR[0][:],
                xprime[:, TOK * (24 + jc):TOK * (25 + jc)], op=OP.add),
              reads=["XR0", f"X{24 + jc}"], writes=[xan(jc)])
        for c in range(DC):
            E("act", lambda e, c=c: e.activation(XB[c % 2][:], xprime[:, TOK * c:TOK * (c + 1)],
                                                   AF.Copy, bias=0.0, scale=1.0),
              reads=[f"X{c}"], writes=[f"XB{c % 2}"])
            E("pe", lambda e, c=c: e.matmul(P4[0:2, :], A1[:, 2 * c:2 * c + 2], XB[c % 2][:],
                                            start=(c == 0), stop=(c == DC - 1)),
              reads=[f"XB{c % 2}", "LA1"], writes=(["P4", "P4k"] if c == 0 else ["P4k"]))
            E("pe", lambda e, c=c: e.matmul(P5[0:2, :], A3[:, 2 * c:2 * c + 2], XB[c % 2][:],
                                            start=(c == 0), stop=(c == DC - 1)),
              reads=[f"XB{c % 2}", "LA3"], writes=(["P5", "P5k"] if c == 0 else ["P5k"]))
        E("act", lambda e: e.activation(T1[:], P4[0:2, :], AF.Copy, bias=0.0, scale=0.5),
          reads=["P4", "P4k"], writes=["T1"])
        E("act", lambda e: e.activation(T3[:], P5[0:2, :], AF.Copy, bias=0.0, scale=0.5),
          reads=["P5", "P5k"], writes=["T3"])

        wb_ctr = 0
        t2q = []

        def emit_t2(hc, grd, gap):
            E("pe", lambda e, hc=hc, gap=gap: e.matmul(
                P4[0:2, :], A2[:, 2 * hc:2 * hc + 2], gap,
                start=(hc == 0), stop=(hc == HC - 1)),
              reads=[grd, "LA2"], writes=(["P4", "P4k"] if hc == 0 else ["P4k"]))

        for hc in range(HC):
            m, cc = hc // CC, hc % CC
            s1 = wb_ctr % 4
            s3 = (wb_ctr + 1) % 4
            wb_ctr += 2
            E("gp", lambda e, hc=hc, s1=s1: e.dma_start(out=WB[s1][:, 0:1024], in_=w1[hc]),
              writes=[wb_names[s1]], dsem=f"WB{s1}")
            E("gp", lambda e, hc=hc, s1=s1: e.dma_start(
                out=WB[s1][0:2, 1024:1152], in_=b1d[:, 128 * hc:128 * (hc + 1)]),
              writes=[wb_names[s1]], dsem=f"WB{s1}")
            E("gp", lambda e, hc=hc, s3=s3: e.dma_start(out=WB[s3][:, 0:1024], in_=w3[hc]),
              writes=[wb_names[s3]], dsem=f"WB{s3}")
            E("gp", lambda e, hc=hc, s3=s3: e.dma_start(
                out=WB[s3][0:2, 1024:1152], in_=b3d[:, 128 * hc:128 * (hc + 1)]),
              writes=[wb_names[s3]], dsem=f"WB{s3}")
            gb = 2 * (hc % 2)
            ub = gb + 1

            def gate_fn(e, hc=hc, s1=s1, gb=gb, xa=cur["XA"]):
                for jc in range(8):
                    e.matmul(G(gb), WB[s1][:, 128 * jc:128 * (jc + 1)],
                             xa[:, TOK * jc:TOK * (jc + 1)],
                             start=(jc == 0), stop=False)
                return e.matmul(G(gb), WB[s1][0:2, 1024:1152], T1[:],
                                start=False, stop=True)
            E("pe", gate_fn,
              reads=[wb_names[s1], "T1"] + [xan(j) for j in range(8)],
              writes=[f"G{gb}"])

            def up_fn(e, hc=hc, s3=s3, ub=ub, xa=cur["XA"]):
                for jc in range(8):
                    e.matmul(G(ub), WB[s3][:, 128 * jc:128 * (jc + 1)],
                             xa[:, TOK * jc:TOK * (jc + 1)],
                             start=(jc == 0), stop=False)
                return e.matmul(G(ub), WB[s3][0:2, 1024:1152], T3[:],
                                start=False, stop=True)
            E("pe", up_fn,
              reads=[wb_names[s3], "T3"] + [xan(j) for j in range(8)],
              writes=[f"G{ub}"])
            if t2q:
                emit_t2(*t2q.pop())
            g1r = 30 + (hc % 2)
            E("act", lambda e, gb=gb, g1r=g1r, xa=cur["XA"]: e.activation(
                xa[:, TOK * g1r:TOK * (g1r + 1)], G(gb), AF.Silu, bias=0.0, scale=1.0),
              reads=[f"G{gb}"], writes=[xan(g1r)])
            gname = xan(8 + cc)
            if m == 0:
                E("dve", lambda e, g1r=g1r, ub=ub, cc=cc, xa=cur["XA"]: e.tensor_tensor(
                    xa[:, TOK * (8 + cc):TOK * (9 + cc)],
                    xa[:, TOK * g1r:TOK * (g1r + 1)], G(ub), op=OP.mult),
                  reads=[xan(g1r), f"G{ub}"], writes=[gname])
                t2q.append((hc, gname, XAr(8 + cc)))
            else:
                E("dve", lambda e, g1r=g1r, ub=ub, xa=cur["XA"]: e.tensor_tensor(
                    GSC[:], xa[:, TOK * g1r:TOK * (g1r + 1)], G(ub), op=OP.mult),
                  reads=[xan(g1r), f"G{ub}"], writes=["GSC"])
                E("dve", lambda e, cc=cc, xa=cur["XA"]: e.tensor_tensor(
                    xa[:, TOK * (8 + cc):TOK * (9 + cc)],
                    xa[:, TOK * (8 + cc):TOK * (9 + cc)], GSC[:], op=OP.add),
                  reads=[gname, "GSC"], writes=[gname])
                t2q.append((hc, "GSC", GSC[:]))
        while t2q:
            emit_t2(*t2q.pop())
        E("act", lambda e: e.activation(T2[:], P4[0:2, :], AF.Copy, bias=0.0, scale=0.5),
          reads=["P4", "P4k"], writes=["T2"])

        for do in range(DC):
            slots = [(2 * wa_ctr) % 4, (2 * wa_ctr + 1) % 4]
            wa_ctr += 1
            for j, sl in enumerate(slots):
                E("gp", lambda e, do=do, j=j, sl=sl: e.dma_start(
                    out=WA[sl][:, 0:1408], in_=w2[do][:, 1408 * j:1408 * (j + 1)]),
                  writes=[wa_names[sl]], dsem=f"WA{sl}")
            E("gp", lambda e, do=do, sl=slots[0]: e.dma_start(
                out=WA[sl][0:2, 1408:1536], in_=b2d[:, 128 * do:128 * (do + 1)]),
              writes=[wa_names[slots[0]]], dsem=f"WA{slots[0]}")
            pso = P4 if do % 2 == 0 else P5
            pn = "P4" if do % 2 == 0 else "P5"

            def down_fn(e, slots=slots, pso=pso, do=do, xa=cur["XA"]):
                for cc in range(CC):
                    sl = slots[cc // 11]
                    e.matmul(pso[:], WA[sl][:, 128 * (cc % 11):128 * (cc % 11) + 128],
                             xa[:, TOK * (8 + cc):TOK * (9 + cc)], start=(cc == 0), stop=False)
                return e.matmul(pso[:], WA[slots[0]][0:2, 1408:1536], T2[:],
                                start=False, stop=True)
            E("pe", down_fn,
              reads=[wa_names[s] for s in set(slots)] + ["T2"] +
                    [xan(8 + cc) for cc in range(CC)],
              writes=[pn, "P4k" if do % 2 == 0 else "P5k"])
            E("dve", lambda e, do=do, pso=pso: e.tensor_tensor(
                xprime[:, TOK * do:TOK * (do + 1)], xprime[:, TOK * do:TOK * (do + 1)],
                pso[:], op=OP.add),
              reads=[pn, f"X{do}"], writes=[f"X{do}"])
            if len(stq) >= 2:
                stats_b(stq.pop(0))
            stats_a(do)
            stq.append(do)

        while stq:
            stats_b(stq.pop(0))
        ln_rows_and_apply(out_to_ob=True)

    em.finalize()
    nc.finalize()
    return nc



def _relayout_blocks(wT, n_out_chunks, K):
    blocks = wT.reshape(K // 128, 128, n_out_chunks, 128)
    blocks = blocks.transpose(2, 1, 0, 3).reshape(n_out_chunks, 128, K)
    return np.ascontiguousarray(blocks)


def _bf(x):
    return np.asarray(x, dtype=ml_dtypes.bfloat16)


def _prep(inputs):
    f32 = np.float32
    X = np.asarray(inputs["L0_attn_hidden_states"], f32)
    Wq = np.asarray(inputs["L0_attn_Wq"], f32)
    Wkv = np.asarray(inputs["L0_attn_Wkv"], f32)
    Wo = np.asarray(inputs["L0_attn_Wo"], f32)
    w1q = np.asarray(inputs["L0_ffn_w1q"], f32)
    a1 = np.asarray(inputs["L0_ffn_a1"], f32)
    b1 = np.asarray(inputs["L0_ffn_b1"], f32)
    w3q = np.asarray(inputs["L0_ffn_w3q"], f32)
    a3 = np.asarray(inputs["L0_ffn_a3"], f32)
    b3 = np.asarray(inputs["L0_ffn_b3"], f32)
    w2q = np.asarray(inputs["L0_ffn_w2q"], f32)
    a2 = np.asarray(inputs["L0_ffn_a2"], f32)
    b2 = np.asarray(inputs["L0_ffn_b2"], f32)

    x_fm = np.ascontiguousarray(X.transpose(0, 2, 1))

    wq_dev = _relayout_blocks(Wq, NH, DM)
    wkv_dev = _relayout_blocks(Wkv, 2, DM)
    wo_dev = _relayout_blocks(Wo, DC, DM)

    w1p = np.zeros((4 * H2, 1024), f32)
    w3p = np.zeros((4 * H2, 1024), f32)
    b1p = np.zeros((4 * H2, 2), f32)
    b3p = np.zeros((4 * H2, 2), f32)
    a2p = np.zeros((2, 4 * H2), f32)
    for m in range(4):
        w1p[m * H2:m * H2 + 2752] = w1q[m * 2752:(m + 1) * 2752]
        w3p[m * H2:m * H2 + 2752] = w3q[m * 2752:(m + 1) * 2752]
        b1p[m * H2:m * H2 + 2752] = b1[m * 2752:(m + 1) * 2752]
        b3p[m * H2:m * H2 + 2752] = b3[m * 2752:(m + 1) * 2752]
        a2p[:, m * H2:m * H2 + 2752] = a2[:, m * 2752:(m + 1) * 2752]
    w1_dev = _relayout_blocks(np.ascontiguousarray(w1p.T), HC, 1024)
    w3_dev = _relayout_blocks(np.ascontiguousarray(w3p.T), HC, 1024)
    w2p = np.zeros((H2, DM), f32)
    w2p[:2752] = w2q.T
    w2_dev = _relayout_blocks(w2p, DC, H2)

    a1_dev = np.ascontiguousarray(a1.T.reshape(DC, 128, 2).transpose(1, 0, 2).reshape(128, 64))
    a3_dev = np.ascontiguousarray(a3.T.reshape(DC, 128, 2).transpose(1, 0, 2).reshape(128, 64))
    a2_dev = np.ascontiguousarray(a2p.T.reshape(HC, 128, 2).transpose(1, 0, 2).reshape(128, 2 * HC))
    b1_dev = np.ascontiguousarray(b1p.T)
    b3_dev = np.ascontiguousarray(b3p.T)
    b2_dev = np.ascontiguousarray(b2.T)

    slopes32 = np.array(SLOPES_F32, np.float64)
    sa = np.zeros((4, NH), np.float64)
    rem = slopes32.copy()
    for a in range(4):
        part = _bf(rem).astype(np.float64)
        sa[a] = part
        rem = rem - part
    k = np.arange(S, dtype=np.float64)
    k_hi, k_lo = np.floor_divide(k, 8), np.mod(k, 8)
    krows_dev = _bf(np.stack([k_hi] * 4 + [k_lo] * 4))
    bl2_dev = np.zeros((NH, 8, 128), np.float64)
    for h in range(NH):
        for a in range(4):
            bl2_dev[h, a, :] = 8.0 * sa[a, h]
            bl2_dev[h, 4 + a, :] = sa[a, h]
    bl2_dev = _bf(bl2_dev)

    qmask_cores = []
    qpos_cores = []
    xown_cores = []
    for core in range(8):
        half = core % 2
        qts = QT_HALF[half]
        qp = np.zeros((128, 8), np.float32)
        for j, t in enumerate(qts):
            qp[:, j] = 128.0 * t + np.arange(128)
        qpos_cores.append(qp)
        qm = np.zeros((128, 512), np.float32)
        for pp in range(2):
            t = qts[4 * pp]
            T = prog_T(pp, 0)
            off = t - (T - 2)
            for j, tt in enumerate(qts[4 * pp:4 * pp + 4]):
                assert tt - (prog_T(pp, j) - 2) == off
            kk = np.arange(256)
            qq = 128 * off + np.arange(128)
            tile = np.where(kk[None, :] <= qq[:, None], 0.0, NEGBIG).astype(np.float32)
            qm[:, 256 * pp:256 * (pp + 1)] = tile
        qmask_cores.append(_bf(qm))
        b = core // 2
        tok_idx = np.concatenate([np.arange(128 * t, 128 * (t + 1)) for t in qts])
        xown_cores.append(np.ascontiguousarray(x_fm[b][:, tok_idx]))

    ident = _bf(np.eye(128, dtype=np.float32))
    onesb = _bf(np.ones((128, 1), np.float32))
    ones1 = np.ones((1, 128), np.float32)

    shared = {
        "wq": wq_dev, "wkv": wkv_dev, "wo": wo_dev,
        "w1": w1_dev, "w3": w3_dev, "w2": w2_dev,
        "a1d": a1_dev, "a3d": a3_dev, "a2d": a2_dev,
        "b1d": b1_dev, "b3d": b3_dev, "b2d": b2_dev,
        "krows": krows_dev, "bl2": bl2_dev,
        "identd": ident, "onesd": onesb, "ones1d": ones1,
    }
    in_maps = []
    for core in range(8):
        m = dict(shared)
        m["x_fm"] = x_fm[core // 2]
        m["x_own"] = xown_cores[core]
        m["qpos"] = qpos_cores[core]
        m["qmask"] = qmask_cores[core]
        in_maps.append(m)
    return in_maps


def _assemble(results):
    out = np.empty((4, S, DM), np.float32)
    for core in range(8):
        half = core % 2
        b = core // 2
        qts = QT_HALF[half]
        tok_idx = np.concatenate([np.arange(128 * t, 128 * (t + 1)) for t in qts])
        yc = np.asarray(results[core]["y"], np.float32).reshape(DM, 1024)
        out[b, tok_idx, :] = yc.T
    return out


_cached = {}


def kernel(**inputs):
    import os
    debug = bool(os.environ.get("KDBG"))
    key = ("nc", debug)
    if key not in _cached:
        _cached[key] = build_program(debug=debug)
    nc = _cached[key]
    in_maps = _prep(inputs)
    res = run_bass_kernel_spmd(nc, in_maps, core_ids=list(range(8)))
    if debug:
        kernel.last_results = res.results
    return _assemble(res.results)


# version 05
# speedup vs baseline: 1.3066x; 1.0312x over previous
import math
import numpy as np
import ml_dtypes

import concourse.bass as bass
import concourse.mybir as mybir
from concourse.bass_utils import run_bass_kernel_spmd

F32 = mybir.dt.float32
BF16 = mybir.dt.bfloat16
AF = mybir.ActivationFunctionType
AX = mybir.AxisListType
OP = mybir.AluOpType

NH = 32
DC = 32
DM = 4096
S = 2048
TOK = 512
NT = 4
NPASS = 2
H2 = 2816
HC = 88
CC = 22
HEAD_SCALE = 1.0 / math.sqrt(128.0)
EPS = 1e-5
NEGBIG = -1.0e9

QT_HALF = [[0, 2, 4, 6, 9, 11, 13, 15], [1, 3, 5, 7, 8, 10, 12, 14]]

_ALIBI_START = 2.0 ** (-(2.0 ** -(math.log2(NH) - 3)))
SLOPES_F32 = [float(np.float32(_ALIBI_START ** (i + 1))) for i in range(NH)]


def prog_T(p, l):
    return (2 * l + 2) if p == 0 else (10 + 2 * l)



class Em:
    COMPUTE = ("pe", "act", "dve")
    QUEUES = ("sy", "gp")

    def __init__(self, nc):
        self.nc = nc
        self.sems = {e: nc.alloc_semaphore(f"sem_{e}") for e in self.COMPUTE}
        self.cnt = {e: 0 for e in self.COMPUTE}
        self.dq = {}
        self.ops = {e: [] for e in self.COMPUTE + self.QUEUES}
        self.buf = {}
        self.last_wait = {}

    def emit(self, e, fn, reads=(), writes=(), dsem=None):
        if e in self.QUEUES:
            assert dsem is not None
            if dsem not in self.sems:
                self.sems[dsem] = self.nc.alloc_semaphore(f"sem_{dsem}")
                self.cnt[dsem] = 0
                self.dq[dsem] = e
            assert self.dq[dsem] == e
            key, inc = dsem, 16
        else:
            assert dsem is None
            key, inc = e, 1

        waits = {}

        def add(tok):
            if tok is None:
                return
            tk, tv = tok
            if tk == e and e == "pe":
                return
            waits[tk] = max(waits.get(tk, 0), tv)

        for b in reads:
            st = self.buf.get(b)
            if st is not None:
                add(st["w"])
        for b in writes:
            st = self.buf.get(b)
            if st is not None:
                add(st["w"])
                for t in st["r"]:
                    add(t)
        lw = self.last_wait.setdefault(e, {})
        waits = {k: v for k, v in waits.items() if lw.get(k, -1) < v}
        for k, v in waits.items():
            lw[k] = v
        self.cnt[key] += inc
        tok = (key, self.cnt[key])
        self.ops[e].append((waits, fn, key, inc))
        for b in reads:
            self.buf.setdefault(b, {"w": None, "r": []})["r"].append(tok)
        for b in writes:
            self.buf[b] = {"w": tok, "r": []}
        return tok

    def coalesce(self, names, tok):
        for b in names:
            self.buf.setdefault(b, {"w": None, "r": []})["w"] = tok

    def finalize(self):
        nc = self.nc
        blk_cm = nc.Block()
        blk = blk_cm.__enter__()
        eng_attr = {"pe": "tensor", "act": "scalar", "dve": "vector",
                    "sy": "sync", "gp": "gpsimd"}
        for ename in ("sy", "gp", "pe", "act", "dve"):
            ops = self.ops[ename]
            fin = [(d, self.cnt[d]) for d, q in self.dq.items() if q == ename]

            def body(eng, _ops=ops, _fin=fin, _sems=self.sems):
                for waits, fn, key, inc in _ops:
                    for se, v in waits.items():
                        eng.wait_ge(_sems[se], v)
                    inst = fn(eng)
                    inst.then_inc(_sems[key], inc)
                for (d, v) in _fin:
                    eng.wait_ge(_sems[d], v)

            getattr(blk, eng_attr[ename])(body)
        blk_cm.__exit__(None, None, None)



def build_program(debug=False):
    nc = bass.Bass("TRN2", target_bir_lowering=False, debug=False)

    def din(name, shape, dt=F32):
        return nc.dram_tensor(name, list(shape), dt, kind="ExternalInput").ap()

    x_fm = din("x_fm", (DM, S))
    x_own = din("x_own", (DM, 1024))
    wq = din("wq", (NH, 128, DM))
    wkv = din("wkv", (2, 128, DM))
    wo = din("wo", (DC, 128, DM))
    w1 = din("w1", (HC, 128, 1024))
    w3 = din("w3", (HC, 128, 1024))
    w2 = din("w2", (DC, 128, H2))
    a1d = din("a1d", (128, 64))
    a3d = din("a3d", (128, 64))
    a2d = din("a2d", (128, 2 * HC))
    b1d = din("b1d", (2, 128 * HC))
    b3d = din("b3d", (2, 128 * HC))
    b2d = din("b2d", (2, DM))
    krows = din("krows", (8, S), BF16)
    bl2 = din("bl2", (NH, 8, 128), BF16)
    qpos = din("qpos", (128, 8))
    qmask = din("qmask", (128, 512), BF16)
    identd = din("identd", (128, 128), BF16)
    onesd = din("onesd", (128, 1), BF16)
    ones1d = din("ones1d", (1, 128))
    y = nc.dram_tensor("y", [DC, 128, 1024], F32, kind="ExternalOutput").ap()
    dbg = {}
    if debug:
        dbg["k"] = nc.dram_tensor("dbg_k", [128, S], F32, kind="ExternalOutput").ap()
        dbg["q"] = nc.dram_tensor("dbg_q", [128, 6144], F32, kind="ExternalOutput").ap()
        dbg["at"] = nc.dram_tensor("dbg_at", [128, 16384], F32, kind="ExternalOutput").ap()
        dbg["x1"] = nc.dram_tensor("dbg_x1", [128, 16384], F32, kind="ExternalOutput").ap()
        dbg["at2"] = nc.dram_tensor("dbg_at2", [128, 16384], F32, kind="ExternalOutput").ap()
        dbg["x12"] = nc.dram_tensor("dbg_x12", [128, 16384], F32, kind="ExternalOutput").ap()
        dbg["q2"] = nc.dram_tensor("dbg_q2", [128, 6144], F32, kind="ExternalOutput").ap()

    sb = nc.alloc_sbuf_tensor
    arena_a = sb("arena_a", [128, 32 * TOK], BF16)
    arena_b = sb("arena_b", [128, 32 * TOK], BF16)
    arena_q = sb("arena_q", [128, 12 * TOK], BF16)
    P_sb = sb("P_sb", [128, S], BF16)
    P2_sb = sb("P2_sb", [128, S], BF16)
    PTS = sb("PTS", [128, S], BF16)
    K_sb = sb("K_sb", [128, S], BF16)
    VT_sb = sb("VT_sb", [128, S], BF16)
    xprime = sb("xprime", [128, 32 * TOK], F32)
    XR = [sb(f"XR{i}", [128, TOK], F32) for i in range(2)]
    XB = [sb(f"XB{i}", [128, TOK], BF16) for i in range(2)]
    WA = [sb(f"WA{i}", [128, 2048], BF16) for i in range(4)]
    WB = [sb(f"WB{i}", [128, 1152], BF16) for i in range(4)]
    KR = sb("KR", [8, S], BF16)
    BL = [sb(f"BL{i}", [8, 128], BF16) for i in range(2)]
    QPOS = sb("QPOS", [128, 8], F32)
    QSQ = [sb(f"QSQ{i}", [128, TOK], BF16) for i in range(2)]
    QM = sb("QM", [128, 512], BF16)
    ID = sb("ID", [128, 128], BF16)
    ONEB = sb("ONEB", [128, 1], BF16)
    ONE1 = sb("ONE1", [1, 128], F32)
    A1 = sb("A1", [128, 64], BF16)
    A3 = sb("A3", [128, 64], BF16)
    A2 = sb("A2", [128, 2 * HC], BF16)
    T1 = sb("T1", [2, TOK], BF16)
    T3 = sb("T3", [2, TOK], BF16)
    T2 = sb("T2", [2, TOK], BF16)
    NB = sb("NB", [128, 4 * NH], F32)
    KN = sb("KN", [1, 1], F32)
    SR2 = sb("SR2", [1, TOK], F32)
    RSP = sb("RSP", [128, 4], F32)
    RS = sb("RS", [128, 1], F32)
    RR = sb("RR", [128, 1], F32)
    GSC = sb("GSC", [128, TOK], BF16)

    ps_big = nc.alloc_psum_tensor("ps_big", [128, 2048], F32)
    P4 = nc.alloc_psum_tensor("P4", [128, 512], F32)
    P5 = nc.alloc_psum_tensor("P5", [128, 512], F32)
    PT0 = nc.alloc_psum_tensor("PT0", [128, 1024], BF16)
    PT1 = nc.alloc_psum_tensor("PT1", [128, 1024], BF16)

    em = Em(nc)
    E = em.emit

    def G(i):
        return ps_big[:, 512 * i:512 * (i + 1)]

    cur = {}

    def XAr(i):
        return cur["XA"][:, TOK * i:TOK * (i + 1)]

    def ABr(i):
        return cur["AB"][:, TOK * i:TOK * (i + 1)]

    def xan(i):
        return f'{cur["xp"]}{i}'

    def abn(i):
        return f'{cur["bp"]}{i}'

    def Qr(i):
        return arena_q[:, TOK * i:TOK * (i + 1)]

    E("sy", lambda e: e.dma_start(out=ID[:], in_=identd), writes=["ID"], dsem="CS")
    E("sy", lambda e: e.dma_start(out=ONEB[:], in_=onesd), writes=["ONEB"], dsem="CS")
    E("sy", lambda e: e.dma_start(out=ONE1[:], in_=ones1d), writes=["ONE1"], dsem="CS")
    E("sy", lambda e: e.dma_start(out=QM[:], in_=qmask), writes=["QM"], dsem="CS")
    E("sy", lambda e: e.dma_start(out=KR[:], in_=krows), writes=["KR"], dsem="CS")
    tk = E("sy", lambda e: e.dma_start(out=QPOS[:], in_=qpos), writes=["QPOS"], dsem="CS")
    em.coalesce(["ID", "ONEB", "ONE1", "QM", "KR", "QPOS"], tk)
    E("gp", lambda e: e.dma_start(out=A1[:], in_=a1d), writes=["LA1"], dsem="LC")
    E("gp", lambda e: e.dma_start(out=A3[:], in_=a3d), writes=["LA3"], dsem="LC")
    tk = E("gp", lambda e: e.dma_start(out=A2[:], in_=a2d), writes=["LA2"], dsem="LC")
    em.coalesce(["LA1", "LA3", "LA2"], tk)

    wa_names = [f"WA{i}" for i in range(4)]
    wb_names = [f"WB{i}" for i in range(4)]

    def load_wa_block(src_ap, slots):
        ncols = src_ap.shape[1]
        half = ncols // 2
        for j, sl in enumerate(slots):
            E("gp",
              lambda e, s=sl, j=j, src=src_ap, h=half:
                  e.dma_start(out=WA[s][:, 0:h], in_=src[:, j * h:(j + 1) * h]),
              writes=[wa_names[sl]], dsem=f"WA{sl}")

    def q_proj_mms(h, p, wslots, part):
        xa = cur["XA"]

        def fn(e, h=h, p=p, ws=wslots, part=part, xa=xa):
            mm = None
            for c in range(8 * part, 8 * part + 8):
                sl = ws[c // 16]
                tile = WA[sl][:, 128 * (c % 16):128 * (c % 16) + 128]
                mm = e.matmul(P4[:], tile, xa[:, TOK * c:TOK * (c + 1)],
                              start=(c == 0), stop=(c == DC - 1))
            return mm
        reads = [wa_names[s] for s in set(wslots)] + [xan(c) for c in range(8 * part, 8 * part + 8)]
        E("pe", fn, reads=reads, writes=(["P4", "P4k"] if part == 0 else ["P4k"]))

    def q_evac(h):
        s = h % 12
        E("act",
          lambda e, s=s: e.activation(Qr(s), P4[:], AF.Copy, bias=0.0, scale=HEAD_SCALE),
          reads=["P4", "P4k"], writes=[f"Q{s}"])

    def negb_a(h):
        s = h % 12
        r = h % 2
        E("act", lambda e, s=s, r=r: e.activation(QSQ[r][:], Qr(s), AF.Square, bias=0.0, scale=1.0),
          reads=[f"Q{s}"], writes=[f"QSQ{r}"])

    def negb_b12(h):
        r = h % 2
        E("pe", lambda e, r=r: e.matmul(P5[0:1, :], ONEB[:], QSQ[r][:], start=True, stop=True),
          reads=[f"QSQ{r}", "ONEB"], writes=["P5", "P5k"])
        E("dve", lambda e: e.tensor_reduce(
            SR2[0:1, 0:4], P5[0:1, 0:512].rearrange("p (g k) -> p g k", k=128),
            axis=AX.X, op=OP.max),
          reads=["P5", "P5k"], writes=["SR2"])
        E("act", lambda e: e.activation(SR2[0:1, 0:4], SR2[0:1, 0:4], AF.Sqrt,
                                        bias=0.0, scale=1.0),
          reads=["SR2"], writes=["SR2"])
        E("dve", lambda e: e.tensor_scalar(SR2[0:1, 0:4], SR2[0:1, 0:4],
                                           KN[0:1, 0:1], -1.0, op0=OP.mult, op1=OP.mult),
          reads=["SR2", "KN"], writes=["SR2"])

    def negb_b3(h, p):
        E("pe", lambda e: e.matmul(P5[:, 0:4], ONE1[:], SR2[0:1, 0:4],
                                   start=True, stop=True),
          reads=["SR2", "ONE1"], writes=["P5", "P5k"])
        E("dve", lambda e, h=h, p=p: e.scalar_tensor_tensor(
            NB[:, 4 * h:4 * h + 4], QPOS[:, 4 * p:4 * p + 4], -SLOPES_F32[h],
            P5[:, 0:4], op0=OP.mult, op1=OP.add),
          reads=["QPOS", "P5", "P5k"], writes=[f"NB{h}"])

    wkv_slots = [0, 1, 2, 3]

    wa_ctr = 0
    for p in range(NPASS):
        if p == 0:
            cur.update(XA=arena_a, xp="A", AB=arena_b, bp="B")
        else:
            cur.update(XA=arena_b, xp="B", AB=arena_a, bp="A")
        for c in range(DC):
            tk = E("gp",
                   lambda e, c=c, p=p, xa=cur["XA"]: e.dma_start(
                       out=xa[:, TOK * c:TOK * (c + 1)],
                       in_=x_own[128 * c:128 * (c + 1), TOK * p:TOK * (p + 1)]),
                   writes=[xan(c)], dsem="XBL")
        em.coalesce([xan(c) for c in range(DC)], tk)

        if p == 0:
            load_wa_block(wkv[0], [0, 1])
            load_wa_block(wkv[1], [2, 3])
            wa_ctr = 2
            for half in range(2):
                for c in range(DC):
                    i = c % 2
                    kb = [abn(2 * i), abn(2 * i + 1)]
                    kvbuf = cur["AB"][:, 1024 * i:1024 * (i + 1)]
                    E("gp",
                      lambda e, c=c, half=half, kvbuf=kvbuf: e.dma_start(
                          out=kvbuf, in_=x_fm[128 * c:128 * (c + 1),
                                              1024 * half:1024 * (half + 1)]),
                      writes=kb, dsem=f"KV{i}")

                    def kv_fn(e, c=c, kvbuf=kvbuf):
                        mm = None
                        kt = WA[c // 16][:, 128 * (c % 16):128 * (c % 16) + 128]
                        vt = WA[2 + c // 16][:, 128 * (c % 16):128 * (c % 16) + 128]
                        for j in range(2):
                            mm = e.matmul(G(j), kt, kvbuf[:, 512 * j:512 * (j + 1)],
                                          start=(c == 0), stop=(c == DC - 1))
                            mm = e.matmul(G(2 + j), vt, kvbuf[:, 512 * j:512 * (j + 1)],
                                          start=(c == 0), stop=(c == DC - 1))
                        return mm
                    E("pe", kv_fn,
                      reads=kb + wa_names,
                      writes=(["G0", "G1", "G2", "G3", "Gk"] if c == 0 else ["Gk"]))
                E("act",
                  lambda e, half=half: e.activation(
                      K_sb[:, 1024 * half:1024 * (half + 1)], ps_big[:, 0:1024],
                      AF.Copy, bias=0.0, scale=1.0),
                  reads=["G0", "G1", "Gk"], writes=["K"])
                E("act",
                  lambda e: e.activation(P_sb[:, 0:1024], ps_big[:, 1024:2048],
                                         AF.Copy, bias=0.0, scale=1.0),
                  reads=["G2", "G3", "Gk"], writes=["P0"])

                def vtr_fn(e, half=half):
                    mm = None
                    for k in range(8):
                        mm = e.transpose(PT0[:, 128 * k:128 * (k + 1)],
                                         P_sb[:, 128 * k:128 * (k + 1)], ID[:])
                    return mm
                E("pe", vtr_fn, reads=["P0", "ID"], writes=["PT0"])
                E("dve",
                  lambda e, half=half: e.tensor_copy(
                      VT_sb[:, 1024 * half:1024 * (half + 1)], PT0[:]),
                  reads=["PT0"], writes=["VT"])

            for c in range(4):
                E("act", lambda e, c=c: e.activation(QSQ[c % 2][:], K_sb[:, 512 * c:512 * (c + 1)],
                                                     AF.Square, bias=0.0, scale=1.0),
                  reads=["K"], writes=[f"QSQ{c % 2}"])
                E("pe", lambda e, c=c: e.matmul(ps_big[0:1, 512 * c:512 * (c + 1)],
                                                ONEB[:], QSQ[c % 2][:], start=True, stop=True),
                  reads=[f"QSQ{c % 2}", "ONEB"], writes=[f"G{c}"])
            E("dve", lambda e: e.reduce_max(KN[0:1, 0:1], ps_big[0:1, :], axis=AX.X),
              reads=["G0", "G1", "G2", "G3"], writes=["KN"])
            E("act", lambda e: e.activation(KN[0:1, 0:1], KN[0:1, 0:1], AF.Sqrt,
                                            bias=0.0, scale=1.0),
              reads=["KN"], writes=["KN"])
            E("dve", lambda e: e.tensor_scalar_mul(KN[0:1, 0:1], KN[0:1, 0:1], 1.0005),
              reads=["KN"], writes=["KN"])
            if debug:
                E("gp", lambda e: e.dma_start(out=dbg["k"], in_=K_sb[:]), reads=["K"], dsem="DBGK")

        nbq = []
        for h in range(12):
            blk_src = wq[h]
            slots = [(2 * wa_ctr) % 4, (2 * wa_ctr + 1) % 4]
            wa_ctr += 1
            load_wa_block(blk_src, slots)
            for part in range(4):
                q_proj_mms(h, p, slots, part)
            if nbq:
                negb_b3(nbq.pop(0), p)
            q_evac(h)
            negb_a(h)
            negb_b12(h)
            nbq.append(h)

        for h in range(NH):
            s = h % 12
            bl = h % 2
            E("sy", lambda e, h=h, bl=bl: e.dma_start(out=BL[bl][:], in_=bl2[h]),
              writes=[f"BL{bl}"], dsem=f"BLD{bl}")

            fill = (1 <= h <= 20)
            if fill:
                fh = h + 11
                fslots = [(2 * wa_ctr) % 4, (2 * wa_ctr + 1) % 4]
                wa_ctr += 1
                load_wa_block(wq[fh], fslots)

            pt_ctr = 0

            def geom(l):
                T = prog_T(p, l)
                L = 128 * T
                nsl = (L + 511) // 512
                return T, L, nsl

            def bank(l, sl):
                return 2 * (l % 2) + sl % 2

            def pbuf(l):
                return (P_sb, "P0") if l % 2 == 0 else (P2_sb, "P1")

            def emit_S_slice(l, sl, s=s, bl=bl):
                T, L, nsl = geom(l)
                c0 = 512 * sl
                c1 = min(512 * (sl + 1), L)
                bk = bank(l, sl)

                def s_fn(e, s=s, l=l, c0=c0, c1=c1, bk=bk, bl=bl):
                    e.matmul(G(bk)[:, 0:c1 - c0],
                             Qr(s)[:, 128 * l:128 * (l + 1)],
                             K_sb[:, c0:c1], start=True, stop=False)
                    return e.matmul(G(bk)[:, 0:c1 - c0],
                                    BL[bl][:], KR[0:8, c0:c1], start=False, stop=True)
                E("pe", s_fn, reads=[f"Q{s}", "K", f"BL{bl}"], writes=[f"G{bk}"])
                if sl == nsl - 1:
                    w = c1 - c0
                    E("dve",
                      lambda e, w=w, p=p, bk=bk: e.tensor_tensor(
                          G(bk)[:, w - 256:w], G(bk)[:, w - 256:w],
                          QM[:, 256 * p:256 * (p + 1)], op=OP.add),
                      reads=["QM", f"G{bk}"], writes=[f"G{bk}"])

            def emit_exp(l, sl, h=h):
                T, L, nsl = geom(l)
                pb, pn = pbuf(l)
                c0 = 512 * sl
                c1 = min(512 * (sl + 1), L)
                bk = bank(l, sl)
                E("act",
                  lambda e, c0=c0, c1=c1, h=h, l=l, sl=sl, bk=bk, pb=pb: e.activation(
                      pb[:, c0:c1], G(bk)[:, 0:c1 - c0], AF.Exp,
                      bias=NB[:, 4 * h + l:4 * h + l + 1], scale=1.0,
                      accum_out=RSP[:, sl:sl + 1]),
                  reads=[f"G{bk}", f"NB{h}"], writes=[pn, f"RSP{sl}"])

            def emit_tile_chain(l):
                T, L, nsl = geom(l)
                pb, pn = pbuf(l)
                E("dve", lambda e, nsl=nsl: e.reduce_sum(RS[:], RSP[:, 0:nsl], axis=AX.X),
                  reads=[f"RSP{i}" for i in range(nsl)], writes=["RS"])
                E("dve", lambda e: e.reciprocal(RR[:], RS[:]), reads=["RS"], writes=["RR"])
                E("dve", lambda e, L=L, pb=pb: e.tensor_scalar_mul(pb[:, 0:L], pb[:, 0:L], RR[:]),
                  reads=[pn, "RR"], writes=[pn])

            def emit_tile_full(l):
                T, L, nsl = geom(l)
                for sl in range(nsl):
                    emit_S_slice(l, sl)
                    emit_exp(l, sl)
                emit_tile_chain(l)

            def emit_tile_start(l):
                T, L, nsl = geom(l)
                for sl in range(min(2, nsl)):
                    emit_S_slice(l, sl)

            def emit_tile_rest(l):
                T, L, nsl = geom(l)
                for sl in range(min(2, nsl)):
                    emit_exp(l, sl)
                for sl in range(2, nsl):
                    emit_S_slice(l, sl)
                    emit_exp(l, sl)
                emit_tile_chain(l)

            def emit_TAV(l):
                nonlocal pt_ctr
                T, L, nsl = geom(l)
                pb, pn = pbuf(l)
                for g in range((T + 7) // 8):
                    k0 = 8 * g
                    k1 = min(8 * g + 8, T)
                    pt = pt_ctr % 2
                    pt_ctr += 1

                    def tr_fn(e, k0=k0, k1=k1, pt=pt, pb=pb):
                        mm = None
                        for k in range(k0, k1):
                            mm = e.transpose((PT0 if pt == 0 else PT1)[:, 128 * (k - k0):128 * (k - k0 + 1)],
                                             pb[:, 128 * k:128 * (k + 1)], ID[:])
                        return mm
                    E("pe", tr_fn, reads=[pn, "ID"], writes=[f"PT{pt}"])
                    E("dve",
                      lambda e, k0=k0, k1=k1, pt=pt: e.tensor_copy(
                          PTS[:, 128 * k0:128 * k1],
                          (PT0 if pt == 0 else PT1)[:, 0:128 * (k1 - k0)]),
                      reads=[f"PT{pt}"], writes=["PTS"])

                def av_fn(e, T=T, l=l):
                    mm = None
                    for kt in range(T):
                        mm = e.matmul(P5[:, 128 * l:128 * (l + 1)],
                                      VT_sb[:, 128 * kt:128 * (kt + 1)],
                                      PTS[:, 128 * kt:128 * (kt + 1)],
                                      start=(kt == 0), stop=(kt == T - 1))
                    return mm
                E("pe", av_fn, reads=["PTS", "VT"],
                  writes=(["P5", "P5k"] if l == 0 else ["P5k"]))

            emit_tile_full(0)
            if nbq:
                negb_b3(nbq.pop(0), p)
            emit_tile_full(1)
            emit_tile_start(2)
            emit_TAV(0)
            emit_tile_rest(2)
            if fill:
                q_proj_mms(fh, p, fslots, 0)
            emit_tile_start(3)
            emit_TAV(1)
            emit_tile_rest(3)
            if fill:
                q_proj_mms(fh, p, fslots, 1)
            emit_TAV(2)
            if fill:
                q_proj_mms(fh, p, fslots, 2)
            emit_TAV(3)
            if fill:
                q_proj_mms(fh, p, fslots, 3)
                q_evac(fh)
                negb_a(fh)
            E("act", lambda e, h=h, ab=cur["AB"]: e.activation(
                ab[:, TOK * h:TOK * (h + 1)], P5[:], AF.Copy, bias=0.0, scale=1.0),
              reads=["P5", "P5k"], writes=[abn(h)])
            if fill:
                negb_b12(fh)
                nbq.append(fh)

        if debug:
            E("gp", lambda e, p=p: e.dma_start(out=dbg["q" if p == 0 else "q2"], in_=arena_q[:]),
              reads=[f"Q{i}" for i in range(12)], dsem="DBGQ")
            E("gp", lambda e, p=p, ab=cur["AB"]: e.dma_start(
                out=dbg["at" if p == 0 else "at2"], in_=ab[:]),
              reads=[abn(i) for i in range(NH)], dsem="DBGA")

        stq = []
        def stats_a(c):
            E("act", lambda e, c=c: e.activation(XB[c % 2][:], xprime[:, TOK * c:TOK * (c + 1)],
                                                 AF.Copy, bias=0.0, scale=1.0),
              reads=[f"X{c}"], writes=[f"XB{c % 2}"])
            E("act", lambda e, c=c: e.activation(QSQ[c % 2][:], xprime[:, TOK * c:TOK * (c + 1)],
                                                 AF.Square, bias=0.0, scale=1.0),
              reads=[f"X{c}"], writes=[f"QSQ{c % 2}"])

        def stats_b(c):
            E("pe", lambda e, c=c: e.matmul(ps_big[0:1, 0:512], ONEB[:], XB[c % 2][:],
                                            start=(c == 0), stop=(c == DC - 1)),
              reads=[f"XB{c % 2}", "ONEB"], writes=(["G0", "Gk"] if c == 0 else ["Gk"]))
            E("pe", lambda e, c=c: e.matmul(ps_big[0:1, 512:1024], ONEB[:], QSQ[c % 2][:],
                                            start=(c == 0), stop=(c == DC - 1)),
              reads=[f"QSQ{c % 2}", "ONEB"], writes=(["G1"] if c == 0 else ["Gk"]))

        def ln_rows_and_apply(out_to_ob):
            r0 = XR[0][0:1, :]
            r1 = XR[1][0:1, :]
            E("act", lambda e, r0=r0: e.activation(r0, ps_big[0:1, 0:512], AF.Copy,
                                                   bias=0.0, scale=1.0 / DM),
              reads=["G0", "Gk"], writes=["XR0"])
            E("act", lambda e, r1=r1: e.activation(r1, ps_big[0:1, 512:1024], AF.Copy,
                                                   bias=0.0, scale=1.0 / DM),
              reads=["G1", "Gk"], writes=["XR1"])
            E("act", lambda e, r0=r0: e.activation(SR2[:], r0, AF.Square, bias=0.0, scale=1.0),
              reads=["XR0"], writes=["SR2"])
            E("dve", lambda e, r1=r1: e.tensor_tensor(r1, r1, SR2[:], op=OP.subtract),
              reads=["XR1", "SR2"], writes=["XR1"])
            E("dve", lambda e, r1=r1: e.tensor_scalar_add(r1, r1, EPS),
              reads=["XR1"], writes=["XR1"])
            E("act", lambda e, r1=r1: e.activation(r1, r1, AF.Sqrt, bias=0.0, scale=1.0),
              reads=["XR1"], writes=["XR1"])
            E("dve", lambda e, r1=r1: e.reciprocal(r1, r1), reads=["XR1"], writes=["XR1"])
            E("dve", lambda e, r0=r0, r1=r1: e.tensor_tensor(r0, r0, r1, op=OP.mult),
              reads=["XR0", "XR1"], writes=["XR0"])
            E("pe", lambda e, r1=r1: e.matmul(ps_big[:, 1024:1536], ONE1[:], r1,
                                              start=True, stop=True),
              reads=["XR1", "ONE1"], writes=["G2"])
            E("pe", lambda e, r0=r0: e.matmul(ps_big[:, 1536:2048], ONE1[:], r0,
                                              start=True, stop=True),
              reads=["XR0", "ONE1"], writes=["G3"])
            for c in range(DC):
                if not out_to_ob:
                    E("dve", lambda e, c=c: e.tensor_tensor(
                        xprime[:, TOK * c:TOK * (c + 1)], xprime[:, TOK * c:TOK * (c + 1)],
                        ps_big[:, 1024:1536], op=OP.mult),
                      reads=[f"X{c}", "G2"], writes=[f"X{c}"])
                    E("dve", lambda e, c=c: e.tensor_tensor(
                        xprime[:, TOK * c:TOK * (c + 1)], xprime[:, TOK * c:TOK * (c + 1)],
                        ps_big[:, 1536:2048], op=OP.subtract),
                      reads=[f"X{c}", "G3"], writes=[f"X{c}"])
                else:
                    ob = c % 2
                    E("dve", lambda e, c=c, ob=ob: e.tensor_tensor(
                        XR[ob][:], xprime[:, TOK * c:TOK * (c + 1)],
                        ps_big[:, 1024:1536], op=OP.mult),
                      reads=[f"X{c}", "G2"], writes=[f"XR{ob}"])
                    E("dve", lambda e, c=c, ob=ob: e.tensor_tensor(
                        XR[ob][:], XR[ob][:], ps_big[:, 1536:2048], op=OP.subtract),
                      reads=[f"XR{ob}", "G3"], writes=[f"XR{ob}"])
                    E("sy", lambda e, c=c, ob=ob, p=p: e.dma_start(
                        out=y[c][:, TOK * p:TOK * (p + 1)], in_=XR[ob][:]),
                      reads=[f"XR{ob}"], dsem=f"OUT{ob}")

        for do in range(DC):
            slots = [(2 * wa_ctr) % 4, (2 * wa_ctr + 1) % 4]
            wa_ctr += 1
            load_wa_block(wo[do], slots)
            pso = [ps_big[:, 1024:1536], ps_big[:, 1536:2048], P4[:], P5[:]][do % 4]
            pn = ["G2", "G3", "P4", "P5"][do % 4]

            def wo_fn(e, slots=slots, pso=pso, ab=cur["AB"]):
                mm = None
                for c in range(DC):
                    sl = slots[c // 16]
                    mm = e.matmul(pso, WA[sl][:, 128 * (c % 16):128 * (c % 16) + 128],
                                  ab[:, TOK * c:TOK * (c + 1)], start=(c == 0), stop=(c == DC - 1))
                return mm
            E("pe", wo_fn,
              reads=[wa_names[s] for s in set(slots)] + [abn(c) for c in range(DC)],
              writes=[pn])
            E("sy", lambda e, do=do, p=p: e.dma_start(
                out=XR[do % 2][:], in_=x_own[128 * do:128 * (do + 1), TOK * p:TOK * (p + 1)]),
              writes=[f"XR{do % 2}"], dsem=f"RXR{do % 2}")
            E("dve", lambda e, do=do, pso=pso: e.tensor_tensor(
                xprime[:, TOK * do:TOK * (do + 1)], XR[do % 2][:], pso, op=OP.add),
              reads=[pn, f"XR{do % 2}"], writes=[f"X{do}"])
            if len(stq) >= 2:
                stats_b(stq.pop(0))
            stats_a(do)
            stq.append(do)

        while stq:
            stats_b(stq.pop(0))
        ln_rows_and_apply(out_to_ob=False)
        if debug:
            E("gp", lambda e, p=p: e.dma_start(out=dbg["x1" if p == 0 else "x12"], in_=xprime[:]),
              reads=[f"X{c}" for c in range(DC)], dsem="DBGX")

        for jc in range(8):
            E("dve", lambda e, jc=jc: e.tensor_tensor(
                XR[0][:], xprime[:, TOK * jc:TOK * (jc + 1)],
                xprime[:, TOK * (8 + jc):TOK * (9 + jc)], op=OP.add),
              reads=[f"X{jc}", f"X{8 + jc}"], writes=["XR0"])
            E("dve", lambda e, jc=jc: e.tensor_tensor(
                XR[0][:], XR[0][:], xprime[:, TOK * (16 + jc):TOK * (17 + jc)], op=OP.add),
              reads=["XR0", f"X{16 + jc}"], writes=["XR0"])
            E("dve", lambda e, jc=jc, xa=cur["XA"]: e.tensor_tensor(
                xa[:, TOK * jc:TOK * (jc + 1)], XR[0][:],
                xprime[:, TOK * (24 + jc):TOK * (25 + jc)], op=OP.add),
              reads=["XR0", f"X{24 + jc}"], writes=[xan(jc)])
        for c in range(DC):
            E("act", lambda e, c=c: e.activation(XB[c % 2][:], xprime[:, TOK * c:TOK * (c + 1)],
                                                   AF.Copy, bias=0.0, scale=1.0),
              reads=[f"X{c}"], writes=[f"XB{c % 2}"])
            E("pe", lambda e, c=c: e.matmul(P4[0:2, :], A1[:, 2 * c:2 * c + 2], XB[c % 2][:],
                                            start=(c == 0), stop=(c == DC - 1)),
              reads=[f"XB{c % 2}", "LA1"], writes=(["P4", "P4k"] if c == 0 else ["P4k"]))
            E("pe", lambda e, c=c: e.matmul(P5[0:2, :], A3[:, 2 * c:2 * c + 2], XB[c % 2][:],
                                            start=(c == 0), stop=(c == DC - 1)),
              reads=[f"XB{c % 2}", "LA3"], writes=(["P5", "P5k"] if c == 0 else ["P5k"]))
        E("act", lambda e: e.activation(T1[:], P4[0:2, :], AF.Copy, bias=0.0, scale=0.5),
          reads=["P4", "P4k"], writes=["T1"])
        E("act", lambda e: e.activation(T3[:], P5[0:2, :], AF.Copy, bias=0.0, scale=0.5),
          reads=["P5", "P5k"], writes=["T3"])

        wb_ctr = 0
        t2q = []

        def emit_t2(hc, grd, gap):
            E("pe", lambda e, hc=hc, gap=gap: e.matmul(
                P4[0:2, :], A2[:, 2 * hc:2 * hc + 2], gap,
                start=(hc == 0), stop=(hc == HC - 1)),
              reads=[grd, "LA2"], writes=(["P4", "P4k"] if hc == 0 else ["P4k"]))

        for hc in range(HC):
            m, cc = hc // CC, hc % CC
            s1 = wb_ctr % 4
            s3 = (wb_ctr + 1) % 4
            wb_ctr += 2
            E("gp", lambda e, hc=hc, s1=s1: e.dma_start(out=WB[s1][:, 0:1024], in_=w1[hc]),
              writes=[wb_names[s1]], dsem=f"WB{s1}")
            E("gp", lambda e, hc=hc, s1=s1: e.dma_start(
                out=WB[s1][0:2, 1024:1152], in_=b1d[:, 128 * hc:128 * (hc + 1)]),
              writes=[wb_names[s1]], dsem=f"WB{s1}")
            E("gp", lambda e, hc=hc, s3=s3: e.dma_start(out=WB[s3][:, 0:1024], in_=w3[hc]),
              writes=[wb_names[s3]], dsem=f"WB{s3}")
            E("gp", lambda e, hc=hc, s3=s3: e.dma_start(
                out=WB[s3][0:2, 1024:1152], in_=b3d[:, 128 * hc:128 * (hc + 1)]),
              writes=[wb_names[s3]], dsem=f"WB{s3}")
            gb = 2 * (hc % 2)
            ub = gb + 1

            def gate_fn(e, hc=hc, s1=s1, gb=gb, xa=cur["XA"]):
                for jc in range(8):
                    e.matmul(G(gb), WB[s1][:, 128 * jc:128 * (jc + 1)],
                             xa[:, TOK * jc:TOK * (jc + 1)],
                             start=(jc == 0), stop=False)
                return e.matmul(G(gb), WB[s1][0:2, 1024:1152], T1[:],
                                start=False, stop=True)
            E("pe", gate_fn,
              reads=[wb_names[s1], "T1"] + [xan(j) for j in range(8)],
              writes=[f"G{gb}"])

            def up_fn(e, hc=hc, s3=s3, ub=ub, xa=cur["XA"]):
                for jc in range(8):
                    e.matmul(G(ub), WB[s3][:, 128 * jc:128 * (jc + 1)],
                             xa[:, TOK * jc:TOK * (jc + 1)],
                             start=(jc == 0), stop=False)
                return e.matmul(G(ub), WB[s3][0:2, 1024:1152], T3[:],
                                start=False, stop=True)
            E("pe", up_fn,
              reads=[wb_names[s3], "T3"] + [xan(j) for j in range(8)],
              writes=[f"G{ub}"])
            if t2q:
                emit_t2(*t2q.pop())
            g1r = 30 + (hc % 2)
            E("act", lambda e, gb=gb, g1r=g1r, xa=cur["XA"]: e.activation(
                xa[:, TOK * g1r:TOK * (g1r + 1)], G(gb), AF.Silu, bias=0.0, scale=1.0),
              reads=[f"G{gb}"], writes=[xan(g1r)])
            gname = xan(8 + cc)
            if m == 0:
                E("dve", lambda e, g1r=g1r, ub=ub, cc=cc, xa=cur["XA"]: e.tensor_tensor(
                    xa[:, TOK * (8 + cc):TOK * (9 + cc)],
                    xa[:, TOK * g1r:TOK * (g1r + 1)], G(ub), op=OP.mult),
                  reads=[xan(g1r), f"G{ub}"], writes=[gname])
                t2q.append((hc, gname, XAr(8 + cc)))
            else:
                E("dve", lambda e, g1r=g1r, ub=ub, xa=cur["XA"]: e.tensor_tensor(
                    GSC[:], xa[:, TOK * g1r:TOK * (g1r + 1)], G(ub), op=OP.mult),
                  reads=[xan(g1r), f"G{ub}"], writes=["GSC"])
                E("dve", lambda e, cc=cc, xa=cur["XA"]: e.tensor_tensor(
                    xa[:, TOK * (8 + cc):TOK * (9 + cc)],
                    xa[:, TOK * (8 + cc):TOK * (9 + cc)], GSC[:], op=OP.add),
                  reads=[gname, "GSC"], writes=[gname])
                t2q.append((hc, "GSC", GSC[:]))
        while t2q:
            emit_t2(*t2q.pop())
        E("act", lambda e: e.activation(T2[:], P4[0:2, :], AF.Copy, bias=0.0, scale=0.5),
          reads=["P4", "P4k"], writes=["T2"])

        for do in range(DC):
            slots = [(2 * wa_ctr) % 4, (2 * wa_ctr + 1) % 4]
            wa_ctr += 1
            for j, sl in enumerate(slots):
                E("gp", lambda e, do=do, j=j, sl=sl: e.dma_start(
                    out=WA[sl][:, 0:1408], in_=w2[do][:, 1408 * j:1408 * (j + 1)]),
                  writes=[wa_names[sl]], dsem=f"WA{sl}")
            E("gp", lambda e, do=do, sl=slots[0]: e.dma_start(
                out=WA[sl][0:2, 1408:1536], in_=b2d[:, 128 * do:128 * (do + 1)]),
              writes=[wa_names[slots[0]]], dsem=f"WA{slots[0]}")
            pso = [ps_big[:, 1024:1536], ps_big[:, 1536:2048], P4[:], P5[:]][do % 4]
            pn = ["G2", "G3", "P4", "P5"][do % 4]

            def down_fn(e, slots=slots, pso=pso, do=do, xa=cur["XA"]):
                for cc in range(CC):
                    sl = slots[cc // 11]
                    e.matmul(pso, WA[sl][:, 128 * (cc % 11):128 * (cc % 11) + 128],
                             xa[:, TOK * (8 + cc):TOK * (9 + cc)], start=(cc == 0), stop=False)
                return e.matmul(pso, WA[slots[0]][0:2, 1408:1536], T2[:],
                                start=False, stop=True)
            E("pe", down_fn,
              reads=[wa_names[s] for s in set(slots)] + ["T2"] +
                    [xan(8 + cc) for cc in range(CC)],
              writes=[pn, "P4k" if do % 2 == 0 else "P5k"])
            E("dve", lambda e, do=do, pso=pso: e.tensor_tensor(
                xprime[:, TOK * do:TOK * (do + 1)], xprime[:, TOK * do:TOK * (do + 1)],
                pso, op=OP.add),
              reads=[pn, f"X{do}"], writes=[f"X{do}"])
            if len(stq) >= 2:
                stats_b(stq.pop(0))
            stats_a(do)
            stq.append(do)

        while stq:
            stats_b(stq.pop(0))
        ln_rows_and_apply(out_to_ob=True)

    em.finalize()
    nc.finalize()
    return nc



def _relayout_blocks(wT, n_out_chunks, K):
    blocks = wT.reshape(K // 128, 128, n_out_chunks, 128)
    blocks = blocks.transpose(2, 1, 0, 3).reshape(n_out_chunks, 128, K)
    return np.ascontiguousarray(blocks)


def _bf(x):
    return np.asarray(x, dtype=ml_dtypes.bfloat16)


def _prep(inputs):
    f32 = np.float32
    X = np.asarray(inputs["L0_attn_hidden_states"], f32)
    Wq = np.asarray(inputs["L0_attn_Wq"], f32)
    Wkv = np.asarray(inputs["L0_attn_Wkv"], f32)
    Wo = np.asarray(inputs["L0_attn_Wo"], f32)
    w1q = np.asarray(inputs["L0_ffn_w1q"], f32)
    a1 = np.asarray(inputs["L0_ffn_a1"], f32)
    b1 = np.asarray(inputs["L0_ffn_b1"], f32)
    w3q = np.asarray(inputs["L0_ffn_w3q"], f32)
    a3 = np.asarray(inputs["L0_ffn_a3"], f32)
    b3 = np.asarray(inputs["L0_ffn_b3"], f32)
    w2q = np.asarray(inputs["L0_ffn_w2q"], f32)
    a2 = np.asarray(inputs["L0_ffn_a2"], f32)
    b2 = np.asarray(inputs["L0_ffn_b2"], f32)

    x_fm = np.ascontiguousarray(X.transpose(0, 2, 1))

    wq_dev = _relayout_blocks(Wq, NH, DM)
    wkv_dev = _relayout_blocks(Wkv, 2, DM)
    wo_dev = _relayout_blocks(Wo, DC, DM)

    w1p = np.zeros((4 * H2, 1024), f32)
    w3p = np.zeros((4 * H2, 1024), f32)
    b1p = np.zeros((4 * H2, 2), f32)
    b3p = np.zeros((4 * H2, 2), f32)
    a2p = np.zeros((2, 4 * H2), f32)
    for m in range(4):
        w1p[m * H2:m * H2 + 2752] = w1q[m * 2752:(m + 1) * 2752]
        w3p[m * H2:m * H2 + 2752] = w3q[m * 2752:(m + 1) * 2752]
        b1p[m * H2:m * H2 + 2752] = b1[m * 2752:(m + 1) * 2752]
        b3p[m * H2:m * H2 + 2752] = b3[m * 2752:(m + 1) * 2752]
        a2p[:, m * H2:m * H2 + 2752] = a2[:, m * 2752:(m + 1) * 2752]
    w1_dev = _relayout_blocks(np.ascontiguousarray(w1p.T), HC, 1024)
    w3_dev = _relayout_blocks(np.ascontiguousarray(w3p.T), HC, 1024)
    w2p = np.zeros((H2, DM), f32)
    w2p[:2752] = w2q.T
    w2_dev = _relayout_blocks(w2p, DC, H2)

    a1_dev = np.ascontiguousarray(a1.T.reshape(DC, 128, 2).transpose(1, 0, 2).reshape(128, 64))
    a3_dev = np.ascontiguousarray(a3.T.reshape(DC, 128, 2).transpose(1, 0, 2).reshape(128, 64))
    a2_dev = np.ascontiguousarray(a2p.T.reshape(HC, 128, 2).transpose(1, 0, 2).reshape(128, 2 * HC))
    b1_dev = np.ascontiguousarray(b1p.T)
    b3_dev = np.ascontiguousarray(b3p.T)
    b2_dev = np.ascontiguousarray(b2.T)

    slopes32 = np.array(SLOPES_F32, np.float64)
    sa = np.zeros((4, NH), np.float64)
    rem = slopes32.copy()
    for a in range(4):
        part = _bf(rem).astype(np.float64)
        sa[a] = part
        rem = rem - part
    k = np.arange(S, dtype=np.float64)
    k_hi, k_lo = np.floor_divide(k, 8), np.mod(k, 8)
    krows_dev = _bf(np.stack([k_hi] * 4 + [k_lo] * 4))
    bl2_dev = np.zeros((NH, 8, 128), np.float64)
    for h in range(NH):
        for a in range(4):
            bl2_dev[h, a, :] = 8.0 * sa[a, h]
            bl2_dev[h, 4 + a, :] = sa[a, h]
    bl2_dev = _bf(bl2_dev)

    qmask_cores = []
    qpos_cores = []
    xown_cores = []
    for core in range(8):
        half = core % 2
        qts = QT_HALF[half]
        qp = np.zeros((128, 8), np.float32)
        for j, t in enumerate(qts):
            qp[:, j] = 128.0 * t + np.arange(128)
        qpos_cores.append(qp)
        qm = np.zeros((128, 512), np.float32)
        for pp in range(2):
            t = qts[4 * pp]
            T = prog_T(pp, 0)
            off = t - (T - 2)
            for j, tt in enumerate(qts[4 * pp:4 * pp + 4]):
                assert tt - (prog_T(pp, j) - 2) == off
            kk = np.arange(256)
            qq = 128 * off + np.arange(128)
            tile = np.where(kk[None, :] <= qq[:, None], 0.0, NEGBIG).astype(np.float32)
            qm[:, 256 * pp:256 * (pp + 1)] = tile
        qmask_cores.append(_bf(qm))
        b = core // 2
        tok_idx = np.concatenate([np.arange(128 * t, 128 * (t + 1)) for t in qts])
        xown_cores.append(np.ascontiguousarray(x_fm[b][:, tok_idx]))

    ident = _bf(np.eye(128, dtype=np.float32))
    onesb = _bf(np.ones((128, 1), np.float32))
    ones1 = np.ones((1, 128), np.float32)

    shared = {
        "wq": wq_dev, "wkv": wkv_dev, "wo": wo_dev,
        "w1": w1_dev, "w3": w3_dev, "w2": w2_dev,
        "a1d": a1_dev, "a3d": a3_dev, "a2d": a2_dev,
        "b1d": b1_dev, "b3d": b3_dev, "b2d": b2_dev,
        "krows": krows_dev, "bl2": bl2_dev,
        "identd": ident, "onesd": onesb, "ones1d": ones1,
    }
    in_maps = []
    for core in range(8):
        m = dict(shared)
        m["x_fm"] = x_fm[core // 2]
        m["x_own"] = xown_cores[core]
        m["qpos"] = qpos_cores[core]
        m["qmask"] = qmask_cores[core]
        in_maps.append(m)
    return in_maps


def _assemble(results):
    out = np.empty((4, S, DM), np.float32)
    for core in range(8):
        half = core % 2
        b = core // 2
        qts = QT_HALF[half]
        tok_idx = np.concatenate([np.arange(128 * t, 128 * (t + 1)) for t in qts])
        yc = np.asarray(results[core]["y"], np.float32).reshape(DM, 1024)
        out[b, tok_idx, :] = yc.T
    return out


_cached = {}


def kernel(**inputs):
    import os
    debug = bool(os.environ.get("KDBG"))
    key = ("nc", debug)
    if key not in _cached:
        _cached[key] = build_program(debug=debug)
    nc = _cached[key]
    in_maps = _prep(inputs)
    res = run_bass_kernel_spmd(nc, in_maps, core_ids=list(range(8)))
    if debug:
        kernel.last_results = res.results
    return _assemble(res.results)


# version 06
# speedup vs baseline: 1.7257x; 1.3207x over previous
import math
import numpy as np
import ml_dtypes

import concourse.bass as bass
import concourse.mybir as mybir
from concourse.bass_utils import run_bass_kernel_spmd

F32 = mybir.dt.float32
BF16 = mybir.dt.bfloat16
AF = mybir.ActivationFunctionType
AX = mybir.AxisListType
OP = mybir.AluOpType

NH = 32
DC = 32
DM = 4096
S = 2048
TOK = 512
NT = 4
NPASS = 2
H2 = 2816
HC = 88
CC = 22
HEAD_SCALE = 1.0 / math.sqrt(128.0)
EPS = 1e-5
NEGBIG = -1.0e9

QT_HALF = [[0, 2, 4, 6, 9, 11, 13, 15], [1, 3, 5, 7, 8, 10, 12, 14]]

_ALIBI_START = 2.0 ** (-(2.0 ** -(math.log2(NH) - 3)))
SLOPES_F32 = [float(np.float32(_ALIBI_START ** (i + 1))) for i in range(NH)]


def prog_T(p, l):
    return (2 * l + 2) if p == 0 else (10 + 2 * l)



class Em:
    COMPUTE = ("pe", "act", "dve")
    QUEUES = ("sy", "gp")

    def __init__(self, nc):
        self.nc = nc
        self.sems = {e: nc.alloc_semaphore(f"sem_{e}") for e in self.COMPUTE}
        self.cnt = {e: 0 for e in self.COMPUTE}
        self.dq = {}
        self.ops = {e: [] for e in self.COMPUTE + self.QUEUES}
        self.buf = {}
        self.last_wait = {}

    def emit(self, e, fn, reads=(), writes=(), dsem=None):
        if e in self.QUEUES:
            assert dsem is not None
            if dsem not in self.sems:
                self.sems[dsem] = self.nc.alloc_semaphore(f"sem_{dsem}")
                self.cnt[dsem] = 0
                self.dq[dsem] = e
            assert self.dq[dsem] == e
            key, inc = dsem, 16
        else:
            assert dsem is None
            key, inc = e, 1

        waits = {}

        def add(tok):
            if tok is None:
                return
            tk, tv = tok
            if tk == e and e == "pe":
                return
            waits[tk] = max(waits.get(tk, 0), tv)

        for b in reads:
            st = self.buf.get(b)
            if st is not None:
                add(st["w"])
        for b in writes:
            st = self.buf.get(b)
            if st is not None:
                add(st["w"])
                for t in st["r"]:
                    add(t)
        lw = self.last_wait.setdefault(e, {})
        waits = {k: v for k, v in waits.items() if lw.get(k, -1) < v}
        for k, v in waits.items():
            lw[k] = v
        self.cnt[key] += inc
        tok = (key, self.cnt[key])
        self.ops[e].append((waits, fn, key, inc))
        for b in reads:
            self.buf.setdefault(b, {"w": None, "r": []})["r"].append(tok)
        for b in writes:
            self.buf[b] = {"w": tok, "r": []}
        return tok

    def carve(self, parent, children):
        st = self.buf.get(parent, {"w": None, "r": []})
        for c in children:
            self.buf[c] = {"w": st["w"], "r": list(st["r"])}

    def merge(self, parent, children):
        toks = []
        for c in children:
            st = self.buf.pop(c, None)
            if st:
                if st["w"]:
                    toks.append(st["w"])
                toks.extend(st["r"])
        self.buf[parent] = {"w": None, "r": toks}

    def coalesce(self, names, tok):
        for b in names:
            self.buf.setdefault(b, {"w": None, "r": []})["w"] = tok

    def finalize(self):
        nc = self.nc
        blk_cm = nc.Block()
        blk = blk_cm.__enter__()
        eng_attr = {"pe": "tensor", "act": "scalar", "dve": "vector",
                    "sy": "sync", "gp": "gpsimd"}
        for ename in ("sy", "gp", "pe", "act", "dve"):
            ops = self.ops[ename]
            fin = [(d, self.cnt[d]) for d, q in self.dq.items() if q == ename]

            def body(eng, _ops=ops, _fin=fin, _sems=self.sems):
                for waits, fn, key, inc in _ops:
                    for se, v in waits.items():
                        eng.wait_ge(_sems[se], v)
                    inst = fn(eng)
                    inst.then_inc(_sems[key], inc)
                for (d, v) in _fin:
                    eng.wait_ge(_sems[d], v)

            getattr(blk, eng_attr[ename])(body)
        blk_cm.__exit__(None, None, None)



def build_program(debug=False):
    nc = bass.Bass("TRN2", target_bir_lowering=False, debug=False)

    def din(name, shape, dt=F32):
        return nc.dram_tensor(name, list(shape), dt, kind="ExternalInput").ap()

    x_fm = din("x_fm", (DM, S))
    x_own = din("x_own", (DM, 1024))
    wq = din("wq", (NH, 128, DM))
    wkv = din("wkv", (2, 128, DM))
    wo = din("wo", (DC, 128, DM))
    w1 = din("w1", (HC, 128, 1152))
    w3 = din("w3", (HC, 128, 1152))
    w2 = din("w2", (DC, 128, H2))
    a1d = din("a1d", (128, 64))
    a3d = din("a3d", (128, 64))
    a2d = din("a2d", (128, 2 * HC))
    b1d = din("b1d", (2, 128 * HC))
    b3d = din("b3d", (2, 128 * HC))
    b2d = din("b2d", (2, DM))
    krows = din("krows", (8, S), BF16)
    bl2 = din("bl2", (NH, 8, 128), BF16)
    qpos = din("qpos", (128, 8))
    qmask = din("qmask", (128, 512), BF16)
    identd = din("identd", (128, 128), BF16)
    onesd = din("onesd", (128, 1), BF16)
    ones1d = din("ones1d", (1, 128))
    y = nc.dram_tensor("y", [DC, 128, 1024], F32, kind="ExternalOutput").ap()
    dbg = {}
    if debug:
        dbg["k"] = nc.dram_tensor("dbg_k", [128, S], F32, kind="ExternalOutput").ap()
        dbg["q"] = nc.dram_tensor("dbg_q", [128, 6144], F32, kind="ExternalOutput").ap()
        dbg["at"] = nc.dram_tensor("dbg_at", [128, 16384], F32, kind="ExternalOutput").ap()
        dbg["x1"] = nc.dram_tensor("dbg_x1", [128, 16384], F32, kind="ExternalOutput").ap()
        dbg["at2"] = nc.dram_tensor("dbg_at2", [128, 16384], F32, kind="ExternalOutput").ap()
        dbg["x12"] = nc.dram_tensor("dbg_x12", [128, 16384], F32, kind="ExternalOutput").ap()
        dbg["q2"] = nc.dram_tensor("dbg_q2", [128, 6144], F32, kind="ExternalOutput").ap()

    sb = nc.alloc_sbuf_tensor
    arena_a = sb("arena_a", [128, 32 * TOK], BF16)
    arena_b = sb("arena_b", [128, 32 * TOK], BF16)
    arena_q = sb("arena_q", [128, 12 * TOK], BF16)
    P_sb = sb("P_sb", [128, S], BF16)
    P2_sb = sb("P2_sb", [128, S], BF16)
    PTS = sb("PTS", [128, S], BF16)
    K_sb = sb("K_sb", [128, S], BF16)
    VT_sb = sb("VT_sb", [128, S], BF16)
    xprime = sb("xprime", [128, 32 * TOK], F32)
    XR = [sb(f"XR{i}", [128, TOK], F32) for i in range(2)]
    XB = [sb(f"XB{i}", [128, TOK], BF16) for i in range(2)]
    WA = [sb(f"WA{i}", [128, 2048], BF16) for i in range(4)]
    WB = [sb(f"WB{i}", [128, 1152], BF16) for i in range(4)]
    KR = sb("KR", [8, S], BF16)
    BL = [sb(f"BL{i}", [8, 128], BF16) for i in range(2)]
    QPOS = sb("QPOS", [128, 8], F32)
    QSQ = [sb(f"QSQ{i}", [128, TOK], BF16) for i in range(2)]
    QM = sb("QM", [128, 512], BF16)
    ID = sb("ID", [128, 128], BF16)
    ONEB = sb("ONEB", [128, 1], BF16)
    ONE1 = sb("ONE1", [1, 128], F32)
    A1 = sb("A1", [128, 64], BF16)
    A3 = sb("A3", [128, 64], BF16)
    A2 = sb("A2", [128, 2 * HC], BF16)
    T1 = sb("T1", [2, TOK], BF16)
    T3 = sb("T3", [2, TOK], BF16)
    T2 = sb("T2", [2, TOK], BF16)
    NB = sb("NB", [128, 4 * NH], F32)
    KN = sb("KN", [1, 1], F32)
    SR2 = sb("SR2", [1, TOK], F32)
    RSP = sb("RSP", [128, 4], F32)
    RS = sb("RS", [128, 1], F32)
    RR = sb("RR", [128, 1], F32)
    GSC = sb("GSC", [128, TOK], BF16)

    ps_big = nc.alloc_psum_tensor("ps_big", [128, 2048], F32)
    P4 = nc.alloc_psum_tensor("P4", [128, 512], F32)
    P5 = nc.alloc_psum_tensor("P5", [128, 512], F32)
    PT0 = nc.alloc_psum_tensor("PT0", [128, 1024], BF16)
    PT1 = nc.alloc_psum_tensor("PT1", [128, 1024], BF16)

    em = Em(nc)
    E = em.emit

    def G(i):
        return ps_big[:, 512 * i:512 * (i + 1)]

    cur = {}

    def XAr(i):
        return cur["XA"][:, TOK * i:TOK * (i + 1)]

    def ABr(i):
        return cur["AB"][:, TOK * i:TOK * (i + 1)]

    def xan(i):
        return f'{cur["xp"]}{i}'

    def abn(i):
        return f'{cur["bp"]}{i}'

    def Qr(i):
        return arena_q[:, TOK * i:TOK * (i + 1)]

    E("sy", lambda e: e.dma_start(out=ID[:], in_=identd), writes=["ID"], dsem="CS")
    E("sy", lambda e: e.dma_start(out=ONEB[:], in_=onesd), writes=["ONEB"], dsem="CS")
    E("sy", lambda e: e.dma_start(out=ONE1[:], in_=ones1d), writes=["ONE1"], dsem="CS")
    E("sy", lambda e: e.dma_start(out=QM[:], in_=qmask), writes=["QM"], dsem="CS")
    E("sy", lambda e: e.dma_start(out=KR[:], in_=krows), writes=["KR"], dsem="CS")
    tk = E("sy", lambda e: e.dma_start(out=QPOS[:], in_=qpos), writes=["QPOS"], dsem="CS")
    em.coalesce(["ID", "ONEB", "ONE1", "QM", "KR", "QPOS"], tk)
    E("gp", lambda e: e.dma_start(out=A1[:], in_=a1d), writes=["LA1"], dsem="LC")
    E("gp", lambda e: e.dma_start(out=A3[:], in_=a3d), writes=["LA3"], dsem="LC")
    tk = E("gp", lambda e: e.dma_start(out=A2[:], in_=a2d), writes=["LA2"], dsem="LC")
    em.coalesce(["LA1", "LA3", "LA2"], tk)

    wa_names = [f"WA{i}" for i in range(4)]
    wb_names = [f"WB{i}" for i in range(4)]

    def load_wa_block(src_ap, slots):
        ncols = src_ap.shape[1]
        half = ncols // 2
        for j, sl in enumerate(slots):
            E("gp",
              lambda e, s=sl, j=j, src=src_ap, h=half:
                  e.dma_start(out=WA[s][:, 0:h], in_=src[:, j * h:(j + 1) * h]),
              writes=[wa_names[sl]], dsem=f"WA{sl}")

    def q_proj_mms(h, p, wslots, part):
        xa = cur["XA"]

        def fn(e, h=h, p=p, ws=wslots, part=part, xa=xa):
            mm = None
            for c in range(8 * part, 8 * part + 8):
                sl = ws[c // 16]
                tile = WA[sl][:, 128 * (c % 16):128 * (c % 16) + 128]
                mm = e.matmul(P4[:], tile, xa[:, TOK * c:TOK * (c + 1)],
                              start=(c == 0), stop=(c == DC - 1))
            return mm
        reads = [wa_names[s] for s in set(wslots)] + [xan(c) for c in range(8 * part, 8 * part + 8)]
        E("pe", fn, reads=reads, writes=(["P4", "P4k"] if part == 0 else ["P4k"]))

    def q_evac(h):
        s = h % 12
        E("act",
          lambda e, s=s: e.activation(Qr(s), P4[:], AF.Copy, bias=0.0, scale=HEAD_SCALE),
          reads=["P4", "P4k"], writes=[f"Q{s}"])

    def negb_a(h):
        s = h % 12
        r = h % 2
        E("act", lambda e, s=s, r=r: e.activation(QSQ[r][:], Qr(s), AF.Square, bias=0.0, scale=1.0),
          reads=[f"Q{s}"], writes=[f"QSQ{r}"])

    def negb_b12(h):
        r = h % 2
        E("pe", lambda e, r=r: e.matmul(P5[0:1, :], ONEB[:], QSQ[r][:], start=True, stop=True),
          reads=[f"QSQ{r}", "ONEB"], writes=["P5", "P5k"])
        E("dve", lambda e: e.tensor_reduce(
            SR2[0:1, 0:4], P5[0:1, 0:512].rearrange("p (g k) -> p g k", k=128),
            axis=AX.X, op=OP.max),
          reads=["P5", "P5k"], writes=["SR2"])
        E("act", lambda e: e.activation(SR2[0:1, 0:4], SR2[0:1, 0:4], AF.Sqrt,
                                        bias=0.0, scale=1.0),
          reads=["SR2"], writes=["SR2"])
        E("dve", lambda e: e.tensor_scalar(SR2[0:1, 0:4], SR2[0:1, 0:4],
                                           KN[0:1, 0:1], -1.0, op0=OP.mult, op1=OP.mult),
          reads=["SR2", "KN"], writes=["SR2"])

    def negb_b3(h, p):
        E("pe", lambda e: e.matmul(P5[:, 0:4], ONE1[:], SR2[0:1, 0:4],
                                   start=True, stop=True),
          reads=["SR2", "ONE1"], writes=["P5", "P5k"])
        E("dve", lambda e, h=h, p=p: e.scalar_tensor_tensor(
            NB[:, 4 * h:4 * h + 4], QPOS[:, 4 * p:4 * p + 4], -SLOPES_F32[h],
            P5[:, 0:4], op0=OP.mult, op1=OP.add),
          reads=["QPOS", "P5", "P5k"], writes=[f"NB{h}"])

    wkv_slots = [0, 1, 2, 3]

    wa_ctr = 0
    for p in range(NPASS):
        if p == 0:
            cur.update(XA=arena_a, xp="A", AB=arena_b, bp="B")
        else:
            cur.update(XA=arena_b, xp="B", AB=arena_a, bp="A")
        for c in range(DC):
            tk = E("gp",
                   lambda e, c=c, p=p, xa=cur["XA"]: e.dma_start(
                       out=xa[:, TOK * c:TOK * (c + 1)],
                       in_=x_own[128 * c:128 * (c + 1), TOK * p:TOK * (p + 1)]),
                   writes=[xan(c)], dsem="XBL")
        em.coalesce([xan(c) for c in range(DC)], tk)

        if p == 0:
            load_wa_block(wkv[0], [0, 1])
            load_wa_block(wkv[1], [2, 3])
            wa_ctr = 2
            for half in range(2):
                for c in range(DC):
                    i = c % 4
                    kb = [abn(2 * i), abn(2 * i + 1)]
                    kvbuf = cur["AB"][:, 1024 * i:1024 * (i + 1)]
                    E("gp",
                      lambda e, c=c, half=half, kvbuf=kvbuf: e.dma_start(
                          out=kvbuf, in_=x_fm[128 * c:128 * (c + 1),
                                              1024 * half:1024 * (half + 1)]),
                      writes=kb, dsem=f"KV{i}")

                    def kv_fn(e, c=c, kvbuf=kvbuf):
                        mm = None
                        kt = WA[c // 16][:, 128 * (c % 16):128 * (c % 16) + 128]
                        vt = WA[2 + c // 16][:, 128 * (c % 16):128 * (c % 16) + 128]
                        for j in range(2):
                            mm = e.matmul(G(j), kt, kvbuf[:, 512 * j:512 * (j + 1)],
                                          start=(c == 0), stop=(c == DC - 1))
                            mm = e.matmul(G(2 + j), vt, kvbuf[:, 512 * j:512 * (j + 1)],
                                          start=(c == 0), stop=(c == DC - 1))
                        return mm
                    E("pe", kv_fn,
                      reads=kb + wa_names,
                      writes=(["G0", "G1", "G2", "G3", "Gk"] if c == 0 else ["Gk"]))
                E("act",
                  lambda e, half=half: e.activation(
                      K_sb[:, 1024 * half:1024 * (half + 1)], ps_big[:, 0:1024],
                      AF.Copy, bias=0.0, scale=1.0),
                  reads=["G0", "G1", "Gk"], writes=["K"])
                E("act",
                  lambda e: e.activation(P_sb[:, 0:1024], ps_big[:, 1024:2048],
                                         AF.Copy, bias=0.0, scale=1.0),
                  reads=["G2", "G3", "Gk"], writes=["P0"])

                def vtr_fn(e, half=half):
                    mm = None
                    for k in range(8):
                        mm = e.transpose(PT0[:, 128 * k:128 * (k + 1)],
                                         P_sb[:, 128 * k:128 * (k + 1)], ID[:])
                    return mm
                E("pe", vtr_fn, reads=["P0", "ID"], writes=["PT0"])
                E("dve",
                  lambda e, half=half: e.tensor_copy(
                      VT_sb[:, 1024 * half:1024 * (half + 1)], PT0[:]),
                  reads=["PT0"], writes=["VT"])

            for c in range(4):
                E("act", lambda e, c=c: e.activation(QSQ[c % 2][:], K_sb[:, 512 * c:512 * (c + 1)],
                                                     AF.Square, bias=0.0, scale=1.0),
                  reads=["K"], writes=[f"QSQ{c % 2}"])
                E("pe", lambda e, c=c: e.matmul(ps_big[0:1, 512 * c:512 * (c + 1)],
                                                ONEB[:], QSQ[c % 2][:], start=True, stop=True),
                  reads=[f"QSQ{c % 2}", "ONEB"], writes=[f"G{c}"])
            E("dve", lambda e: e.reduce_max(KN[0:1, 0:1], ps_big[0:1, :], axis=AX.X),
              reads=["G0", "G1", "G2", "G3"], writes=["KN"])
            E("act", lambda e: e.activation(KN[0:1, 0:1], KN[0:1, 0:1], AF.Sqrt,
                                            bias=0.0, scale=1.0),
              reads=["KN"], writes=["KN"])
            E("dve", lambda e: e.tensor_scalar_mul(KN[0:1, 0:1], KN[0:1, 0:1], 1.0005),
              reads=["KN"], writes=["KN"])
            if debug:
                E("gp", lambda e: e.dma_start(out=dbg["k"], in_=K_sb[:]), reads=["K"], dsem="DBGK")

        nbq = []
        for h in range(12):
            blk_src = wq[h]
            slots = [(2 * wa_ctr) % 4, (2 * wa_ctr + 1) % 4]
            wa_ctr += 1
            load_wa_block(blk_src, slots)
            for part in range(4):
                q_proj_mms(h, p, slots, part)
            if nbq:
                negb_b3(nbq.pop(0), p)
            q_evac(h)
            negb_a(h)
            negb_b12(h)
            nbq.append(h)

        for h in range(NH):
            s = h % 12
            bl = h % 2
            E("sy", lambda e, h=h, bl=bl: e.dma_start(out=BL[bl][:], in_=bl2[h]),
              writes=[f"BL{bl}"], dsem=f"BLD{bl}")

            fill = (1 <= h <= 20)
            if fill:
                fh = h + 11
                fslots = [(2 * wa_ctr) % 4, (2 * wa_ctr + 1) % 4]
                wa_ctr += 1
                load_wa_block(wq[fh], fslots)

            pt_ctr = 0

            def geom(l):
                T = prog_T(p, l)
                L = 128 * T
                nsl = (L + 511) // 512
                return T, L, nsl

            def bank(l, sl):
                return 2 * (l % 2) + sl % 2

            def pbuf(l):
                return (P_sb, "P0") if l % 2 == 0 else (P2_sb, "P1")

            def emit_S_slice(l, sl, s=s, bl=bl):
                T, L, nsl = geom(l)
                c0 = 512 * sl
                c1 = min(512 * (sl + 1), L)
                bk = bank(l, sl)

                def s_fn(e, s=s, l=l, c0=c0, c1=c1, bk=bk, bl=bl):
                    e.matmul(G(bk)[:, 0:c1 - c0],
                             Qr(s)[:, 128 * l:128 * (l + 1)],
                             K_sb[:, c0:c1], start=True, stop=False)
                    return e.matmul(G(bk)[:, 0:c1 - c0],
                                    BL[bl][:], KR[0:8, c0:c1], start=False, stop=True)
                E("pe", s_fn, reads=[f"Q{s}", "K", f"BL{bl}"], writes=[f"G{bk}"])
                if sl == nsl - 1:
                    w = c1 - c0
                    E("dve",
                      lambda e, w=w, p=p, bk=bk: e.tensor_tensor(
                          G(bk)[:, w - 256:w], G(bk)[:, w - 256:w],
                          QM[:, 256 * p:256 * (p + 1)], op=OP.add),
                      reads=["QM", f"G{bk}"], writes=[f"G{bk}"])

            def emit_exp(l, sl, h=h):
                T, L, nsl = geom(l)
                pb, pn = pbuf(l)
                c0 = 512 * sl
                c1 = min(512 * (sl + 1), L)
                bk = bank(l, sl)
                E("act",
                  lambda e, c0=c0, c1=c1, h=h, l=l, sl=sl, bk=bk, pb=pb: e.activation(
                      pb[:, c0:c1], G(bk)[:, 0:c1 - c0], AF.Exp,
                      bias=NB[:, 4 * h + l:4 * h + l + 1], scale=1.0,
                      accum_out=RSP[:, sl:sl + 1]),
                  reads=[f"G{bk}", f"NB{h}"], writes=[pn, f"RSP{sl}"])

            def emit_tile_chain(l):
                T, L, nsl = geom(l)
                pb, pn = pbuf(l)
                E("dve", lambda e, nsl=nsl: e.reduce_sum(RS[:], RSP[:, 0:nsl], axis=AX.X),
                  reads=[f"RSP{i}" for i in range(nsl)], writes=["RS"])
                E("dve", lambda e: e.reciprocal(RR[:], RS[:]), reads=["RS"], writes=["RR"])
                E("dve", lambda e, L=L, pb=pb: e.tensor_scalar_mul(pb[:, 0:L], pb[:, 0:L], RR[:]),
                  reads=[pn, "RR"], writes=[pn])

            def emit_tile_full(l):
                T, L, nsl = geom(l)
                for sl in range(nsl):
                    emit_S_slice(l, sl)
                    emit_exp(l, sl)
                emit_tile_chain(l)

            def emit_tile_start(l):
                T, L, nsl = geom(l)
                for sl in range(min(2, nsl)):
                    emit_S_slice(l, sl)

            def emit_tile_rest(l):
                T, L, nsl = geom(l)
                for sl in range(min(2, nsl)):
                    emit_exp(l, sl)
                for sl in range(2, nsl):
                    emit_S_slice(l, sl)
                    emit_exp(l, sl)
                emit_tile_chain(l)

            def emit_T(l):
                nonlocal pt_ctr
                T, L, nsl = geom(l)
                pb, pn = pbuf(l)
                for g in range((T + 7) // 8):
                    k0 = 8 * g
                    k1 = min(8 * g + 8, T)
                    pt = pt_ctr % 2
                    pt_ctr += 1

                    def tr_fn(e, k0=k0, k1=k1, pt=pt, pb=pb):
                        mm = None
                        for k in range(k0, k1):
                            mm = e.transpose((PT0 if pt == 0 else PT1)[:, 128 * (k - k0):128 * (k - k0 + 1)],
                                             pb[:, 128 * k:128 * (k + 1)], ID[:])
                        return mm
                    E("pe", tr_fn, reads=[pn, "ID"], writes=[f"PT{pt}"])
                    E("dve",
                      lambda e, k0=k0, k1=k1, pt=pt: e.tensor_copy(
                          PTS[:, 128 * k0:128 * k1],
                          (PT0 if pt == 0 else PT1)[:, 0:128 * (k1 - k0)]),
                      reads=[f"PT{pt}"], writes=["PTS"])

            def emit_AV(l):
                T, L, nsl = geom(l)

                def av_fn(e, T=T, l=l):
                    mm = None
                    for kt in range(T):
                        mm = e.matmul(P5[:, 128 * l:128 * (l + 1)],
                                      VT_sb[:, 128 * kt:128 * (kt + 1)],
                                      PTS[:, 128 * kt:128 * (kt + 1)],
                                      start=(kt == 0), stop=(kt == T - 1))
                    return mm
                E("pe", av_fn, reads=["PTS", "VT"],
                  writes=(["P5", "P5k"] if l == 0 else ["P5k"]))

            emit_tile_full(0)
            emit_tile_full(1)
            if nbq:
                negb_b3(nbq.pop(0), p)
            emit_T(0)
            emit_tile_start(2)
            emit_AV(0)
            emit_tile_rest(2)
            if fill:
                q_proj_mms(fh, p, fslots, 0)
            emit_T(1)
            emit_tile_start(3)
            emit_AV(1)
            emit_tile_rest(3)
            if fill:
                q_proj_mms(fh, p, fslots, 1)
            emit_T(2)
            if fill:
                q_proj_mms(fh, p, fslots, 2)
            emit_AV(2)
            emit_T(3)
            if fill:
                q_proj_mms(fh, p, fslots, 3)
                q_evac(fh)
            emit_AV(3)
            E("act", lambda e, h=h, ab=cur["AB"]: e.activation(
                ab[:, TOK * h:TOK * (h + 1)], P5[:], AF.Copy, bias=0.0, scale=1.0),
              reads=["P5", "P5k"], writes=[abn(h)])
            if fill:
                negb_a(fh)
                negb_b12(fh)
                nbq.append(fh)

        if debug:
            E("gp", lambda e, p=p: e.dma_start(out=dbg["q" if p == 0 else "q2"], in_=arena_q[:]),
              reads=[f"Q{i}" for i in range(12)], dsem="DBGQ")
            E("gp", lambda e, p=p, ab=cur["AB"]: e.dma_start(
                out=dbg["at" if p == 0 else "at2"], in_=ab[:]),
              reads=[abn(i) for i in range(NH)], dsem="DBGA")

        stq = []
        em.carve("PTS", ["PTc", "PTd", "PTe", "PTf"])
        em.carve("P0", ["PSa", "PSb"])
        em.carve("P1", ["GSa", "GSb", "GSc"])
        XBS = [(XB[0][:], "XB0"), (XB[1][:], "XB1"),
               (PTS[:, 0:512], "PTc"), (PTS[:, 512:1024], "PTd")]
        SQS = [(QSQ[0][:], "QSQ0"), (QSQ[1][:], "QSQ1"),
               (PTS[:, 1024:1536], "PTe"), (PTS[:, 1536:2048], "PTf")]
        G1S = [(XAr(30), xan(30)), (XAr(31), xan(31)),
               (P_sb[:, 0:512], "PSa"), (P_sb[:, 512:1024], "PSb")]
        GSCS = [(GSC[:], "GSC"), (P2_sb[:, 0:512], "GSa"),
                (P2_sb[:, 512:1024], "GSb"), (P2_sb[:, 1024:1536], "GSc")]

        def stats_a(c):
            xb, xbn = XBS[c % 4]
            sq, sqn = SQS[c % 4]
            E("act", lambda e, c=c, xb=xb: e.activation(xb, xprime[:, TOK * c:TOK * (c + 1)],
                                                        AF.Copy, bias=0.0, scale=1.0),
              reads=[f"X{c}"], writes=[xbn])
            E("act", lambda e, c=c, sq=sq: e.activation(sq, xprime[:, TOK * c:TOK * (c + 1)],
                                                        AF.Square, bias=0.0, scale=1.0),
              reads=[f"X{c}"], writes=[sqn])

        def stats_b(c):
            xb, xbn = XBS[c % 4]
            sq, sqn = SQS[c % 4]
            E("pe", lambda e, c=c, xb=xb: e.matmul(ps_big[0:1, 0:512], ONEB[:], xb,
                                                   start=(c == 0), stop=(c == DC - 1)),
              reads=[xbn, "ONEB"], writes=(["G0", "Gk"] if c == 0 else ["Gk"]))
            E("pe", lambda e, c=c, sq=sq: e.matmul(ps_big[0:1, 512:1024], ONEB[:], sq,
                                                   start=(c == 0), stop=(c == DC - 1)),
              reads=[sqn, "ONEB"], writes=(["G1"] if c == 0 else ["Gk"]))

        def ln_rows_and_apply(out_to_ob):
            r0 = XR[0][0:1, :]
            r1 = XR[1][0:1, :]
            E("act", lambda e, r0=r0: e.activation(r0, ps_big[0:1, 0:512], AF.Copy,
                                                   bias=0.0, scale=1.0 / DM),
              reads=["G0", "Gk"], writes=["XR0"])
            E("act", lambda e, r1=r1: e.activation(r1, ps_big[0:1, 512:1024], AF.Copy,
                                                   bias=0.0, scale=1.0 / DM),
              reads=["G1", "Gk"], writes=["XR1"])
            E("act", lambda e, r0=r0: e.activation(SR2[:], r0, AF.Square, bias=0.0, scale=1.0),
              reads=["XR0"], writes=["SR2"])
            E("dve", lambda e, r1=r1: e.tensor_tensor(r1, r1, SR2[:], op=OP.subtract),
              reads=["XR1", "SR2"], writes=["XR1"])
            E("dve", lambda e, r1=r1: e.tensor_scalar_add(r1, r1, EPS),
              reads=["XR1"], writes=["XR1"])
            E("act", lambda e, r1=r1: e.activation(r1, r1, AF.Sqrt, bias=0.0, scale=1.0),
              reads=["XR1"], writes=["XR1"])
            E("dve", lambda e, r1=r1: e.reciprocal(r1, r1), reads=["XR1"], writes=["XR1"])
            E("dve", lambda e, r0=r0, r1=r1: e.tensor_tensor(r0, r0, r1, op=OP.mult),
              reads=["XR0", "XR1"], writes=["XR0"])
            E("pe", lambda e, r1=r1: e.matmul(ps_big[:, 1024:1536], ONE1[:], r1,
                                              start=True, stop=True),
              reads=["XR1", "ONE1"], writes=["G2"])
            E("pe", lambda e, r0=r0: e.matmul(ps_big[:, 1536:2048], ONE1[:], r0,
                                              start=True, stop=True),
              reads=["XR0", "ONE1"], writes=["G3"])
            for c in range(DC):
                if not out_to_ob:
                    E("dve", lambda e, c=c: e.tensor_tensor(
                        xprime[:, TOK * c:TOK * (c + 1)], xprime[:, TOK * c:TOK * (c + 1)],
                        ps_big[:, 1024:1536], op=OP.mult),
                      reads=[f"X{c}", "G2"], writes=[f"X{c}"])
                    E("dve", lambda e, c=c: e.tensor_tensor(
                        xprime[:, TOK * c:TOK * (c + 1)], xprime[:, TOK * c:TOK * (c + 1)],
                        ps_big[:, 1536:2048], op=OP.subtract),
                      reads=[f"X{c}", "G3"], writes=[f"X{c}"])
                else:
                    ob = c % 2
                    E("dve", lambda e, c=c, ob=ob: e.tensor_tensor(
                        XR[ob][:], xprime[:, TOK * c:TOK * (c + 1)],
                        ps_big[:, 1024:1536], op=OP.mult),
                      reads=[f"X{c}", "G2"], writes=[f"XR{ob}"])
                    E("dve", lambda e, c=c, ob=ob: e.tensor_tensor(
                        XR[ob][:], XR[ob][:], ps_big[:, 1536:2048], op=OP.subtract),
                      reads=[f"XR{ob}", "G3"], writes=[f"XR{ob}"])
                    E("sy", lambda e, c=c, ob=ob, p=p: e.dma_start(
                        out=y[c][:, TOK * p:TOK * (p + 1)], in_=XR[ob][:]),
                      reads=[f"XR{ob}"], dsem=f"OUT{ob}")

        for do in range(DC):
            slots = [(2 * wa_ctr) % 4, (2 * wa_ctr + 1) % 4]
            wa_ctr += 1
            load_wa_block(wo[do], slots)
            pso = [ps_big[:, 1024:1536], ps_big[:, 1536:2048], P4[:], P5[:]][do % 4]
            pn = ["G2", "G3", "P4", "P5"][do % 4]

            def wo_fn(e, slots=slots, pso=pso, ab=cur["AB"]):
                mm = None
                for c in range(DC):
                    sl = slots[c // 16]
                    mm = e.matmul(pso, WA[sl][:, 128 * (c % 16):128 * (c % 16) + 128],
                                  ab[:, TOK * c:TOK * (c + 1)], start=(c == 0), stop=(c == DC - 1))
                return mm
            E("pe", wo_fn,
              reads=[wa_names[s] for s in set(slots)] + [abn(c) for c in range(DC)],
              writes=[pn])
            E("sy", lambda e, do=do, p=p: e.dma_start(
                out=XR[do % 2][:], in_=x_own[128 * do:128 * (do + 1), TOK * p:TOK * (p + 1)]),
              writes=[f"XR{do % 2}"], dsem=f"RXR{do % 2}")
            E("dve", lambda e, do=do, pso=pso: e.tensor_tensor(
                xprime[:, TOK * do:TOK * (do + 1)], XR[do % 2][:], pso, op=OP.add),
              reads=[pn, f"XR{do % 2}"], writes=[f"X{do}"])
            if len(stq) >= 3:
                stats_b(stq.pop(0))
            stats_a(do)
            stq.append(do)

        while stq:
            stats_b(stq.pop(0))
        ln_rows_and_apply(out_to_ob=False)
        if debug:
            E("gp", lambda e, p=p: e.dma_start(out=dbg["x1" if p == 0 else "x12"], in_=xprime[:]),
              reads=[f"X{c}" for c in range(DC)], dsem="DBGX")

        for jc in range(8):
            E("dve", lambda e, jc=jc: e.tensor_tensor(
                XR[0][:], xprime[:, TOK * jc:TOK * (jc + 1)],
                xprime[:, TOK * (8 + jc):TOK * (9 + jc)], op=OP.add),
              reads=[f"X{jc}", f"X{8 + jc}"], writes=["XR0"])
            E("dve", lambda e, jc=jc: e.tensor_tensor(
                XR[0][:], XR[0][:], xprime[:, TOK * (16 + jc):TOK * (17 + jc)], op=OP.add),
              reads=["XR0", f"X{16 + jc}"], writes=["XR0"])
            E("dve", lambda e, jc=jc, xa=cur["XA"]: e.tensor_tensor(
                xa[:, TOK * jc:TOK * (jc + 1)], XR[0][:],
                xprime[:, TOK * (24 + jc):TOK * (25 + jc)], op=OP.add),
              reads=["XR0", f"X{24 + jc}"], writes=[xan(jc)])
        for c in range(DC):
            xb, xbn = XBS[c % 4]
            E("act", lambda e, c=c, xb=xb: e.activation(xb, xprime[:, TOK * c:TOK * (c + 1)],
                                                        AF.Copy, bias=0.0, scale=1.0),
              reads=[f"X{c}"], writes=[xbn])
            E("pe", lambda e, c=c, xb=xb: e.matmul(P4[0:2, :], A1[:, 2 * c:2 * c + 2], xb,
                                            start=(c == 0), stop=(c == DC - 1)),
              reads=[xbn, "LA1"], writes=(["P4", "P4k"] if c == 0 else ["P4k"]))
            E("pe", lambda e, c=c, xb=xb: e.matmul(P5[0:2, :], A3[:, 2 * c:2 * c + 2], xb,
                                            start=(c == 0), stop=(c == DC - 1)),
              reads=[xbn, "LA3"], writes=(["P5", "P5k"] if c == 0 else ["P5k"]))
        E("act", lambda e: e.activation(T1[:], P4[0:2, :], AF.Copy, bias=0.0, scale=0.5),
          reads=["P4", "P4k"], writes=["T1"])
        E("act", lambda e: e.activation(T3[:], P5[0:2, :], AF.Copy, bias=0.0, scale=0.5),
          reads=["P5", "P5k"], writes=["T3"])

        wb_ctr = 0
        t2q = []

        def emit_t2(hc, grd, gap):
            E("pe", lambda e, hc=hc, gap=gap: e.matmul(
                P4[0:2, :], A2[:, 2 * hc:2 * hc + 2], gap,
                start=(hc == 0), stop=(hc == HC - 1)),
              reads=[grd, "LA2"], writes=(["P4", "P4k"] if hc == 0 else ["P4k"]))

        for hc in range(HC):
            m, cc = hc // CC, hc % CC
            s1 = wb_ctr % 4
            s3 = (wb_ctr + 1) % 4
            wb_ctr += 2
            E("gp", lambda e, hc=hc, s1=s1: e.dma_start(out=WB[s1][:], in_=w1[hc]),
              writes=[wb_names[s1]], dsem=f"WB{s1}")
            E("gp", lambda e, hc=hc, s3=s3: e.dma_start(out=WB[s3][:], in_=w3[hc]),
              writes=[wb_names[s3]], dsem=f"WB{s3}")
            gb = 2 * (hc % 2)
            ub = gb + 1

            def gate_fn(e, hc=hc, s1=s1, gb=gb, xa=cur["XA"]):
                for jc in range(8):
                    e.matmul(G(gb), WB[s1][:, 128 * jc:128 * (jc + 1)],
                             xa[:, TOK * jc:TOK * (jc + 1)],
                             start=(jc == 0), stop=False)
                return e.matmul(G(gb), WB[s1][0:2, 1024:1152], T1[:],
                                start=False, stop=True)
            E("pe", gate_fn,
              reads=[wb_names[s1], "T1"] + [xan(j) for j in range(8)],
              writes=[f"G{gb}"])

            def up_fn(e, hc=hc, s3=s3, ub=ub, xa=cur["XA"]):
                for jc in range(8):
                    e.matmul(G(ub), WB[s3][:, 128 * jc:128 * (jc + 1)],
                             xa[:, TOK * jc:TOK * (jc + 1)],
                             start=(jc == 0), stop=False)
                return e.matmul(G(ub), WB[s3][0:2, 1024:1152], T3[:],
                                start=False, stop=True)
            E("pe", up_fn,
              reads=[wb_names[s3], "T3"] + [xan(j) for j in range(8)],
              writes=[f"G{ub}"])
            if t2q:
                emit_t2(*t2q.pop())
            g1, g1n = G1S[hc % 4]
            E("act", lambda e, gb=gb, g1=g1: e.activation(
                g1, G(gb), AF.Silu, bias=0.0, scale=1.0),
              reads=[f"G{gb}"], writes=[g1n])
            gname = xan(8 + cc)
            if m == 0:
                E("dve", lambda e, ub=ub, cc=cc, g1=g1, xa=cur["XA"]: e.tensor_tensor(
                    xa[:, TOK * (8 + cc):TOK * (9 + cc)], g1, G(ub), op=OP.mult),
                  reads=[g1n, f"G{ub}"], writes=[gname])
                t2q.append((hc, gname, XAr(8 + cc)))
            else:
                gs, gsn = GSCS[hc % 4]
                E("dve", lambda e, ub=ub, g1=g1, gs=gs: e.tensor_tensor(
                    gs, g1, G(ub), op=OP.mult),
                  reads=[g1n, f"G{ub}"], writes=[gsn])
                E("dve", lambda e, cc=cc, gs=gs, xa=cur["XA"]: e.tensor_tensor(
                    xa[:, TOK * (8 + cc):TOK * (9 + cc)],
                    xa[:, TOK * (8 + cc):TOK * (9 + cc)], gs, op=OP.add),
                  reads=[gname, gsn], writes=[gname])
                t2q.append((hc, gsn, gs))
        while t2q:
            emit_t2(*t2q.pop())
        E("act", lambda e: e.activation(T2[:], P4[0:2, :], AF.Copy, bias=0.0, scale=0.5),
          reads=["P4", "P4k"], writes=["T2"])

        for do in range(DC):
            slots = [(2 * wa_ctr) % 4, (2 * wa_ctr + 1) % 4]
            wa_ctr += 1
            for j, sl in enumerate(slots):
                E("gp", lambda e, do=do, j=j, sl=sl: e.dma_start(
                    out=WA[sl][:, 0:1408], in_=w2[do][:, 1408 * j:1408 * (j + 1)]),
                  writes=[wa_names[sl]], dsem=f"WA{sl}")
            E("gp", lambda e, do=do, sl=slots[0]: e.dma_start(
                out=WA[sl][0:2, 1408:1536], in_=b2d[:, 128 * do:128 * (do + 1)]),
              writes=[wa_names[slots[0]]], dsem=f"WA{slots[0]}")
            pso = [ps_big[:, 1024:1536], ps_big[:, 1536:2048], P4[:], P5[:]][do % 4]
            pn = ["G2", "G3", "P4", "P5"][do % 4]

            def down_fn(e, slots=slots, pso=pso, do=do, xa=cur["XA"]):
                for cc in range(CC):
                    sl = slots[cc // 11]
                    e.matmul(pso, WA[sl][:, 128 * (cc % 11):128 * (cc % 11) + 128],
                             xa[:, TOK * (8 + cc):TOK * (9 + cc)], start=(cc == 0), stop=False)
                return e.matmul(pso, WA[slots[0]][0:2, 1408:1536], T2[:],
                                start=False, stop=True)
            E("pe", down_fn,
              reads=[wa_names[s] for s in set(slots)] + ["T2"] +
                    [xan(8 + cc) for cc in range(CC)],
              writes=[pn, "P4k" if do % 2 == 0 else "P5k"])
            E("dve", lambda e, do=do, pso=pso: e.tensor_tensor(
                xprime[:, TOK * do:TOK * (do + 1)], xprime[:, TOK * do:TOK * (do + 1)],
                pso, op=OP.add),
              reads=[pn, f"X{do}"], writes=[f"X{do}"])
            if len(stq) >= 3:
                stats_b(stq.pop(0))
            stats_a(do)
            stq.append(do)

        while stq:
            stats_b(stq.pop(0))
        ln_rows_and_apply(out_to_ob=True)
        em.merge("PTS", ["PTc", "PTd", "PTe", "PTf"])
        em.merge("P0", ["PSa", "PSb"])
        em.merge("P1", ["GSa", "GSb", "GSc"])

    em.finalize()
    nc.finalize()
    return nc



def _relayout_blocks(wT, n_out_chunks, K):
    blocks = wT.reshape(K // 128, 128, n_out_chunks, 128)
    blocks = blocks.transpose(2, 1, 0, 3).reshape(n_out_chunks, 128, K)
    return np.ascontiguousarray(blocks)


def _bf(x):
    return np.asarray(x, dtype=ml_dtypes.bfloat16)


def _prep(inputs):
    f32 = np.float32
    X = np.asarray(inputs["L0_attn_hidden_states"], f32)
    Wq = np.asarray(inputs["L0_attn_Wq"], f32)
    Wkv = np.asarray(inputs["L0_attn_Wkv"], f32)
    Wo = np.asarray(inputs["L0_attn_Wo"], f32)
    w1q = np.asarray(inputs["L0_ffn_w1q"], f32)
    a1 = np.asarray(inputs["L0_ffn_a1"], f32)
    b1 = np.asarray(inputs["L0_ffn_b1"], f32)
    w3q = np.asarray(inputs["L0_ffn_w3q"], f32)
    a3 = np.asarray(inputs["L0_ffn_a3"], f32)
    b3 = np.asarray(inputs["L0_ffn_b3"], f32)
    w2q = np.asarray(inputs["L0_ffn_w2q"], f32)
    a2 = np.asarray(inputs["L0_ffn_a2"], f32)
    b2 = np.asarray(inputs["L0_ffn_b2"], f32)

    x_fm = np.ascontiguousarray(X.transpose(0, 2, 1))

    wq_dev = _relayout_blocks(Wq, NH, DM)
    wkv_dev = _relayout_blocks(Wkv, 2, DM)
    wo_dev = _relayout_blocks(Wo, DC, DM)

    w1p = np.zeros((4 * H2, 1024), f32)
    w3p = np.zeros((4 * H2, 1024), f32)
    b1p = np.zeros((4 * H2, 2), f32)
    b3p = np.zeros((4 * H2, 2), f32)
    a2p = np.zeros((2, 4 * H2), f32)
    for m in range(4):
        w1p[m * H2:m * H2 + 2752] = w1q[m * 2752:(m + 1) * 2752]
        w3p[m * H2:m * H2 + 2752] = w3q[m * 2752:(m + 1) * 2752]
        b1p[m * H2:m * H2 + 2752] = b1[m * 2752:(m + 1) * 2752]
        b3p[m * H2:m * H2 + 2752] = b3[m * 2752:(m + 1) * 2752]
        a2p[:, m * H2:m * H2 + 2752] = a2[:, m * 2752:(m + 1) * 2752]
    w1_dev = _relayout_blocks(np.ascontiguousarray(w1p.T), HC, 1024)
    w3_dev = _relayout_blocks(np.ascontiguousarray(w3p.T), HC, 1024)
    w1a = np.zeros((HC, 128, 1152), np.float32)
    w3a = np.zeros((HC, 128, 1152), np.float32)
    w1a[:, :, 0:1024] = w1_dev
    w3a[:, :, 0:1024] = w3_dev
    for hc in range(HC):
        w1a[hc, 0:2, 1024:1152] = b1p.T[:, 128 * hc:128 * (hc + 1)]
        w3a[hc, 0:2, 1024:1152] = b3p.T[:, 128 * hc:128 * (hc + 1)]
    w1_dev, w3_dev = w1a, w3a
    w2p = np.zeros((H2, DM), f32)
    w2p[:2752] = w2q.T
    w2_dev = _relayout_blocks(w2p, DC, H2)

    a1_dev = np.ascontiguousarray(a1.T.reshape(DC, 128, 2).transpose(1, 0, 2).reshape(128, 64))
    a3_dev = np.ascontiguousarray(a3.T.reshape(DC, 128, 2).transpose(1, 0, 2).reshape(128, 64))
    a2_dev = np.ascontiguousarray(a2p.T.reshape(HC, 128, 2).transpose(1, 0, 2).reshape(128, 2 * HC))
    b1_dev = np.ascontiguousarray(b1p.T)
    b3_dev = np.ascontiguousarray(b3p.T)
    b2_dev = np.ascontiguousarray(b2.T)

    slopes32 = np.array(SLOPES_F32, np.float64)
    sa = np.zeros((4, NH), np.float64)
    rem = slopes32.copy()
    for a in range(4):
        part = _bf(rem).astype(np.float64)
        sa[a] = part
        rem = rem - part
    k = np.arange(S, dtype=np.float64)
    k_hi, k_lo = np.floor_divide(k, 8), np.mod(k, 8)
    krows_dev = _bf(np.stack([k_hi] * 4 + [k_lo] * 4))
    bl2_dev = np.zeros((NH, 8, 128), np.float64)
    for h in range(NH):
        for a in range(4):
            bl2_dev[h, a, :] = 8.0 * sa[a, h]
            bl2_dev[h, 4 + a, :] = sa[a, h]
    bl2_dev = _bf(bl2_dev)

    qmask_cores = []
    qpos_cores = []
    xown_cores = []
    for core in range(8):
        half = core % 2
        qts = QT_HALF[half]
        qp = np.zeros((128, 8), np.float32)
        for j, t in enumerate(qts):
            qp[:, j] = 128.0 * t + np.arange(128)
        qpos_cores.append(qp)
        qm = np.zeros((128, 512), np.float32)
        for pp in range(2):
            t = qts[4 * pp]
            T = prog_T(pp, 0)
            off = t - (T - 2)
            for j, tt in enumerate(qts[4 * pp:4 * pp + 4]):
                assert tt - (prog_T(pp, j) - 2) == off
            kk = np.arange(256)
            qq = 128 * off + np.arange(128)
            tile = np.where(kk[None, :] <= qq[:, None], 0.0, NEGBIG).astype(np.float32)
            qm[:, 256 * pp:256 * (pp + 1)] = tile
        qmask_cores.append(_bf(qm))
        b = core // 2
        tok_idx = np.concatenate([np.arange(128 * t, 128 * (t + 1)) for t in qts])
        xown_cores.append(np.ascontiguousarray(x_fm[b][:, tok_idx]))

    ident = _bf(np.eye(128, dtype=np.float32))
    onesb = _bf(np.ones((128, 1), np.float32))
    ones1 = np.ones((1, 128), np.float32)

    shared = {
        "wq": wq_dev, "wkv": wkv_dev, "wo": wo_dev,
        "w1": w1_dev, "w3": w3_dev, "w2": w2_dev,
        "a1d": a1_dev, "a3d": a3_dev, "a2d": a2_dev,
        "b1d": b1_dev, "b3d": b3_dev, "b2d": b2_dev,
        "krows": krows_dev, "bl2": bl2_dev,
        "identd": ident, "onesd": onesb, "ones1d": ones1,
    }
    in_maps = []
    for core in range(8):
        m = dict(shared)
        m["x_fm"] = x_fm[core // 2]
        m["x_own"] = xown_cores[core]
        m["qpos"] = qpos_cores[core]
        m["qmask"] = qmask_cores[core]
        in_maps.append(m)
    return in_maps


def _assemble(results):
    out = np.empty((4, S, DM), np.float32)
    for core in range(8):
        half = core % 2
        b = core // 2
        qts = QT_HALF[half]
        tok_idx = np.concatenate([np.arange(128 * t, 128 * (t + 1)) for t in qts])
        yc = np.asarray(results[core]["y"], np.float32).reshape(DM, 1024)
        out[b, tok_idx, :] = yc.T
    return out


_cached = {}


def kernel(**inputs):
    import os
    debug = bool(os.environ.get("KDBG"))
    key = ("nc", debug)
    if key not in _cached:
        _cached[key] = build_program(debug=debug)
    nc = _cached[key]
    in_maps = _prep(inputs)
    res = run_bass_kernel_spmd(nc, in_maps, core_ids=list(range(8)))
    if debug:
        kernel.last_results = res.results
    return _assemble(res.results)


# version 07
# speedup vs baseline: 1.7466x; 1.0121x over previous
import math
import numpy as np
import ml_dtypes

import concourse.bass as bass
import concourse.mybir as mybir
from concourse.bass_utils import run_bass_kernel_spmd

F32 = mybir.dt.float32
BF16 = mybir.dt.bfloat16
AF = mybir.ActivationFunctionType
AX = mybir.AxisListType
OP = mybir.AluOpType

NH = 32
DC = 32
DM = 4096
S = 2048
TOK = 512
NT = 4
NPASS = 2
H2 = 2816
HC = 88
CC = 22
HEAD_SCALE = 1.0 / math.sqrt(128.0)
EPS = 1e-5
NEGBIG = -1.0e9

QT_HALF = [[0, 2, 4, 6, 9, 11, 13, 15], [1, 3, 5, 7, 8, 10, 12, 14]]

_ALIBI_START = 2.0 ** (-(2.0 ** -(math.log2(NH) - 3)))
SLOPES_F32 = [float(np.float32(_ALIBI_START ** (i + 1))) for i in range(NH)]


def prog_T(p, l):
    return (2 * l + 2) if p == 0 else (10 + 2 * l)



class Em:
    COMPUTE = ("pe", "act", "dve")
    QUEUES = ("sy", "gp")

    def __init__(self, nc):
        self.nc = nc
        self.sems = {e: nc.alloc_semaphore(f"sem_{e}") for e in self.COMPUTE}
        self.cnt = {e: 0 for e in self.COMPUTE}
        self.dq = {}
        self.ops = {e: [] for e in self.COMPUTE + self.QUEUES}
        self.buf = {}
        self.last_wait = {}

    def emit(self, e, fn, reads=(), writes=(), dsem=None):
        if e in self.QUEUES:
            assert dsem is not None
            if dsem not in self.sems:
                self.sems[dsem] = self.nc.alloc_semaphore(f"sem_{dsem}")
                self.cnt[dsem] = 0
                self.dq[dsem] = e
            assert self.dq[dsem] == e
            key, inc = dsem, 16
        else:
            assert dsem is None
            key, inc = e, 1

        waits = {}

        def add(tok):
            if tok is None:
                return
            tk, tv = tok
            if tk == e and e == "pe":
                return
            waits[tk] = max(waits.get(tk, 0), tv)

        for b in reads:
            st = self.buf.get(b)
            if st is not None:
                add(st["w"])
        for b in writes:
            st = self.buf.get(b)
            if st is not None:
                add(st["w"])
                for t in st["r"]:
                    add(t)
        lw = self.last_wait.setdefault(e, {})
        waits = {k: v for k, v in waits.items() if lw.get(k, -1) < v}
        for k, v in waits.items():
            lw[k] = v
        self.cnt[key] += inc
        tok = (key, self.cnt[key])
        self.ops[e].append((waits, fn, key, inc))
        for b in reads:
            self.buf.setdefault(b, {"w": None, "r": []})["r"].append(tok)
        for b in writes:
            self.buf[b] = {"w": tok, "r": []}
        return tok

    def carve(self, parent, children):
        st = self.buf.get(parent, {"w": None, "r": []})
        for c in children:
            self.buf[c] = {"w": st["w"], "r": list(st["r"])}

    def merge(self, parent, children):
        toks = []
        for c in children:
            st = self.buf.pop(c, None)
            if st:
                if st["w"]:
                    toks.append(st["w"])
                toks.extend(st["r"])
        self.buf[parent] = {"w": None, "r": toks}

    def coalesce(self, names, tok):
        for b in names:
            self.buf.setdefault(b, {"w": None, "r": []})["w"] = tok

    def finalize(self):
        nc = self.nc
        blk_cm = nc.Block()
        blk = blk_cm.__enter__()
        eng_attr = {"pe": "tensor", "act": "scalar", "dve": "vector",
                    "sy": "sync", "gp": "gpsimd"}
        for ename in ("sy", "gp", "pe", "act", "dve"):
            ops = self.ops[ename]
            fin = [(d, self.cnt[d]) for d, q in self.dq.items() if q == ename]

            def body(eng, _ops=ops, _fin=fin, _sems=self.sems):
                for waits, fn, key, inc in _ops:
                    for se, v in waits.items():
                        eng.wait_ge(_sems[se], v)
                    inst = fn(eng)
                    inst.then_inc(_sems[key], inc)
                for (d, v) in _fin:
                    eng.wait_ge(_sems[d], v)

            getattr(blk, eng_attr[ename])(body)
        blk_cm.__exit__(None, None, None)



def build_program(debug=False):
    nc = bass.Bass("TRN2", target_bir_lowering=False, debug=False)

    def din(name, shape, dt=F32):
        return nc.dram_tensor(name, list(shape), dt, kind="ExternalInput").ap()

    x_fm = din("x_fm", (DM, S))
    x_own = din("x_own", (DM, 1024))
    wq = din("wq", (NH, 128, DM))
    wkv = din("wkv", (2, 128, DM))
    wo = din("wo", (DC, 128, DM))
    w1 = din("w1", (HC, 128, 1152))
    w3 = din("w3", (HC, 128, 1152))
    w2 = din("w2", (DC, 128, H2))
    a1d = din("a1d", (128, 64))
    a3d = din("a3d", (128, 64))
    a2d = din("a2d", (128, 2 * HC))
    b1d = din("b1d", (2, 128 * HC))
    b3d = din("b3d", (2, 128 * HC))
    b2d = din("b2d", (2, DM))
    krows = din("krows", (8, S), BF16)
    bl2 = din("bl2", (NH, 8, 128), BF16)
    qpos = din("qpos", (128, 8))
    qmask = din("qmask", (128, 512), BF16)
    identd = din("identd", (128, 128), BF16)
    onesd = din("onesd", (128, 1), BF16)
    ones1d = din("ones1d", (1, 128))
    y = nc.dram_tensor("y", [DC, 128, 1024], F32, kind="ExternalOutput").ap()
    dbg = {}
    if debug:
        dbg["k"] = nc.dram_tensor("dbg_k", [128, S], F32, kind="ExternalOutput").ap()
        dbg["q"] = nc.dram_tensor("dbg_q", [128, 6144], F32, kind="ExternalOutput").ap()
        dbg["at"] = nc.dram_tensor("dbg_at", [128, 16384], F32, kind="ExternalOutput").ap()
        dbg["x1"] = nc.dram_tensor("dbg_x1", [128, 16384], F32, kind="ExternalOutput").ap()
        dbg["at2"] = nc.dram_tensor("dbg_at2", [128, 16384], F32, kind="ExternalOutput").ap()
        dbg["x12"] = nc.dram_tensor("dbg_x12", [128, 16384], F32, kind="ExternalOutput").ap()
        dbg["q2"] = nc.dram_tensor("dbg_q2", [128, 6144], F32, kind="ExternalOutput").ap()

    sb = nc.alloc_sbuf_tensor
    arena_a = sb("arena_a", [128, 32 * TOK], BF16)
    arena_b = sb("arena_b", [128, 32 * TOK], BF16)
    arena_q = sb("arena_q", [128, 12 * TOK], BF16)
    P_sb = sb("P_sb", [128, S], BF16)
    P2_sb = sb("P2_sb", [128, S], BF16)
    PTS = sb("PTS", [128, S], BF16)
    K_sb = sb("K_sb", [128, S], BF16)
    VT_sb = sb("VT_sb", [128, S], BF16)
    xprime = sb("xprime", [128, 32 * TOK], F32)
    XR = [sb(f"XR{i}", [128, TOK], F32) for i in range(2)]
    XB = [sb(f"XB{i}", [128, TOK], BF16) for i in range(2)]
    WA = [sb(f"WA{i}", [128, 2048], BF16) for i in range(4)]
    WB = [sb(f"WB{i}", [128, 1152], BF16) for i in range(4)]
    KR = sb("KR", [8, S], BF16)
    BL = [sb(f"BL{i}", [8, 128], BF16) for i in range(2)]
    QPOS = sb("QPOS", [128, 8], F32)
    QSQ = [sb(f"QSQ{i}", [128, TOK], BF16) for i in range(2)]
    QM = sb("QM", [128, 512], BF16)
    ID = sb("ID", [128, 128], BF16)
    ONEB = sb("ONEB", [128, 1], BF16)
    ONE1 = sb("ONE1", [1, 128], F32)
    A1 = sb("A1", [128, 64], BF16)
    A3 = sb("A3", [128, 64], BF16)
    A2 = sb("A2", [128, 2 * HC], BF16)
    T1 = sb("T1", [2, TOK], BF16)
    T3 = sb("T3", [2, TOK], BF16)
    T2 = sb("T2", [2, TOK], BF16)
    NB = sb("NB", [128, 4 * NH], F32)
    KN = sb("KN", [1, 1], F32)
    SR2 = sb("SR2", [1, TOK], F32)
    RSP = sb("RSP", [128, 8], F32)
    RS = sb("RS", [128, 1], F32)
    RR = sb("RR", [128, 1], F32)
    GSC = sb("GSC", [128, TOK], BF16)

    ps_big = nc.alloc_psum_tensor("ps_big", [128, 2048], F32)
    P4 = nc.alloc_psum_tensor("P4", [128, 512], F32)
    P5 = nc.alloc_psum_tensor("P5", [128, 512], F32)
    PT0 = nc.alloc_psum_tensor("PT0", [128, 1024], BF16)
    PT1 = nc.alloc_psum_tensor("PT1", [128, 1024], BF16)

    em = Em(nc)
    E = em.emit

    def G(i):
        return ps_big[:, 512 * i:512 * (i + 1)]

    cur = {}

    def XAr(i):
        return cur["XA"][:, TOK * i:TOK * (i + 1)]

    def ABr(i):
        return cur["AB"][:, TOK * i:TOK * (i + 1)]

    def xan(i):
        return f'{cur["xp"]}{i}'

    def abn(i):
        return f'{cur["bp"]}{i}'

    def Qr(i):
        return arena_q[:, TOK * i:TOK * (i + 1)]

    E("sy", lambda e: e.dma_start(out=ID[:], in_=identd), writes=["ID"], dsem="CS")
    E("sy", lambda e: e.dma_start(out=ONEB[:], in_=onesd), writes=["ONEB"], dsem="CS")
    E("sy", lambda e: e.dma_start(out=ONE1[:], in_=ones1d), writes=["ONE1"], dsem="CS")
    E("sy", lambda e: e.dma_start(out=QM[:], in_=qmask), writes=["QM"], dsem="CS")
    E("sy", lambda e: e.dma_start(out=KR[:], in_=krows), writes=["KR"], dsem="CS")
    tk = E("sy", lambda e: e.dma_start(out=QPOS[:], in_=qpos), writes=["QPOS"], dsem="CS")
    em.coalesce(["ID", "ONEB", "ONE1", "QM", "KR", "QPOS"], tk)
    E("gp", lambda e: e.dma_start(out=A1[:], in_=a1d), writes=["LA1"], dsem="LC")
    E("gp", lambda e: e.dma_start(out=A3[:], in_=a3d), writes=["LA3"], dsem="LC")
    tk = E("gp", lambda e: e.dma_start(out=A2[:], in_=a2d), writes=["LA2"], dsem="LC")
    em.coalesce(["LA1", "LA3", "LA2"], tk)

    wa_names = [f"WA{i}" for i in range(4)]
    wb_names = [f"WB{i}" for i in range(4)]

    def load_wa_block(src_ap, slots):
        ncols = src_ap.shape[1]
        half = ncols // 2
        for j, sl in enumerate(slots):
            E("gp",
              lambda e, s=sl, j=j, src=src_ap, h=half:
                  e.dma_start(out=WA[s][:, 0:h], in_=src[:, j * h:(j + 1) * h]),
              writes=[wa_names[sl]], dsem=f"WA{sl}")

    def q_proj_mms(h, p, wslots, part):
        xa = cur["XA"]

        def fn(e, h=h, p=p, ws=wslots, part=part, xa=xa):
            mm = None
            for c in range(8 * part, 8 * part + 8):
                sl = ws[c // 16]
                tile = WA[sl][:, 128 * (c % 16):128 * (c % 16) + 128]
                mm = e.matmul(P4[:], tile, xa[:, TOK * c:TOK * (c + 1)],
                              start=(c == 0), stop=(c == DC - 1))
            return mm
        reads = [wa_names[s] for s in set(wslots)] + [xan(c) for c in range(8 * part, 8 * part + 8)]
        E("pe", fn, reads=reads, writes=(["P4", "P4k"] if part == 0 else ["P4k"]))

    def q_evac(h):
        s = h % 12
        E("act",
          lambda e, s=s: e.activation(Qr(s), P4[:], AF.Copy, bias=0.0, scale=HEAD_SCALE),
          reads=["P4", "P4k"], writes=[f"Q{s}"])

    def negb_a(h):
        s = h % 12
        r = h % 2
        E("act", lambda e, s=s, r=r: e.activation(QSQ[r][:], Qr(s), AF.Square, bias=0.0, scale=1.0),
          reads=[f"Q{s}"], writes=[f"QSQ{r}"])

    def negb_b12(h):
        r = h % 2
        E("pe", lambda e, r=r: e.matmul(P5[0:1, :], ONEB[:], QSQ[r][:], start=True, stop=True),
          reads=[f"QSQ{r}", "ONEB"], writes=["P5", "P5k"])
        E("dve", lambda e: e.tensor_reduce(
            SR2[0:1, 0:4], P5[0:1, 0:512].rearrange("p (g k) -> p g k", k=128),
            axis=AX.X, op=OP.max),
          reads=["P5", "P5k"], writes=["SR2"])
        E("act", lambda e: e.activation(SR2[0:1, 0:4], SR2[0:1, 0:4], AF.Sqrt,
                                        bias=0.0, scale=1.0),
          reads=["SR2"], writes=["SR2"])
        E("dve", lambda e: e.tensor_scalar(SR2[0:1, 0:4], SR2[0:1, 0:4],
                                           KN[0:1, 0:1], -1.0, op0=OP.mult, op1=OP.mult),
          reads=["SR2", "KN"], writes=["SR2"])

    def negb_b3(h, p):
        E("pe", lambda e: e.matmul(P5[:, 0:4], ONE1[:], SR2[0:1, 0:4],
                                   start=True, stop=True),
          reads=["SR2", "ONE1"], writes=["P5", "P5k"])
        E("dve", lambda e, h=h, p=p: e.scalar_tensor_tensor(
            NB[:, 4 * h:4 * h + 4], QPOS[:, 4 * p:4 * p + 4], -SLOPES_F32[h],
            P5[:, 0:4], op0=OP.mult, op1=OP.add),
          reads=["QPOS", "P5", "P5k"], writes=[f"NB{h}"])

    wkv_slots = [0, 1, 2, 3]

    wa_ctr = 0
    for p in range(NPASS):
        if p == 0:
            cur.update(XA=arena_a, xp="A", AB=arena_b, bp="B")
        else:
            cur.update(XA=arena_b, xp="B", AB=arena_a, bp="A")
        for c in range(DC):
            tk = E("gp",
                   lambda e, c=c, p=p, xa=cur["XA"]: e.dma_start(
                       out=xa[:, TOK * c:TOK * (c + 1)],
                       in_=x_own[128 * c:128 * (c + 1), TOK * p:TOK * (p + 1)]),
                   writes=[xan(c)], dsem="XBL")
        em.coalesce([xan(c) for c in range(DC)], tk)

        if p == 0:
            load_wa_block(wkv[0], [0, 1])
            load_wa_block(wkv[1], [2, 3])
            wa_ctr = 2
            for half in range(2):
                for c in range(DC):
                    i = c % 4
                    kb = [abn(2 * i), abn(2 * i + 1)]
                    kvbuf = cur["AB"][:, 1024 * i:1024 * (i + 1)]
                    E("gp",
                      lambda e, c=c, half=half, kvbuf=kvbuf: e.dma_start(
                          out=kvbuf, in_=x_fm[128 * c:128 * (c + 1),
                                              1024 * half:1024 * (half + 1)]),
                      writes=kb, dsem=f"KV{i}")

                    def kv_fn(e, c=c, kvbuf=kvbuf):
                        mm = None
                        kt = WA[c // 16][:, 128 * (c % 16):128 * (c % 16) + 128]
                        vt = WA[2 + c // 16][:, 128 * (c % 16):128 * (c % 16) + 128]
                        for j in range(2):
                            mm = e.matmul(G(j), kt, kvbuf[:, 512 * j:512 * (j + 1)],
                                          start=(c == 0), stop=(c == DC - 1))
                            mm = e.matmul(G(2 + j), vt, kvbuf[:, 512 * j:512 * (j + 1)],
                                          start=(c == 0), stop=(c == DC - 1))
                        return mm
                    E("pe", kv_fn,
                      reads=kb + wa_names,
                      writes=(["G0", "G1", "G2", "G3", "Gk"] if c == 0 else ["Gk"]))
                E("act",
                  lambda e, half=half: e.activation(
                      K_sb[:, 1024 * half:1024 * (half + 1)], ps_big[:, 0:1024],
                      AF.Copy, bias=0.0, scale=1.0),
                  reads=["G0", "G1", "Gk"], writes=["K"])
                E("act",
                  lambda e: e.activation(P_sb[:, 0:1024], ps_big[:, 1024:2048],
                                         AF.Copy, bias=0.0, scale=1.0),
                  reads=["G2", "G3", "Gk"], writes=["P0"])

                def vtr_fn(e, half=half):
                    mm = None
                    for k in range(8):
                        mm = e.transpose(PT0[:, 128 * k:128 * (k + 1)],
                                         P_sb[:, 128 * k:128 * (k + 1)], ID[:])
                    return mm
                E("pe", vtr_fn, reads=["P0", "ID"], writes=["PT0"])
                E("dve",
                  lambda e, half=half: e.tensor_copy(
                      VT_sb[:, 1024 * half:1024 * (half + 1)], PT0[:]),
                  reads=["PT0"], writes=["VT"])

            for c in range(4):
                E("act", lambda e, c=c: e.activation(QSQ[c % 2][:], K_sb[:, 512 * c:512 * (c + 1)],
                                                     AF.Square, bias=0.0, scale=1.0),
                  reads=["K"], writes=[f"QSQ{c % 2}"])
                E("pe", lambda e, c=c: e.matmul(ps_big[0:1, 512 * c:512 * (c + 1)],
                                                ONEB[:], QSQ[c % 2][:], start=True, stop=True),
                  reads=[f"QSQ{c % 2}", "ONEB"], writes=[f"G{c}"])
            E("dve", lambda e: e.reduce_max(KN[0:1, 0:1], ps_big[0:1, :], axis=AX.X),
              reads=["G0", "G1", "G2", "G3"], writes=["KN"])
            E("act", lambda e: e.activation(KN[0:1, 0:1], KN[0:1, 0:1], AF.Sqrt,
                                            bias=0.0, scale=1.0),
              reads=["KN"], writes=["KN"])
            E("dve", lambda e: e.tensor_scalar_mul(KN[0:1, 0:1], KN[0:1, 0:1], 1.0005),
              reads=["KN"], writes=["KN"])
            if debug:
                E("gp", lambda e: e.dma_start(out=dbg["k"], in_=K_sb[:]), reads=["K"], dsem="DBGK")

        nbq = []
        for h in range(12):
            blk_src = wq[h]
            slots = [(2 * wa_ctr) % 4, (2 * wa_ctr + 1) % 4]
            wa_ctr += 1
            load_wa_block(blk_src, slots)
            for part in range(4):
                q_proj_mms(h, p, slots, part)
            if nbq:
                negb_b3(nbq.pop(0), p)
            q_evac(h)
            negb_a(h)
            negb_b12(h)
            nbq.append(h)

        for h in range(NH):
            s = h % 12
            bl = h % 2
            E("sy", lambda e, h=h, bl=bl: e.dma_start(out=BL[bl][:], in_=bl2[h]),
              writes=[f"BL{bl}"], dsem=f"BLD{bl}")

            fill = (1 <= h <= 20)
            if fill:
                fh = h + 11
                fslots = [(2 * wa_ctr) % 4, (2 * wa_ctr + 1) % 4]
                wa_ctr += 1
                load_wa_block(wq[fh], fslots)

            pt_ctr = 0

            def geom(l):
                T = prog_T(p, l)
                L = 128 * T
                nsl = (L + 511) // 512
                return T, L, nsl

            def bank(l, sl):
                return 2 * (l % 2) + sl % 2

            def pbuf(l):
                return (P_sb, "P0") if l % 2 == 0 else (P2_sb, "P1")

            def emit_S_slice(l, sl, s=s, bl=bl):
                T, L, nsl = geom(l)
                c0 = 512 * sl
                c1 = min(512 * (sl + 1), L)
                bk = bank(l, sl)

                def s_fn(e, s=s, l=l, c0=c0, c1=c1, bk=bk, bl=bl):
                    e.matmul(G(bk)[:, 0:c1 - c0],
                             Qr(s)[:, 128 * l:128 * (l + 1)],
                             K_sb[:, c0:c1], start=True, stop=False)
                    return e.matmul(G(bk)[:, 0:c1 - c0],
                                    BL[bl][:], KR[0:8, c0:c1], start=False, stop=True)
                E("pe", s_fn, reads=[f"Q{s}", "K", f"BL{bl}"], writes=[f"G{bk}"])
                if sl == nsl - 1:
                    w = c1 - c0
                    E("dve",
                      lambda e, w=w, p=p, bk=bk: e.tensor_tensor(
                          G(bk)[:, w - 256:w], G(bk)[:, w - 256:w],
                          QM[:, 256 * p:256 * (p + 1)], op=OP.add),
                      reads=["QM", f"G{bk}"], writes=[f"G{bk}"])

            def emit_exp(l, sl, h=h):
                T, L, nsl = geom(l)
                pb, pn = pbuf(l)
                c0 = 512 * sl
                c1 = min(512 * (sl + 1), L)
                bk = bank(l, sl)
                ri = 4 * (l % 2) + sl
                E("act",
                  lambda e, c0=c0, c1=c1, h=h, l=l, sl=sl, bk=bk, pb=pb, ri=ri: e.activation(
                      pb[:, c0:c1], G(bk)[:, 0:c1 - c0], AF.Exp,
                      bias=NB[:, 4 * h + l:4 * h + l + 1], scale=1.0,
                      accum_out=RSP[:, ri:ri + 1]),
                  reads=[f"G{bk}", f"NB{h}"], writes=[pn, f"RSP{ri}"])

            def emit_tile_chain(l):
                T, L, nsl = geom(l)
                pb, pn = pbuf(l)
                r0 = 4 * (l % 2)
                E("dve", lambda e, nsl=nsl, r0=r0: e.reduce_sum(RS[:], RSP[:, r0:r0 + nsl], axis=AX.X),
                  reads=[f"RSP{r0 + i}" for i in range(nsl)], writes=["RS"])
                E("dve", lambda e: e.reciprocal(RR[:], RS[:]), reads=["RS"], writes=["RR"])
                E("dve", lambda e, L=L, pb=pb: e.tensor_scalar_mul(pb[:, 0:L], pb[:, 0:L], RR[:]),
                  reads=[pn, "RR"], writes=[pn])

            def emit_tile_full(l):
                T, L, nsl = geom(l)
                for sl in range(nsl):
                    emit_S_slice(l, sl)
                    emit_exp(l, sl)
                emit_tile_chain(l)

            def emit_tile_pair(la, lb):
                Ta, La, nsla = geom(la)
                Tb, Lb, nslb = geom(lb)
                for sl in range(max(nsla, nslb)):
                    if sl < nsla:
                        emit_S_slice(la, sl)
                    if sl < nslb:
                        emit_S_slice(lb, sl)
                    if sl < nsla:
                        emit_exp(la, sl)
                    if sl < nslb:
                        emit_exp(lb, sl)
                emit_tile_chain(la)
                emit_tile_chain(lb)

            def emit_tile_start(l):
                T, L, nsl = geom(l)
                for sl in range(min(2, nsl)):
                    emit_S_slice(l, sl)

            def emit_tile_rest(l):
                T, L, nsl = geom(l)
                for sl in range(min(2, nsl)):
                    emit_exp(l, sl)
                for sl in range(2, nsl):
                    emit_S_slice(l, sl)
                    emit_exp(l, sl)
                emit_tile_chain(l)

            def emit_T(l):
                nonlocal pt_ctr
                T, L, nsl = geom(l)
                pb, pn = pbuf(l)
                for g in range((T + 7) // 8):
                    k0 = 8 * g
                    k1 = min(8 * g + 8, T)
                    pt = pt_ctr % 2
                    pt_ctr += 1

                    def tr_fn(e, k0=k0, k1=k1, pt=pt, pb=pb):
                        mm = None
                        for k in range(k0, k1):
                            mm = e.transpose((PT0 if pt == 0 else PT1)[:, 128 * (k - k0):128 * (k - k0 + 1)],
                                             pb[:, 128 * k:128 * (k + 1)], ID[:])
                        return mm
                    E("pe", tr_fn, reads=[pn, "ID"], writes=[f"PT{pt}"])
                    E("dve",
                      lambda e, k0=k0, k1=k1, pt=pt: e.tensor_copy(
                          PTS[:, 128 * k0:128 * k1],
                          (PT0 if pt == 0 else PT1)[:, 0:128 * (k1 - k0)]),
                      reads=[f"PT{pt}"], writes=["PTS"])

            def emit_AV(l):
                T, L, nsl = geom(l)

                def av_fn(e, T=T, l=l):
                    mm = None
                    for kt in range(T):
                        mm = e.matmul(P5[:, 128 * l:128 * (l + 1)],
                                      VT_sb[:, 128 * kt:128 * (kt + 1)],
                                      PTS[:, 128 * kt:128 * (kt + 1)],
                                      start=(kt == 0), stop=(kt == T - 1))
                    return mm
                E("pe", av_fn, reads=["PTS", "VT"],
                  writes=(["P5", "P5k"] if l == 0 else ["P5k"]))

            emit_tile_pair(0, 1)
            emit_T(0)
            if nbq:
                negb_b3(nbq.pop(0), p)
            emit_tile_start(2)
            emit_tile_rest(2)
            emit_AV(0)
            if fill:
                q_proj_mms(fh, p, fslots, 0)
            emit_T(1)
            emit_tile_start(3)
            emit_tile_rest(3)
            emit_AV(1)
            if fill:
                q_proj_mms(fh, p, fslots, 1)
            emit_T(2)
            if fill:
                q_proj_mms(fh, p, fslots, 2)
            emit_AV(2)
            emit_T(3)
            if fill:
                q_proj_mms(fh, p, fslots, 3)
                q_evac(fh)
            emit_AV(3)
            E("act", lambda e, h=h, ab=cur["AB"]: e.activation(
                ab[:, TOK * h:TOK * (h + 1)], P5[:], AF.Copy, bias=0.0, scale=1.0),
              reads=["P5", "P5k"], writes=[abn(h)])
            if fill:
                negb_a(fh)
                negb_b12(fh)
                nbq.append(fh)

        if debug:
            E("gp", lambda e, p=p: e.dma_start(out=dbg["q" if p == 0 else "q2"], in_=arena_q[:]),
              reads=[f"Q{i}" for i in range(12)], dsem="DBGQ")
            E("gp", lambda e, p=p, ab=cur["AB"]: e.dma_start(
                out=dbg["at" if p == 0 else "at2"], in_=ab[:]),
              reads=[abn(i) for i in range(NH)], dsem="DBGA")

        stq = []
        em.carve("PTS", ["PTc", "PTd", "PTe", "PTf"])
        em.carve("P0", ["PSa", "PSb"])
        em.carve("P1", ["GSa", "GSb", "XR2"])
        XBS = [(XB[0][:], "XB0"), (XB[1][:], "XB1"),
               (PTS[:, 0:512], "PTc"), (PTS[:, 512:1024], "PTd")]
        SQS = [(QSQ[0][:], "QSQ0"), (QSQ[1][:], "QSQ1"),
               (PTS[:, 1024:1536], "PTe"), (PTS[:, 1536:2048], "PTf")]
        G1S = [(XAr(30), xan(30)), (XAr(31), xan(31)),
               (P_sb[:, 0:512], "PSa"), (P_sb[:, 512:1024], "PSb")]
        GSCS = [(GSC[:], "GSC"), (P2_sb[:, 0:512], "GSa"), (P2_sb[:, 512:1024], "GSb")]
        XRS = [(XR[0][:], "XR0"), (XR[1][:], "XR1"),
               (P2_sb[:, 1024:2048].bitcast(F32), "XR2")]

        def stats_a(c):
            xb, xbn = XBS[c % 4]
            sq, sqn = SQS[c % 4]
            E("act", lambda e, c=c, xb=xb: e.activation(xb, xprime[:, TOK * c:TOK * (c + 1)],
                                                        AF.Copy, bias=0.0, scale=1.0),
              reads=[f"X{c}"], writes=[xbn])
            E("act", lambda e, c=c, sq=sq: e.activation(sq, xprime[:, TOK * c:TOK * (c + 1)],
                                                        AF.Square, bias=0.0, scale=1.0),
              reads=[f"X{c}"], writes=[sqn])

        def stats_b(c):
            xb, xbn = XBS[c % 4]
            sq, sqn = SQS[c % 4]
            E("pe", lambda e, c=c, xb=xb: e.matmul(ps_big[0:1, 0:512], ONEB[:], xb,
                                                   start=(c == 0), stop=(c == DC - 1)),
              reads=[xbn, "ONEB"], writes=(["G0", "Gk"] if c == 0 else ["Gk"]))
            E("pe", lambda e, c=c, sq=sq: e.matmul(ps_big[0:1, 512:1024], ONEB[:], sq,
                                                   start=(c == 0), stop=(c == DC - 1)),
              reads=[sqn, "ONEB"], writes=(["G1"] if c == 0 else ["Gk"]))

        def ln_rows_and_apply(out_to_ob):
            r0 = XR[0][0:1, :]
            r1 = XR[1][0:1, :]
            E("act", lambda e, r0=r0: e.activation(r0, ps_big[0:1, 0:512], AF.Copy,
                                                   bias=0.0, scale=1.0 / DM),
              reads=["G0", "Gk"], writes=["XR0"])
            E("act", lambda e, r1=r1: e.activation(r1, ps_big[0:1, 512:1024], AF.Copy,
                                                   bias=0.0, scale=1.0 / DM),
              reads=["G1", "Gk"], writes=["XR1"])
            E("act", lambda e, r0=r0: e.activation(SR2[:], r0, AF.Square, bias=0.0, scale=1.0),
              reads=["XR0"], writes=["SR2"])
            E("dve", lambda e, r1=r1: e.tensor_tensor(r1, r1, SR2[:], op=OP.subtract),
              reads=["XR1", "SR2"], writes=["XR1"])
            E("dve", lambda e, r1=r1: e.tensor_scalar_add(r1, r1, EPS),
              reads=["XR1"], writes=["XR1"])
            E("act", lambda e, r1=r1: e.activation(r1, r1, AF.Sqrt, bias=0.0, scale=1.0),
              reads=["XR1"], writes=["XR1"])
            E("dve", lambda e, r1=r1: e.reciprocal(r1, r1), reads=["XR1"], writes=["XR1"])
            E("dve", lambda e, r0=r0, r1=r1: e.tensor_tensor(r0, r0, r1, op=OP.mult),
              reads=["XR0", "XR1"], writes=["XR0"])
            E("pe", lambda e, r1=r1: e.matmul(ps_big[:, 1024:1536], ONE1[:], r1,
                                              start=True, stop=True),
              reads=["XR1", "ONE1"], writes=["G2"])
            E("pe", lambda e, r0=r0: e.matmul(ps_big[:, 1536:2048], ONE1[:], r0,
                                              start=True, stop=True),
              reads=["XR0", "ONE1"], writes=["G3"])
            for c in range(DC):
                if not out_to_ob:
                    E("dve", lambda e, c=c: e.tensor_tensor(
                        xprime[:, TOK * c:TOK * (c + 1)], xprime[:, TOK * c:TOK * (c + 1)],
                        ps_big[:, 1024:1536], op=OP.mult),
                      reads=[f"X{c}", "G2"], writes=[f"X{c}"])
                    E("dve", lambda e, c=c: e.tensor_tensor(
                        xprime[:, TOK * c:TOK * (c + 1)], xprime[:, TOK * c:TOK * (c + 1)],
                        ps_big[:, 1536:2048], op=OP.subtract),
                      reads=[f"X{c}", "G3"], writes=[f"X{c}"])
                else:
                    xr, xrn = XRS[c % 3]
                    E("dve", lambda e, c=c, xr=xr: e.tensor_tensor(
                        xr, xprime[:, TOK * c:TOK * (c + 1)],
                        ps_big[:, 1024:1536], op=OP.mult),
                      reads=[f"X{c}", "G2"], writes=[xrn])
                    E("dve", lambda e, c=c, xr=xr: e.tensor_tensor(
                        xr, xr, ps_big[:, 1536:2048], op=OP.subtract),
                      reads=[xrn, "G3"], writes=[xrn])
                    E("sy", lambda e, c=c, xr=xr, p=p: e.dma_start(
                        out=y[c][:, TOK * p:TOK * (p + 1)], in_=xr),
                      reads=[xrn], dsem=f"O{xrn}")

        for do in range(DC):
            slots = [(2 * wa_ctr) % 4, (2 * wa_ctr + 1) % 4]
            wa_ctr += 1
            load_wa_block(wo[do], slots)
            pso = [ps_big[:, 1024:1536], ps_big[:, 1536:2048], P4[:], P5[:]][do % 4]
            pn = ["G2", "G3", "P4", "P5"][do % 4]

            def wo_fn(e, slots=slots, pso=pso, ab=cur["AB"]):
                mm = None
                for c in range(DC):
                    sl = slots[c // 16]
                    mm = e.matmul(pso, WA[sl][:, 128 * (c % 16):128 * (c % 16) + 128],
                                  ab[:, TOK * c:TOK * (c + 1)], start=(c == 0), stop=(c == DC - 1))
                return mm
            E("pe", wo_fn,
              reads=[wa_names[s] for s in set(slots)] + [abn(c) for c in range(DC)],
              writes=[pn])
            xr, xrn = XRS[do % 3]
            E("sy", lambda e, do=do, p=p, xr=xr: e.dma_start(
                out=xr, in_=x_own[128 * do:128 * (do + 1), TOK * p:TOK * (p + 1)]),
              writes=[xrn], dsem=f"R{xrn}")
            E("dve", lambda e, do=do, pso=pso, xr=xr: e.tensor_tensor(
                xprime[:, TOK * do:TOK * (do + 1)], xr, pso, op=OP.add),
              reads=[pn, xrn], writes=[f"X{do}"])
            if len(stq) >= 3:
                stats_b(stq.pop(0))
            stats_a(do)
            stq.append(do)

        while stq:
            stats_b(stq.pop(0))
        ln_rows_and_apply(out_to_ob=False)
        if debug:
            E("gp", lambda e, p=p: e.dma_start(out=dbg["x1" if p == 0 else "x12"], in_=xprime[:]),
              reads=[f"X{c}" for c in range(DC)], dsem="DBGX")

        for jc in range(8):
            E("dve", lambda e, jc=jc: e.tensor_tensor(
                XR[0][:], xprime[:, TOK * jc:TOK * (jc + 1)],
                xprime[:, TOK * (8 + jc):TOK * (9 + jc)], op=OP.add),
              reads=[f"X{jc}", f"X{8 + jc}"], writes=["XR0"])
            E("dve", lambda e, jc=jc: e.tensor_tensor(
                XR[0][:], XR[0][:], xprime[:, TOK * (16 + jc):TOK * (17 + jc)], op=OP.add),
              reads=["XR0", f"X{16 + jc}"], writes=["XR0"])
            E("dve", lambda e, jc=jc, xa=cur["XA"]: e.tensor_tensor(
                xa[:, TOK * jc:TOK * (jc + 1)], XR[0][:],
                xprime[:, TOK * (24 + jc):TOK * (25 + jc)], op=OP.add),
              reads=["XR0", f"X{24 + jc}"], writes=[xan(jc)])
        for c in range(DC):
            xb, xbn = XBS[c % 4]
            E("act", lambda e, c=c, xb=xb: e.activation(xb, xprime[:, TOK * c:TOK * (c + 1)],
                                                        AF.Copy, bias=0.0, scale=1.0),
              reads=[f"X{c}"], writes=[xbn])
            E("pe", lambda e, c=c, xb=xb: e.matmul(P4[0:2, :], A1[:, 2 * c:2 * c + 2], xb,
                                            start=(c == 0), stop=(c == DC - 1)),
              reads=[xbn, "LA1"], writes=(["P4", "P4k"] if c == 0 else ["P4k"]))
            E("pe", lambda e, c=c, xb=xb: e.matmul(P5[0:2, :], A3[:, 2 * c:2 * c + 2], xb,
                                            start=(c == 0), stop=(c == DC - 1)),
              reads=[xbn, "LA3"], writes=(["P5", "P5k"] if c == 0 else ["P5k"]))
        E("act", lambda e: e.activation(T1[:], P4[0:2, :], AF.Copy, bias=0.0, scale=0.5),
          reads=["P4", "P4k"], writes=["T1"])
        E("act", lambda e: e.activation(T3[:], P5[0:2, :], AF.Copy, bias=0.0, scale=0.5),
          reads=["P5", "P5k"], writes=["T3"])

        wb_ctr = 0
        t2q = []

        def emit_t2(hc, grd, gap):
            E("pe", lambda e, hc=hc, gap=gap: e.matmul(
                P4[0:2, :], A2[:, 2 * hc:2 * hc + 2], gap,
                start=(hc == 0), stop=(hc == HC - 1)),
              reads=[grd, "LA2"], writes=(["P4", "P4k"] if hc == 0 else ["P4k"]))

        for hc in range(HC):
            m, cc = hc // CC, hc % CC
            s1 = wb_ctr % 4
            s3 = (wb_ctr + 1) % 4
            wb_ctr += 2
            E("gp", lambda e, hc=hc, s1=s1: e.dma_start(out=WB[s1][:], in_=w1[hc]),
              writes=[wb_names[s1]], dsem=f"WB{s1}")
            E("gp", lambda e, hc=hc, s3=s3: e.dma_start(out=WB[s3][:], in_=w3[hc]),
              writes=[wb_names[s3]], dsem=f"WB{s3}")
            gb = 2 * (hc % 2)
            ub = gb + 1

            def gate_fn(e, hc=hc, s1=s1, gb=gb, xa=cur["XA"]):
                for jc in range(8):
                    e.matmul(G(gb), WB[s1][:, 128 * jc:128 * (jc + 1)],
                             xa[:, TOK * jc:TOK * (jc + 1)],
                             start=(jc == 0), stop=False)
                return e.matmul(G(gb), WB[s1][0:2, 1024:1152], T1[:],
                                start=False, stop=True)
            E("pe", gate_fn,
              reads=[wb_names[s1], "T1"] + [xan(j) for j in range(8)],
              writes=[f"G{gb}"])

            def up_fn(e, hc=hc, s3=s3, ub=ub, xa=cur["XA"]):
                for jc in range(8):
                    e.matmul(G(ub), WB[s3][:, 128 * jc:128 * (jc + 1)],
                             xa[:, TOK * jc:TOK * (jc + 1)],
                             start=(jc == 0), stop=False)
                return e.matmul(G(ub), WB[s3][0:2, 1024:1152], T3[:],
                                start=False, stop=True)
            E("pe", up_fn,
              reads=[wb_names[s3], "T3"] + [xan(j) for j in range(8)],
              writes=[f"G{ub}"])
            if t2q:
                emit_t2(*t2q.pop())
            g1, g1n = G1S[hc % 4]
            E("act", lambda e, gb=gb, g1=g1: e.activation(
                g1, G(gb), AF.Silu, bias=0.0, scale=1.0),
              reads=[f"G{gb}"], writes=[g1n])
            gname = xan(8 + cc)
            if m == 0:
                E("dve", lambda e, ub=ub, cc=cc, g1=g1, xa=cur["XA"]: e.tensor_tensor(
                    xa[:, TOK * (8 + cc):TOK * (9 + cc)], g1, G(ub), op=OP.mult),
                  reads=[g1n, f"G{ub}"], writes=[gname])
                t2q.append((hc, gname, XAr(8 + cc)))
            else:
                gs, gsn = GSCS[hc % 3]
                E("dve", lambda e, ub=ub, g1=g1, gs=gs: e.tensor_tensor(
                    gs, g1, G(ub), op=OP.mult),
                  reads=[g1n, f"G{ub}"], writes=[gsn])
                E("dve", lambda e, cc=cc, gs=gs, xa=cur["XA"]: e.tensor_tensor(
                    xa[:, TOK * (8 + cc):TOK * (9 + cc)],
                    xa[:, TOK * (8 + cc):TOK * (9 + cc)], gs, op=OP.add),
                  reads=[gname, gsn], writes=[gname])
                t2q.append((hc, gsn, gs))
        while t2q:
            emit_t2(*t2q.pop())
        E("act", lambda e: e.activation(T2[:], P4[0:2, :], AF.Copy, bias=0.0, scale=0.5),
          reads=["P4", "P4k"], writes=["T2"])

        for do in range(DC):
            slots = [(2 * wa_ctr) % 4, (2 * wa_ctr + 1) % 4]
            wa_ctr += 1
            for j, sl in enumerate(slots):
                E("gp", lambda e, do=do, j=j, sl=sl: e.dma_start(
                    out=WA[sl][:, 0:1408], in_=w2[do][:, 1408 * j:1408 * (j + 1)]),
                  writes=[wa_names[sl]], dsem=f"WA{sl}")
            E("gp", lambda e, do=do, sl=slots[0]: e.dma_start(
                out=WA[sl][0:2, 1408:1536], in_=b2d[:, 128 * do:128 * (do + 1)]),
              writes=[wa_names[slots[0]]], dsem=f"WA{slots[0]}")
            pso = [ps_big[:, 1024:1536], ps_big[:, 1536:2048], P4[:], P5[:]][do % 4]
            pn = ["G2", "G3", "P4", "P5"][do % 4]

            def down_fn(e, slots=slots, pso=pso, do=do, xa=cur["XA"]):
                for cc in range(CC):
                    sl = slots[cc // 11]
                    e.matmul(pso, WA[sl][:, 128 * (cc % 11):128 * (cc % 11) + 128],
                             xa[:, TOK * (8 + cc):TOK * (9 + cc)], start=(cc == 0), stop=False)
                return e.matmul(pso, WA[slots[0]][0:2, 1408:1536], T2[:],
                                start=False, stop=True)
            E("pe", down_fn,
              reads=[wa_names[s] for s in set(slots)] + ["T2"] +
                    [xan(8 + cc) for cc in range(CC)],
              writes=[pn, "P4k" if do % 2 == 0 else "P5k"])
            E("dve", lambda e, do=do, pso=pso: e.tensor_tensor(
                xprime[:, TOK * do:TOK * (do + 1)], xprime[:, TOK * do:TOK * (do + 1)],
                pso, op=OP.add),
              reads=[pn, f"X{do}"], writes=[f"X{do}"])
            if len(stq) >= 3:
                stats_b(stq.pop(0))
            stats_a(do)
            stq.append(do)

        while stq:
            stats_b(stq.pop(0))
        ln_rows_and_apply(out_to_ob=True)
        em.merge("PTS", ["PTc", "PTd", "PTe", "PTf"])
        em.merge("P0", ["PSa", "PSb"])
        em.merge("P1", ["GSa", "GSb", "XR2"])

    em.finalize()
    nc.finalize()
    return nc



def _relayout_blocks(wT, n_out_chunks, K):
    blocks = wT.reshape(K // 128, 128, n_out_chunks, 128)
    blocks = blocks.transpose(2, 1, 0, 3).reshape(n_out_chunks, 128, K)
    return np.ascontiguousarray(blocks)


def _bf(x):
    return np.asarray(x, dtype=ml_dtypes.bfloat16)


def _prep(inputs):
    f32 = np.float32
    X = np.asarray(inputs["L0_attn_hidden_states"], f32)
    Wq = np.asarray(inputs["L0_attn_Wq"], f32)
    Wkv = np.asarray(inputs["L0_attn_Wkv"], f32)
    Wo = np.asarray(inputs["L0_attn_Wo"], f32)
    w1q = np.asarray(inputs["L0_ffn_w1q"], f32)
    a1 = np.asarray(inputs["L0_ffn_a1"], f32)
    b1 = np.asarray(inputs["L0_ffn_b1"], f32)
    w3q = np.asarray(inputs["L0_ffn_w3q"], f32)
    a3 = np.asarray(inputs["L0_ffn_a3"], f32)
    b3 = np.asarray(inputs["L0_ffn_b3"], f32)
    w2q = np.asarray(inputs["L0_ffn_w2q"], f32)
    a2 = np.asarray(inputs["L0_ffn_a2"], f32)
    b2 = np.asarray(inputs["L0_ffn_b2"], f32)

    x_fm = np.ascontiguousarray(X.transpose(0, 2, 1))

    wq_dev = _relayout_blocks(Wq, NH, DM)
    wkv_dev = _relayout_blocks(Wkv, 2, DM)
    wo_dev = _relayout_blocks(Wo, DC, DM)

    w1p = np.zeros((4 * H2, 1024), f32)
    w3p = np.zeros((4 * H2, 1024), f32)
    b1p = np.zeros((4 * H2, 2), f32)
    b3p = np.zeros((4 * H2, 2), f32)
    a2p = np.zeros((2, 4 * H2), f32)
    for m in range(4):
        w1p[m * H2:m * H2 + 2752] = w1q[m * 2752:(m + 1) * 2752]
        w3p[m * H2:m * H2 + 2752] = w3q[m * 2752:(m + 1) * 2752]
        b1p[m * H2:m * H2 + 2752] = b1[m * 2752:(m + 1) * 2752]
        b3p[m * H2:m * H2 + 2752] = b3[m * 2752:(m + 1) * 2752]
        a2p[:, m * H2:m * H2 + 2752] = a2[:, m * 2752:(m + 1) * 2752]
    w1_dev = _relayout_blocks(np.ascontiguousarray(w1p.T), HC, 1024)
    w3_dev = _relayout_blocks(np.ascontiguousarray(w3p.T), HC, 1024)
    w1a = np.zeros((HC, 128, 1152), np.float32)
    w3a = np.zeros((HC, 128, 1152), np.float32)
    w1a[:, :, 0:1024] = w1_dev
    w3a[:, :, 0:1024] = w3_dev
    for hc in range(HC):
        w1a[hc, 0:2, 1024:1152] = b1p.T[:, 128 * hc:128 * (hc + 1)]
        w3a[hc, 0:2, 1024:1152] = b3p.T[:, 128 * hc:128 * (hc + 1)]
    w1_dev, w3_dev = w1a, w3a
    w2p = np.zeros((H2, DM), f32)
    w2p[:2752] = w2q.T
    w2_dev = _relayout_blocks(w2p, DC, H2)

    a1_dev = np.ascontiguousarray(a1.T.reshape(DC, 128, 2).transpose(1, 0, 2).reshape(128, 64))
    a3_dev = np.ascontiguousarray(a3.T.reshape(DC, 128, 2).transpose(1, 0, 2).reshape(128, 64))
    a2_dev = np.ascontiguousarray(a2p.T.reshape(HC, 128, 2).transpose(1, 0, 2).reshape(128, 2 * HC))
    b1_dev = np.ascontiguousarray(b1p.T)
    b3_dev = np.ascontiguousarray(b3p.T)
    b2_dev = np.ascontiguousarray(b2.T)

    slopes32 = np.array(SLOPES_F32, np.float64)
    sa = np.zeros((4, NH), np.float64)
    rem = slopes32.copy()
    for a in range(4):
        part = _bf(rem).astype(np.float64)
        sa[a] = part
        rem = rem - part
    k = np.arange(S, dtype=np.float64)
    k_hi, k_lo = np.floor_divide(k, 8), np.mod(k, 8)
    krows_dev = _bf(np.stack([k_hi] * 4 + [k_lo] * 4))
    bl2_dev = np.zeros((NH, 8, 128), np.float64)
    for h in range(NH):
        for a in range(4):
            bl2_dev[h, a, :] = 8.0 * sa[a, h]
            bl2_dev[h, 4 + a, :] = sa[a, h]
    bl2_dev = _bf(bl2_dev)

    qmask_cores = []
    qpos_cores = []
    xown_cores = []
    for core in range(8):
        half = core % 2
        qts = QT_HALF[half]
        qp = np.zeros((128, 8), np.float32)
        for j, t in enumerate(qts):
            qp[:, j] = 128.0 * t + np.arange(128)
        qpos_cores.append(qp)
        qm = np.zeros((128, 512), np.float32)
        for pp in range(2):
            t = qts[4 * pp]
            T = prog_T(pp, 0)
            off = t - (T - 2)
            for j, tt in enumerate(qts[4 * pp:4 * pp + 4]):
                assert tt - (prog_T(pp, j) - 2) == off
            kk = np.arange(256)
            qq = 128 * off + np.arange(128)
            tile = np.where(kk[None, :] <= qq[:, None], 0.0, NEGBIG).astype(np.float32)
            qm[:, 256 * pp:256 * (pp + 1)] = tile
        qmask_cores.append(_bf(qm))
        b = core // 2
        tok_idx = np.concatenate([np.arange(128 * t, 128 * (t + 1)) for t in qts])
        xown_cores.append(np.ascontiguousarray(x_fm[b][:, tok_idx]))

    ident = _bf(np.eye(128, dtype=np.float32))
    onesb = _bf(np.ones((128, 1), np.float32))
    ones1 = np.ones((1, 128), np.float32)

    shared = {
        "wq": wq_dev, "wkv": wkv_dev, "wo": wo_dev,
        "w1": w1_dev, "w3": w3_dev, "w2": w2_dev,
        "a1d": a1_dev, "a3d": a3_dev, "a2d": a2_dev,
        "b1d": b1_dev, "b3d": b3_dev, "b2d": b2_dev,
        "krows": krows_dev, "bl2": bl2_dev,
        "identd": ident, "onesd": onesb, "ones1d": ones1,
    }
    in_maps = []
    for core in range(8):
        m = dict(shared)
        m["x_fm"] = x_fm[core // 2]
        m["x_own"] = xown_cores[core]
        m["qpos"] = qpos_cores[core]
        m["qmask"] = qmask_cores[core]
        in_maps.append(m)
    return in_maps


def _assemble(results):
    out = np.empty((4, S, DM), np.float32)
    for core in range(8):
        half = core % 2
        b = core // 2
        qts = QT_HALF[half]
        tok_idx = np.concatenate([np.arange(128 * t, 128 * (t + 1)) for t in qts])
        yc = np.asarray(results[core]["y"], np.float32).reshape(DM, 1024)
        out[b, tok_idx, :] = yc.T
    return out


_cached = {}


def kernel(**inputs):
    import os
    debug = bool(os.environ.get("KDBG"))
    key = ("nc", debug)
    if key not in _cached:
        _cached[key] = build_program(debug=debug)
    nc = _cached[key]
    in_maps = _prep(inputs)
    res = run_bass_kernel_spmd(nc, in_maps, core_ids=list(range(8)))
    if debug:
        kernel.last_results = res.results
    return _assemble(res.results)


# version 08
# speedup vs baseline: 1.7658x; 1.0110x over previous
import math
import numpy as np
import ml_dtypes

import concourse.bass as bass
import concourse.mybir as mybir
from concourse.bass_utils import run_bass_kernel_spmd

F32 = mybir.dt.float32
BF16 = mybir.dt.bfloat16
AF = mybir.ActivationFunctionType
AX = mybir.AxisListType
OP = mybir.AluOpType

NH = 32
DC = 32
DM = 4096
S = 2048
TOK = 512
NT = 4
NPASS = 2
H2 = 2816
HC = 88
CC = 22
HEAD_SCALE = 1.0 / math.sqrt(128.0)
EPS = 1e-5
NEGBIG = -1.0e9

QT_HALF = [[0, 2, 4, 6, 9, 11, 13, 15], [1, 3, 5, 7, 8, 10, 12, 14]]

_ALIBI_START = 2.0 ** (-(2.0 ** -(math.log2(NH) - 3)))
SLOPES_F32 = [float(np.float32(_ALIBI_START ** (i + 1))) for i in range(NH)]


def prog_T(p, l):
    return (2 * l + 2) if p == 0 else (10 + 2 * l)



class Em:
    COMPUTE = ("pe", "act", "dve")
    QUEUES = ("sy", "gp")

    def __init__(self, nc):
        self.nc = nc
        self.sems = {e: nc.alloc_semaphore(f"sem_{e}") for e in self.COMPUTE}
        self.cnt = {e: 0 for e in self.COMPUTE}
        self.dq = {}
        self.ops = {e: [] for e in self.COMPUTE + self.QUEUES}
        self.buf = {}
        self.last_wait = {}

    def emit(self, e, fn, reads=(), writes=(), dsem=None):
        if e in self.QUEUES:
            assert dsem is not None
            if dsem not in self.sems:
                self.sems[dsem] = self.nc.alloc_semaphore(f"sem_{dsem}")
                self.cnt[dsem] = 0
                self.dq[dsem] = e
            assert self.dq[dsem] == e
            key, inc = dsem, 16
        else:
            assert dsem is None
            key, inc = e, 1

        waits = {}

        def add(tok):
            if tok is None:
                return
            tk, tv = tok
            if tk == e and e == "pe":
                return
            waits[tk] = max(waits.get(tk, 0), tv)

        for b in reads:
            st = self.buf.get(b)
            if st is not None:
                add(st["w"])
        for b in writes:
            st = self.buf.get(b)
            if st is not None:
                add(st["w"])
                for t in st["r"]:
                    add(t)
        lw = self.last_wait.setdefault(e, {})
        waits = {k: v for k, v in waits.items() if lw.get(k, -1) < v}
        for k, v in waits.items():
            lw[k] = v
        self.cnt[key] += inc
        tok = (key, self.cnt[key])
        self.ops[e].append((waits, fn, key, inc))
        for b in reads:
            self.buf.setdefault(b, {"w": None, "r": []})["r"].append(tok)
        for b in writes:
            self.buf[b] = {"w": tok, "r": []}
        return tok

    def carve(self, parent, children):
        st = self.buf.get(parent, {"w": None, "r": []})
        for c in children:
            self.buf[c] = {"w": st["w"], "r": list(st["r"])}

    def merge(self, parent, children):
        toks = []
        for c in children:
            st = self.buf.pop(c, None)
            if st:
                if st["w"]:
                    toks.append(st["w"])
                toks.extend(st["r"])
        self.buf[parent] = {"w": None, "r": toks}

    def coalesce(self, names, tok):
        for b in names:
            self.buf.setdefault(b, {"w": None, "r": []})["w"] = tok

    def finalize(self):
        nc = self.nc
        blk_cm = nc.Block()
        blk = blk_cm.__enter__()
        eng_attr = {"pe": "tensor", "act": "scalar", "dve": "vector",
                    "sy": "sync", "gp": "gpsimd"}
        for ename in ("sy", "gp", "pe", "act", "dve"):
            ops = self.ops[ename]
            fin = [(d, self.cnt[d]) for d, q in self.dq.items() if q == ename]

            def body(eng, _ops=ops, _fin=fin, _sems=self.sems):
                for waits, fn, key, inc in _ops:
                    for se, v in waits.items():
                        eng.wait_ge(_sems[se], v)
                    inst = fn(eng)
                    inst.then_inc(_sems[key], inc)
                for (d, v) in _fin:
                    eng.wait_ge(_sems[d], v)

            getattr(blk, eng_attr[ename])(body)
        blk_cm.__exit__(None, None, None)



def build_program(debug=False):
    nc = bass.Bass("TRN2", target_bir_lowering=False, debug=False)

    def din(name, shape, dt=F32):
        return nc.dram_tensor(name, list(shape), dt, kind="ExternalInput").ap()

    x_fm = din("x_fm", (DM, S))
    x_own = din("x_own", (DM, 1024))
    wq = din("wq", (NH, 128, DM))
    wkv = din("wkv", (2, 128, DM))
    wo = din("wo", (DC, 128, DM))
    w1 = din("w1", (HC, 128, 1152))
    w3 = din("w3", (HC, 128, 1152))
    w2 = din("w2", (DC, 128, H2))
    a1d = din("a1d", (128, 64))
    a3d = din("a3d", (128, 64))
    a2d = din("a2d", (128, 2 * HC))
    b1d = din("b1d", (2, 128 * HC))
    b3d = din("b3d", (2, 128 * HC))
    b2d = din("b2d", (2, DM))
    krows = din("krows", (8, S), BF16)
    bl2 = din("bl2", (NH, 8, 128), BF16)
    qpos = din("qpos", (128, 8))
    qmask = din("qmask", (128, 512), BF16)
    identd = din("identd", (128, 128), BF16)
    onesd = din("onesd", (128, 1), BF16)
    ones1d = din("ones1d", (1, 128))
    y = nc.dram_tensor("y", [DC, 128, 1024], F32, kind="ExternalOutput").ap()
    dbg = {}
    if debug:
        dbg["k"] = nc.dram_tensor("dbg_k", [128, S], F32, kind="ExternalOutput").ap()
        dbg["q"] = nc.dram_tensor("dbg_q", [128, 6144], F32, kind="ExternalOutput").ap()
        dbg["at"] = nc.dram_tensor("dbg_at", [128, 16384], F32, kind="ExternalOutput").ap()
        dbg["x1"] = nc.dram_tensor("dbg_x1", [128, 16384], F32, kind="ExternalOutput").ap()
        dbg["at2"] = nc.dram_tensor("dbg_at2", [128, 16384], F32, kind="ExternalOutput").ap()
        dbg["x12"] = nc.dram_tensor("dbg_x12", [128, 16384], F32, kind="ExternalOutput").ap()
        dbg["q2"] = nc.dram_tensor("dbg_q2", [128, 6144], F32, kind="ExternalOutput").ap()

    sb = nc.alloc_sbuf_tensor
    arena_a = sb("arena_a", [128, 32 * TOK], BF16)
    arena_b = sb("arena_b", [128, 32 * TOK], BF16)
    arena_q = sb("arena_q", [128, 12 * TOK], BF16)
    P_sb = sb("P_sb", [128, S], BF16)
    P2_sb = sb("P2_sb", [128, S], BF16)
    PTS = sb("PTS", [128, S], BF16)
    K_sb = sb("K_sb", [128, S], BF16)
    VT_sb = sb("VT_sb", [128, S], BF16)
    xprime = sb("xprime", [128, 32 * TOK], F32)
    XR = [sb(f"XR{i}", [128, TOK], F32) for i in range(2)]
    XB = [sb(f"XB{i}", [128, TOK], BF16) for i in range(2)]
    WA = [sb(f"WA{i}", [128, 2048], BF16) for i in range(4)]
    WB = [sb(f"WB{i}", [128, 1152], BF16) for i in range(4)]
    KR = sb("KR", [8, S], BF16)
    BL = [sb(f"BL{i}", [8, 128], BF16) for i in range(2)]
    QPOS = sb("QPOS", [128, 8], F32)
    QSQ = [sb(f"QSQ{i}", [128, TOK], BF16) for i in range(2)]
    QM = sb("QM", [128, 512], BF16)
    ID = sb("ID", [128, 128], BF16)
    ONEB = sb("ONEB", [128, 1], BF16)
    ONE1 = sb("ONE1", [1, 128], F32)
    A1 = sb("A1", [128, 64], BF16)
    A3 = sb("A3", [128, 64], BF16)
    A2 = sb("A2", [128, 2 * HC], BF16)
    T1 = sb("T1", [2, TOK], BF16)
    T3 = sb("T3", [2, TOK], BF16)
    T2 = sb("T2", [2, TOK], BF16)
    NB = sb("NB", [128, 4 * NH], F32)
    KN = sb("KN", [1, 1], F32)
    SR2 = sb("SR2", [1, TOK], F32)
    RSP = sb("RSP", [128, 8], F32)
    RS = sb("RS", [128, 1], F32)
    RR = sb("RR", [128, 1], F32)
    GSC = sb("GSC", [128, TOK], BF16)

    ps_big = nc.alloc_psum_tensor("ps_big", [128, 2048], F32)
    P4 = nc.alloc_psum_tensor("P4", [128, 512], F32)
    P5 = nc.alloc_psum_tensor("P5", [128, 512], F32)
    PT0 = nc.alloc_psum_tensor("PT0", [128, 1024], BF16)
    PT1 = nc.alloc_psum_tensor("PT1", [128, 1024], BF16)

    em = Em(nc)
    E = em.emit

    def G(i):
        return ps_big[:, 512 * i:512 * (i + 1)]

    cur = {}

    def XAr(i):
        return cur["XA"][:, TOK * i:TOK * (i + 1)]

    def ABr(i):
        return cur["AB"][:, TOK * i:TOK * (i + 1)]

    def xan(i):
        return f'{cur["xp"]}{i}'

    def abn(i):
        return f'{cur["bp"]}{i}'

    def Qr(i):
        return arena_q[:, TOK * i:TOK * (i + 1)]

    E("sy", lambda e: e.dma_start(out=ID[:], in_=identd), writes=["ID"], dsem="CS")
    E("sy", lambda e: e.dma_start(out=ONEB[:], in_=onesd), writes=["ONEB"], dsem="CS")
    E("sy", lambda e: e.dma_start(out=ONE1[:], in_=ones1d), writes=["ONE1"], dsem="CS")
    E("sy", lambda e: e.dma_start(out=QM[:], in_=qmask), writes=["QM"], dsem="CS")
    E("sy", lambda e: e.dma_start(out=KR[:], in_=krows), writes=["KR"], dsem="CS")
    tk = E("sy", lambda e: e.dma_start(out=QPOS[:], in_=qpos), writes=["QPOS"], dsem="CS")
    em.coalesce(["ID", "ONEB", "ONE1", "QM", "KR", "QPOS"], tk)
    E("gp", lambda e: e.dma_start(out=A1[:], in_=a1d), writes=["LA1"], dsem="LC")
    E("gp", lambda e: e.dma_start(out=A3[:], in_=a3d), writes=["LA3"], dsem="LC")
    tk = E("gp", lambda e: e.dma_start(out=A2[:], in_=a2d), writes=["LA2"], dsem="LC")
    em.coalesce(["LA1", "LA3", "LA2"], tk)

    wa_names = [f"WA{i}" for i in range(4)]
    wb_names = [f"WB{i}" for i in range(4)]

    def load_wa_block(src_ap, slots):
        ncols = src_ap.shape[1]
        half = ncols // 2
        for j, sl in enumerate(slots):
            E("gp",
              lambda e, s=sl, j=j, src=src_ap, h=half:
                  e.dma_start(out=WA[s][:, 0:h], in_=src[:, j * h:(j + 1) * h]),
              writes=[wa_names[sl]], dsem=f"WA{sl}")

    def q_proj_mms(h, p, wslots, part):
        xa = cur["XA"]

        def fn(e, h=h, p=p, ws=wslots, part=part, xa=xa):
            mm = None
            for c in range(8 * part, 8 * part + 8):
                sl = ws[c // 16]
                tile = WA[sl][:, 128 * (c % 16):128 * (c % 16) + 128]
                mm = e.matmul(P4[:], tile, xa[:, TOK * c:TOK * (c + 1)],
                              start=(c == 0), stop=(c == DC - 1))
            return mm
        reads = [wa_names[s] for s in set(wslots)] + [xan(c) for c in range(8 * part, 8 * part + 8)]
        E("pe", fn, reads=reads, writes=(["P4", "P4k"] if part == 0 else ["P4k"]))

    def q_evac(h):
        s = h % 12
        E("act",
          lambda e, s=s: e.activation(Qr(s), P4[:], AF.Copy, bias=0.0, scale=HEAD_SCALE),
          reads=["P4", "P4k"], writes=[f"Q{s}"])

    def negb_a(h):
        s = h % 12
        r = h % 2
        E("act", lambda e, s=s, r=r: e.activation(QSQ[r][:], Qr(s), AF.Square, bias=0.0, scale=1.0),
          reads=[f"Q{s}"], writes=[f"QSQ{r}"])

    def negb_b12(h):
        r = h % 2
        E("pe", lambda e, r=r: e.matmul(P5[0:1, :], ONEB[:], QSQ[r][:], start=True, stop=True),
          reads=[f"QSQ{r}", "ONEB"], writes=["P5", "P5k"])
        E("dve", lambda e: e.tensor_reduce(
            SR2[0:1, 0:4], P5[0:1, 0:512].rearrange("p (g k) -> p g k", k=128),
            axis=AX.X, op=OP.max),
          reads=["P5", "P5k"], writes=["SR2"])
        E("act", lambda e: e.activation(SR2[0:1, 0:4], SR2[0:1, 0:4], AF.Sqrt,
                                        bias=0.0, scale=1.0),
          reads=["SR2"], writes=["SR2"])
        E("dve", lambda e: e.tensor_scalar(SR2[0:1, 0:4], SR2[0:1, 0:4],
                                           KN[0:1, 0:1], -1.0, op0=OP.mult, op1=OP.mult),
          reads=["SR2", "KN"], writes=["SR2"])

    def negb_b3(h, p):
        E("pe", lambda e: e.matmul(P5[:, 0:4], ONE1[:], SR2[0:1, 0:4],
                                   start=True, stop=True),
          reads=["SR2", "ONE1"], writes=["P5", "P5k"])
        E("dve", lambda e, h=h, p=p: e.scalar_tensor_tensor(
            NB[:, 4 * h:4 * h + 4], QPOS[:, 4 * p:4 * p + 4], -SLOPES_F32[h],
            P5[:, 0:4], op0=OP.mult, op1=OP.add),
          reads=["QPOS", "P5", "P5k"], writes=[f"NB{h}"])

    wkv_slots = [0, 1, 2, 3]

    wa_ctr = 0
    for p in range(NPASS):
        if p == 0:
            cur.update(XA=arena_a, xp="A", AB=arena_b, bp="B")
        else:
            cur.update(XA=arena_b, xp="B", AB=arena_a, bp="A")
        for c in range(DC):
            tk = E("gp",
                   lambda e, c=c, p=p, xa=cur["XA"]: e.dma_start(
                       out=xa[:, TOK * c:TOK * (c + 1)],
                       in_=x_own[128 * c:128 * (c + 1), TOK * p:TOK * (p + 1)]),
                   writes=[xan(c)], dsem="XBL")
        em.coalesce([xan(c) for c in range(DC)], tk)

        if p == 0:
            load_wa_block(wkv[0], [0, 1])
            load_wa_block(wkv[1], [2, 3])
            wa_ctr = 2
            for half in range(2):
                for c in range(DC):
                    i = c % 4
                    kb = [abn(2 * i), abn(2 * i + 1)]
                    kvbuf = cur["AB"][:, 1024 * i:1024 * (i + 1)]
                    E("gp",
                      lambda e, c=c, half=half, kvbuf=kvbuf: e.dma_start(
                          out=kvbuf, in_=x_fm[128 * c:128 * (c + 1),
                                              1024 * half:1024 * (half + 1)]),
                      writes=kb, dsem=f"KV{i}")

                    def kv_fn(e, c=c, kvbuf=kvbuf):
                        mm = None
                        kt = WA[c // 16][:, 128 * (c % 16):128 * (c % 16) + 128]
                        vt = WA[2 + c // 16][:, 128 * (c % 16):128 * (c % 16) + 128]
                        for j in range(2):
                            mm = e.matmul(G(j), kt, kvbuf[:, 512 * j:512 * (j + 1)],
                                          start=(c == 0), stop=(c == DC - 1))
                            mm = e.matmul(G(2 + j), vt, kvbuf[:, 512 * j:512 * (j + 1)],
                                          start=(c == 0), stop=(c == DC - 1))
                        return mm
                    E("pe", kv_fn,
                      reads=kb + wa_names,
                      writes=(["G0", "G1", "G2", "G3", "Gk"] if c == 0 else ["Gk"]))
                E("act",
                  lambda e, half=half: e.activation(
                      K_sb[:, 1024 * half:1024 * (half + 1)], ps_big[:, 0:1024],
                      AF.Copy, bias=0.0, scale=1.0),
                  reads=["G0", "G1", "Gk"], writes=["K"])
                E("act",
                  lambda e: e.activation(P_sb[:, 0:1024], ps_big[:, 1024:2048],
                                         AF.Copy, bias=0.0, scale=1.0),
                  reads=["G2", "G3", "Gk"], writes=["P0"])

                def vtr_fn(e, half=half):
                    mm = None
                    for k in range(8):
                        mm = e.transpose(PT0[:, 128 * k:128 * (k + 1)],
                                         P_sb[:, 128 * k:128 * (k + 1)], ID[:])
                    return mm
                E("pe", vtr_fn, reads=["P0", "ID"], writes=["PT0"])
                E("dve",
                  lambda e, half=half: e.tensor_copy(
                      VT_sb[:, 1024 * half:1024 * (half + 1)], PT0[:]),
                  reads=["PT0"], writes=["VT"])

            for c in range(4):
                E("act", lambda e, c=c: e.activation(QSQ[c % 2][:], K_sb[:, 512 * c:512 * (c + 1)],
                                                     AF.Square, bias=0.0, scale=1.0),
                  reads=["K"], writes=[f"QSQ{c % 2}"])
                E("pe", lambda e, c=c: e.matmul(ps_big[0:1, 512 * c:512 * (c + 1)],
                                                ONEB[:], QSQ[c % 2][:], start=True, stop=True),
                  reads=[f"QSQ{c % 2}", "ONEB"], writes=[f"G{c}"])
            E("dve", lambda e: e.reduce_max(KN[0:1, 0:1], ps_big[0:1, :], axis=AX.X),
              reads=["G0", "G1", "G2", "G3"], writes=["KN"])
            E("act", lambda e: e.activation(KN[0:1, 0:1], KN[0:1, 0:1], AF.Sqrt,
                                            bias=0.0, scale=1.0),
              reads=["KN"], writes=["KN"])
            E("dve", lambda e: e.tensor_scalar_mul(KN[0:1, 0:1], KN[0:1, 0:1], 1.0005),
              reads=["KN"], writes=["KN"])
            if debug:
                E("gp", lambda e: e.dma_start(out=dbg["k"], in_=K_sb[:]), reads=["K"], dsem="DBGK")

        nbq = []
        for h in range(12):
            blk_src = wq[h]
            slots = [(2 * wa_ctr) % 4, (2 * wa_ctr + 1) % 4]
            wa_ctr += 1
            load_wa_block(blk_src, slots)
            for part in range(4):
                q_proj_mms(h, p, slots, part)
            if nbq:
                negb_b3(nbq.pop(0), p)
            q_evac(h)
            negb_a(h)
            negb_b12(h)
            nbq.append(h)

        nb12q = []
        for h in range(NH):
            s = h % 12
            bl = h % 2
            E("sy", lambda e, h=h, bl=bl: e.dma_start(out=BL[bl][:], in_=bl2[h]),
              writes=[f"BL{bl}"], dsem=f"BLD{bl}")

            fill = (1 <= h <= 20)
            if fill:
                fh = h + 11
                fslots = [(2 * wa_ctr) % 4, (2 * wa_ctr + 1) % 4]
                wa_ctr += 1
                load_wa_block(wq[fh], fslots)

            pt_ctr = 0

            def geom(l):
                T = prog_T(p, l)
                L = 128 * T
                nsl = (L + 511) // 512
                return T, L, nsl

            def bank(l, sl):
                return 2 * (l % 2) + sl % 2

            def pbuf(l):
                return (P_sb, "P0") if l % 2 == 0 else (P2_sb, "P1")

            def emit_S_slice(l, sl, s=s, bl=bl):
                T, L, nsl = geom(l)
                c0 = 512 * sl
                c1 = min(512 * (sl + 1), L)
                bk = bank(l, sl)

                def s_fn(e, s=s, l=l, c0=c0, c1=c1, bk=bk, bl=bl):
                    e.matmul(G(bk)[:, 0:c1 - c0],
                             Qr(s)[:, 128 * l:128 * (l + 1)],
                             K_sb[:, c0:c1], start=True, stop=False)
                    return e.matmul(G(bk)[:, 0:c1 - c0],
                                    BL[bl][:], KR[0:8, c0:c1], start=False, stop=True)
                E("pe", s_fn, reads=[f"Q{s}", "K", f"BL{bl}"], writes=[f"G{bk}"])
                if sl == nsl - 1:
                    w = c1 - c0
                    E("dve",
                      lambda e, w=w, p=p, bk=bk: e.tensor_tensor(
                          G(bk)[:, w - 256:w], G(bk)[:, w - 256:w],
                          QM[:, 256 * p:256 * (p + 1)], op=OP.add),
                      reads=["QM", f"G{bk}"], writes=[f"G{bk}"])

            def emit_exp(l, sl, h=h):
                T, L, nsl = geom(l)
                pb, pn = pbuf(l)
                c0 = 512 * sl
                c1 = min(512 * (sl + 1), L)
                bk = bank(l, sl)
                ri = 4 * (l % 2) + sl
                E("act",
                  lambda e, c0=c0, c1=c1, h=h, l=l, sl=sl, bk=bk, pb=pb, ri=ri: e.activation(
                      pb[:, c0:c1], G(bk)[:, 0:c1 - c0], AF.Exp,
                      bias=NB[:, 4 * h + l:4 * h + l + 1], scale=1.0,
                      accum_out=RSP[:, ri:ri + 1]),
                  reads=[f"G{bk}", f"NB{h}"], writes=[pn, f"RSP{ri}"])

            def emit_tile_chain(l):
                T, L, nsl = geom(l)
                pb, pn = pbuf(l)
                r0 = 4 * (l % 2)
                E("dve", lambda e, nsl=nsl, r0=r0: e.reduce_sum(RS[:], RSP[:, r0:r0 + nsl], axis=AX.X),
                  reads=[f"RSP{r0 + i}" for i in range(nsl)], writes=["RS"])
                E("dve", lambda e: e.reciprocal(RR[:], RS[:]), reads=["RS"], writes=["RR"])
                E("dve", lambda e, L=L, pb=pb: e.tensor_scalar_mul(pb[:, 0:L], pb[:, 0:L], RR[:]),
                  reads=[pn, "RR"], writes=[pn])

            def emit_tile_full(l):
                T, L, nsl = geom(l)
                for sl in range(nsl):
                    emit_S_slice(l, sl)
                    emit_exp(l, sl)
                emit_tile_chain(l)

            def emit_tile_pair(la, lb):
                Ta, La, nsla = geom(la)
                Tb, Lb, nslb = geom(lb)
                for sl in range(max(nsla, nslb)):
                    if sl < nsla:
                        emit_S_slice(la, sl)
                    if sl < nslb:
                        emit_S_slice(lb, sl)
                    if sl < nsla:
                        emit_exp(la, sl)
                    if sl < nslb:
                        emit_exp(lb, sl)
                emit_tile_chain(la)
                emit_tile_chain(lb)

            def emit_tile_start(l):
                T, L, nsl = geom(l)
                for sl in range(min(2, nsl)):
                    emit_S_slice(l, sl)

            def emit_tile_rest(l):
                T, L, nsl = geom(l)
                for sl in range(min(2, nsl)):
                    emit_exp(l, sl)
                for sl in range(2, nsl):
                    emit_S_slice(l, sl)
                    emit_exp(l, sl)
                emit_tile_chain(l)

            def emit_T(l):
                nonlocal pt_ctr
                T, L, nsl = geom(l)
                pb, pn = pbuf(l)
                for g in range((T + 7) // 8):
                    k0 = 8 * g
                    k1 = min(8 * g + 8, T)
                    pt = pt_ctr % 2
                    pt_ctr += 1

                    def tr_fn(e, k0=k0, k1=k1, pt=pt, pb=pb):
                        mm = None
                        for k in range(k0, k1):
                            mm = e.transpose((PT0 if pt == 0 else PT1)[:, 128 * (k - k0):128 * (k - k0 + 1)],
                                             pb[:, 128 * k:128 * (k + 1)], ID[:])
                        return mm
                    E("pe", tr_fn, reads=[pn, "ID"], writes=[f"PT{pt}"])
                    E("dve",
                      lambda e, k0=k0, k1=k1, pt=pt: e.tensor_copy(
                          PTS[:, 128 * k0:128 * k1],
                          (PT0 if pt == 0 else PT1)[:, 0:128 * (k1 - k0)]),
                      reads=[f"PT{pt}"], writes=["PTS"])

            def emit_AV(l):
                T, L, nsl = geom(l)

                def av_fn(e, T=T, l=l):
                    mm = None
                    for kt in range(T):
                        mm = e.matmul(P5[:, 128 * l:128 * (l + 1)],
                                      VT_sb[:, 128 * kt:128 * (kt + 1)],
                                      PTS[:, 128 * kt:128 * (kt + 1)],
                                      start=(kt == 0), stop=(kt == T - 1))
                    return mm
                E("pe", av_fn, reads=["PTS", "VT"],
                  writes=(["P5", "P5k"] if l == 0 else ["P5k"]))

            emit_tile_pair(0, 1)
            if fill:
                q_proj_mms(fh, p, fslots, 0)
            if nb12q:
                bh = nb12q.pop(0)
                negb_b12(bh)
                nbq.append(bh)
            emit_T(0)
            if nbq:
                negb_b3(nbq.pop(0), p)
            emit_tile_start(2)
            emit_tile_rest(2)
            emit_AV(0)
            emit_T(1)
            emit_tile_start(3)
            emit_tile_rest(3)
            emit_AV(1)
            if fill:
                q_proj_mms(fh, p, fslots, 1)
            emit_T(2)
            if fill:
                q_proj_mms(fh, p, fslots, 2)
            emit_AV(2)
            emit_T(3)
            if fill:
                q_proj_mms(fh, p, fslots, 3)
                q_evac(fh)
            emit_AV(3)
            E("act", lambda e, h=h, ab=cur["AB"]: e.activation(
                ab[:, TOK * h:TOK * (h + 1)], P5[:], AF.Copy, bias=0.0, scale=1.0),
              reads=["P5", "P5k"], writes=[abn(h)])
            if fill:
                negb_a(fh)
                nb12q.append(fh)

        if debug:
            E("gp", lambda e, p=p: e.dma_start(out=dbg["q" if p == 0 else "q2"], in_=arena_q[:]),
              reads=[f"Q{i}" for i in range(12)], dsem="DBGQ")
            E("gp", lambda e, p=p, ab=cur["AB"]: e.dma_start(
                out=dbg["at" if p == 0 else "at2"], in_=ab[:]),
              reads=[abn(i) for i in range(NH)], dsem="DBGA")

        stq = []
        em.carve("PTS", ["PTc", "PTd", "PTe", "PTf"])
        em.carve("P0", ["PSa", "PSb"])
        em.carve("P1", ["GSa", "GSb", "XR2"])
        XBS = [(XB[0][:], "XB0"), (XB[1][:], "XB1"),
               (PTS[:, 0:512], "PTc"), (PTS[:, 512:1024], "PTd")]
        SQS = [(QSQ[0][:], "QSQ0"), (QSQ[1][:], "QSQ1"),
               (PTS[:, 1024:1536], "PTe"), (PTS[:, 1536:2048], "PTf")]
        G1S = [(XAr(30), xan(30)), (XAr(31), xan(31)),
               (P_sb[:, 0:512], "PSa"), (P_sb[:, 512:1024], "PSb")]
        GSCS = [(GSC[:], "GSC"), (P2_sb[:, 0:512], "GSa"), (P2_sb[:, 512:1024], "GSb")]
        XRS = [(XR[0][:], "XR0"), (XR[1][:], "XR1"),
               (P2_sb[:, 1024:2048].bitcast(F32), "XR2")]

        def stats_a(c):
            xb, xbn = XBS[c % 4]
            sq, sqn = SQS[c % 4]
            E("act", lambda e, c=c, xb=xb: e.activation(xb, xprime[:, TOK * c:TOK * (c + 1)],
                                                        AF.Copy, bias=0.0, scale=1.0),
              reads=[f"X{c}"], writes=[xbn])
            E("act", lambda e, c=c, sq=sq: e.activation(sq, xprime[:, TOK * c:TOK * (c + 1)],
                                                        AF.Square, bias=0.0, scale=1.0),
              reads=[f"X{c}"], writes=[sqn])

        def stats_b(c):
            xb, xbn = XBS[c % 4]
            sq, sqn = SQS[c % 4]
            E("pe", lambda e, c=c, xb=xb: e.matmul(ps_big[0:1, 0:512], ONEB[:], xb,
                                                   start=(c == 0), stop=(c == DC - 1)),
              reads=[xbn, "ONEB"], writes=(["G0", "Gk"] if c == 0 else ["Gk"]))
            E("pe", lambda e, c=c, sq=sq: e.matmul(ps_big[0:1, 512:1024], ONEB[:], sq,
                                                   start=(c == 0), stop=(c == DC - 1)),
              reads=[sqn, "ONEB"], writes=(["G1"] if c == 0 else ["Gk"]))

        def ln_rows_and_apply(out_to_ob):
            r0 = XR[0][0:1, :]
            r1 = XR[1][0:1, :]
            E("act", lambda e, r0=r0: e.activation(r0, ps_big[0:1, 0:512], AF.Copy,
                                                   bias=0.0, scale=1.0 / DM),
              reads=["G0", "Gk"], writes=["XR0"])
            E("act", lambda e, r1=r1: e.activation(r1, ps_big[0:1, 512:1024], AF.Copy,
                                                   bias=0.0, scale=1.0 / DM),
              reads=["G1", "Gk"], writes=["XR1"])
            E("act", lambda e, r0=r0: e.activation(SR2[:], r0, AF.Square, bias=0.0, scale=1.0),
              reads=["XR0"], writes=["SR2"])
            E("dve", lambda e, r1=r1: e.tensor_tensor(r1, r1, SR2[:], op=OP.subtract),
              reads=["XR1", "SR2"], writes=["XR1"])
            E("dve", lambda e, r1=r1: e.tensor_scalar_add(r1, r1, EPS),
              reads=["XR1"], writes=["XR1"])
            E("act", lambda e, r1=r1: e.activation(r1, r1, AF.Sqrt, bias=0.0, scale=1.0),
              reads=["XR1"], writes=["XR1"])
            E("dve", lambda e, r1=r1: e.reciprocal(r1, r1), reads=["XR1"], writes=["XR1"])
            E("dve", lambda e, r0=r0, r1=r1: e.tensor_tensor(r0, r0, r1, op=OP.mult),
              reads=["XR0", "XR1"], writes=["XR0"])
            E("pe", lambda e, r1=r1: e.matmul(ps_big[:, 1024:1536], ONE1[:], r1,
                                              start=True, stop=True),
              reads=["XR1", "ONE1"], writes=["G2"])
            E("pe", lambda e, r0=r0: e.matmul(ps_big[:, 1536:2048], ONE1[:], r0,
                                              start=True, stop=True),
              reads=["XR0", "ONE1"], writes=["G3"])
            for c in range(DC):
                if not out_to_ob:
                    E("dve", lambda e, c=c: e.tensor_tensor(
                        xprime[:, TOK * c:TOK * (c + 1)], xprime[:, TOK * c:TOK * (c + 1)],
                        ps_big[:, 1024:1536], op=OP.mult),
                      reads=[f"X{c}", "G2"], writes=[f"X{c}"])
                    E("dve", lambda e, c=c: e.tensor_tensor(
                        xprime[:, TOK * c:TOK * (c + 1)], xprime[:, TOK * c:TOK * (c + 1)],
                        ps_big[:, 1536:2048], op=OP.subtract),
                      reads=[f"X{c}", "G3"], writes=[f"X{c}"])
                else:
                    xr, xrn = XRS[c % 3]
                    E("dve", lambda e, c=c, xr=xr: e.tensor_tensor(
                        xr, xprime[:, TOK * c:TOK * (c + 1)],
                        ps_big[:, 1024:1536], op=OP.mult),
                      reads=[f"X{c}", "G2"], writes=[xrn])
                    E("dve", lambda e, c=c, xr=xr: e.tensor_tensor(
                        xr, xr, ps_big[:, 1536:2048], op=OP.subtract),
                      reads=[xrn, "G3"], writes=[xrn])
                    E("sy", lambda e, c=c, xr=xr, p=p: e.dma_start(
                        out=y[c][:, TOK * p:TOK * (p + 1)], in_=xr),
                      reads=[xrn], dsem=f"O{xrn}")

        for do in range(DC):
            slots = [(2 * wa_ctr) % 4, (2 * wa_ctr + 1) % 4]
            wa_ctr += 1
            load_wa_block(wo[do], slots)
            pso = [ps_big[:, 1024:1536], ps_big[:, 1536:2048], P4[:], P5[:]][do % 4]
            pn = ["G2", "G3", "P4", "P5"][do % 4]

            def wo_fn(e, slots=slots, pso=pso, ab=cur["AB"]):
                mm = None
                for c in range(DC):
                    sl = slots[c // 16]
                    mm = e.matmul(pso, WA[sl][:, 128 * (c % 16):128 * (c % 16) + 128],
                                  ab[:, TOK * c:TOK * (c + 1)], start=(c == 0), stop=(c == DC - 1))
                return mm
            E("pe", wo_fn,
              reads=[wa_names[s] for s in set(slots)] + [abn(c) for c in range(DC)],
              writes=[pn])
            xr, xrn = XRS[do % 3]
            E("sy", lambda e, do=do, p=p, xr=xr: e.dma_start(
                out=xr, in_=x_own[128 * do:128 * (do + 1), TOK * p:TOK * (p + 1)]),
              writes=[xrn], dsem=f"R{xrn}")
            E("dve", lambda e, do=do, pso=pso, xr=xr: e.tensor_tensor(
                xprime[:, TOK * do:TOK * (do + 1)], xr, pso, op=OP.add),
              reads=[pn, xrn], writes=[f"X{do}"])
            if len(stq) >= 3:
                stats_b(stq.pop(0))
            stats_a(do)
            stq.append(do)

        while stq:
            stats_b(stq.pop(0))
        ln_rows_and_apply(out_to_ob=False)
        if debug:
            E("gp", lambda e, p=p: e.dma_start(out=dbg["x1" if p == 0 else "x12"], in_=xprime[:]),
              reads=[f"X{c}" for c in range(DC)], dsem="DBGX")

        for jc in range(8):
            E("dve", lambda e, jc=jc: e.tensor_tensor(
                XR[0][:], xprime[:, TOK * jc:TOK * (jc + 1)],
                xprime[:, TOK * (8 + jc):TOK * (9 + jc)], op=OP.add),
              reads=[f"X{jc}", f"X{8 + jc}"], writes=["XR0"])
            E("dve", lambda e, jc=jc: e.tensor_tensor(
                XR[0][:], XR[0][:], xprime[:, TOK * (16 + jc):TOK * (17 + jc)], op=OP.add),
              reads=["XR0", f"X{16 + jc}"], writes=["XR0"])
            E("dve", lambda e, jc=jc, xa=cur["XA"]: e.tensor_tensor(
                xa[:, TOK * jc:TOK * (jc + 1)], XR[0][:],
                xprime[:, TOK * (24 + jc):TOK * (25 + jc)], op=OP.add),
              reads=["XR0", f"X{24 + jc}"], writes=[xan(jc)])
        for c in range(DC):
            xb, xbn = XBS[c % 4]
            E("act", lambda e, c=c, xb=xb: e.activation(xb, xprime[:, TOK * c:TOK * (c + 1)],
                                                        AF.Copy, bias=0.0, scale=1.0),
              reads=[f"X{c}"], writes=[xbn])
            E("pe", lambda e, c=c, xb=xb: e.matmul(P4[0:2, :], A1[:, 2 * c:2 * c + 2], xb,
                                            start=(c == 0), stop=(c == DC - 1)),
              reads=[xbn, "LA1"], writes=(["P4", "P4k"] if c == 0 else ["P4k"]))
            E("pe", lambda e, c=c, xb=xb: e.matmul(P5[0:2, :], A3[:, 2 * c:2 * c + 2], xb,
                                            start=(c == 0), stop=(c == DC - 1)),
              reads=[xbn, "LA3"], writes=(["P5", "P5k"] if c == 0 else ["P5k"]))
        E("act", lambda e: e.activation(T1[:], P4[0:2, :], AF.Copy, bias=0.0, scale=0.5),
          reads=["P4", "P4k"], writes=["T1"])
        E("act", lambda e: e.activation(T3[:], P5[0:2, :], AF.Copy, bias=0.0, scale=0.5),
          reads=["P5", "P5k"], writes=["T3"])

        wb_ctr = 0
        t2q = []

        def emit_t2(hc, grd, gap):
            E("pe", lambda e, hc=hc, gap=gap: e.matmul(
                P4[0:2, :], A2[:, 2 * hc:2 * hc + 2], gap,
                start=(hc == 0), stop=(hc == HC - 1)),
              reads=[grd, "LA2"], writes=(["P4", "P4k"] if hc == 0 else ["P4k"]))

        for hc in range(HC):
            m, cc = hc // CC, hc % CC
            s1 = wb_ctr % 4
            s3 = (wb_ctr + 1) % 4
            wb_ctr += 2
            E("gp", lambda e, hc=hc, s1=s1: e.dma_start(out=WB[s1][:], in_=w1[hc]),
              writes=[wb_names[s1]], dsem=f"WB{s1}")
            E("gp", lambda e, hc=hc, s3=s3: e.dma_start(out=WB[s3][:], in_=w3[hc]),
              writes=[wb_names[s3]], dsem=f"WB{s3}")
            gb = 2 * (hc % 2)
            ub = gb + 1

            def gate_fn(e, hc=hc, s1=s1, gb=gb, xa=cur["XA"]):
                for jc in range(8):
                    e.matmul(G(gb), WB[s1][:, 128 * jc:128 * (jc + 1)],
                             xa[:, TOK * jc:TOK * (jc + 1)],
                             start=(jc == 0), stop=False)
                return e.matmul(G(gb), WB[s1][0:2, 1024:1152], T1[:],
                                start=False, stop=True)
            E("pe", gate_fn,
              reads=[wb_names[s1], "T1"] + [xan(j) for j in range(8)],
              writes=[f"G{gb}"])

            def up_fn(e, hc=hc, s3=s3, ub=ub, xa=cur["XA"]):
                for jc in range(8):
                    e.matmul(G(ub), WB[s3][:, 128 * jc:128 * (jc + 1)],
                             xa[:, TOK * jc:TOK * (jc + 1)],
                             start=(jc == 0), stop=False)
                return e.matmul(G(ub), WB[s3][0:2, 1024:1152], T3[:],
                                start=False, stop=True)
            E("pe", up_fn,
              reads=[wb_names[s3], "T3"] + [xan(j) for j in range(8)],
              writes=[f"G{ub}"])
            if t2q:
                emit_t2(*t2q.pop())
            g1, g1n = G1S[hc % 4]
            E("act", lambda e, gb=gb, g1=g1: e.activation(
                g1, G(gb), AF.Silu, bias=0.0, scale=1.0),
              reads=[f"G{gb}"], writes=[g1n])
            gname = xan(8 + cc)
            if m == 0:
                E("dve", lambda e, ub=ub, cc=cc, g1=g1, xa=cur["XA"]: e.tensor_tensor(
                    xa[:, TOK * (8 + cc):TOK * (9 + cc)], g1, G(ub), op=OP.mult),
                  reads=[g1n, f"G{ub}"], writes=[gname])
                t2q.append((hc, gname, XAr(8 + cc)))
            else:
                gs, gsn = GSCS[hc % 3]
                E("dve", lambda e, ub=ub, g1=g1, gs=gs: e.tensor_tensor(
                    gs, g1, G(ub), op=OP.mult),
                  reads=[g1n, f"G{ub}"], writes=[gsn])
                E("dve", lambda e, cc=cc, gs=gs, xa=cur["XA"]: e.tensor_tensor(
                    xa[:, TOK * (8 + cc):TOK * (9 + cc)],
                    xa[:, TOK * (8 + cc):TOK * (9 + cc)], gs, op=OP.add),
                  reads=[gname, gsn], writes=[gname])
                t2q.append((hc, gsn, gs))
        while t2q:
            emit_t2(*t2q.pop())
        E("act", lambda e: e.activation(T2[:], P4[0:2, :], AF.Copy, bias=0.0, scale=0.5),
          reads=["P4", "P4k"], writes=["T2"])

        for do in range(DC):
            slots = [(2 * wa_ctr) % 4, (2 * wa_ctr + 1) % 4]
            wa_ctr += 1
            for j, sl in enumerate(slots):
                E("gp", lambda e, do=do, j=j, sl=sl: e.dma_start(
                    out=WA[sl][:, 0:1408], in_=w2[do][:, 1408 * j:1408 * (j + 1)]),
                  writes=[wa_names[sl]], dsem=f"WA{sl}")
            E("gp", lambda e, do=do, sl=slots[0]: e.dma_start(
                out=WA[sl][0:2, 1408:1536], in_=b2d[:, 128 * do:128 * (do + 1)]),
              writes=[wa_names[slots[0]]], dsem=f"WA{slots[0]}")
            pso = [ps_big[:, 1024:1536], ps_big[:, 1536:2048], P4[:], P5[:]][do % 4]
            pn = ["G2", "G3", "P4", "P5"][do % 4]

            def down_fn(e, slots=slots, pso=pso, do=do, xa=cur["XA"]):
                for cc in range(CC):
                    sl = slots[cc // 11]
                    e.matmul(pso, WA[sl][:, 128 * (cc % 11):128 * (cc % 11) + 128],
                             xa[:, TOK * (8 + cc):TOK * (9 + cc)], start=(cc == 0), stop=False)
                return e.matmul(pso, WA[slots[0]][0:2, 1408:1536], T2[:],
                                start=False, stop=True)
            E("pe", down_fn,
              reads=[wa_names[s] for s in set(slots)] + ["T2"] +
                    [xan(8 + cc) for cc in range(CC)],
              writes=[pn, "P4k" if do % 2 == 0 else "P5k"])
            E("dve", lambda e, do=do, pso=pso: e.tensor_tensor(
                xprime[:, TOK * do:TOK * (do + 1)], xprime[:, TOK * do:TOK * (do + 1)],
                pso, op=OP.add),
              reads=[pn, f"X{do}"], writes=[f"X{do}"])
            if len(stq) >= 3:
                stats_b(stq.pop(0))
            stats_a(do)
            stq.append(do)

        while stq:
            stats_b(stq.pop(0))
        ln_rows_and_apply(out_to_ob=True)
        em.merge("PTS", ["PTc", "PTd", "PTe", "PTf"])
        em.merge("P0", ["PSa", "PSb"])
        em.merge("P1", ["GSa", "GSb", "XR2"])

    em.finalize()
    nc.finalize()
    return nc



def _relayout_blocks(wT, n_out_chunks, K):
    blocks = wT.reshape(K // 128, 128, n_out_chunks, 128)
    blocks = blocks.transpose(2, 1, 0, 3).reshape(n_out_chunks, 128, K)
    return np.ascontiguousarray(blocks)


def _bf(x):
    return np.asarray(x, dtype=ml_dtypes.bfloat16)


def _prep(inputs):
    f32 = np.float32
    X = np.asarray(inputs["L0_attn_hidden_states"], f32)
    Wq = np.asarray(inputs["L0_attn_Wq"], f32)
    Wkv = np.asarray(inputs["L0_attn_Wkv"], f32)
    Wo = np.asarray(inputs["L0_attn_Wo"], f32)
    w1q = np.asarray(inputs["L0_ffn_w1q"], f32)
    a1 = np.asarray(inputs["L0_ffn_a1"], f32)
    b1 = np.asarray(inputs["L0_ffn_b1"], f32)
    w3q = np.asarray(inputs["L0_ffn_w3q"], f32)
    a3 = np.asarray(inputs["L0_ffn_a3"], f32)
    b3 = np.asarray(inputs["L0_ffn_b3"], f32)
    w2q = np.asarray(inputs["L0_ffn_w2q"], f32)
    a2 = np.asarray(inputs["L0_ffn_a2"], f32)
    b2 = np.asarray(inputs["L0_ffn_b2"], f32)

    x_fm = np.ascontiguousarray(X.transpose(0, 2, 1))

    wq_dev = _relayout_blocks(Wq, NH, DM)
    wkv_dev = _relayout_blocks(Wkv, 2, DM)
    wo_dev = _relayout_blocks(Wo, DC, DM)

    w1p = np.zeros((4 * H2, 1024), f32)
    w3p = np.zeros((4 * H2, 1024), f32)
    b1p = np.zeros((4 * H2, 2), f32)
    b3p = np.zeros((4 * H2, 2), f32)
    a2p = np.zeros((2, 4 * H2), f32)
    for m in range(4):
        w1p[m * H2:m * H2 + 2752] = w1q[m * 2752:(m + 1) * 2752]
        w3p[m * H2:m * H2 + 2752] = w3q[m * 2752:(m + 1) * 2752]
        b1p[m * H2:m * H2 + 2752] = b1[m * 2752:(m + 1) * 2752]
        b3p[m * H2:m * H2 + 2752] = b3[m * 2752:(m + 1) * 2752]
        a2p[:, m * H2:m * H2 + 2752] = a2[:, m * 2752:(m + 1) * 2752]
    w1_dev = _relayout_blocks(np.ascontiguousarray(w1p.T), HC, 1024)
    w3_dev = _relayout_blocks(np.ascontiguousarray(w3p.T), HC, 1024)
    w1a = np.zeros((HC, 128, 1152), np.float32)
    w3a = np.zeros((HC, 128, 1152), np.float32)
    w1a[:, :, 0:1024] = w1_dev
    w3a[:, :, 0:1024] = w3_dev
    for hc in range(HC):
        w1a[hc, 0:2, 1024:1152] = b1p.T[:, 128 * hc:128 * (hc + 1)]
        w3a[hc, 0:2, 1024:1152] = b3p.T[:, 128 * hc:128 * (hc + 1)]
    w1_dev, w3_dev = w1a, w3a
    w2p = np.zeros((H2, DM), f32)
    w2p[:2752] = w2q.T
    w2_dev = _relayout_blocks(w2p, DC, H2)

    a1_dev = np.ascontiguousarray(a1.T.reshape(DC, 128, 2).transpose(1, 0, 2).reshape(128, 64))
    a3_dev = np.ascontiguousarray(a3.T.reshape(DC, 128, 2).transpose(1, 0, 2).reshape(128, 64))
    a2_dev = np.ascontiguousarray(a2p.T.reshape(HC, 128, 2).transpose(1, 0, 2).reshape(128, 2 * HC))
    b1_dev = np.ascontiguousarray(b1p.T)
    b3_dev = np.ascontiguousarray(b3p.T)
    b2_dev = np.ascontiguousarray(b2.T)

    slopes32 = np.array(SLOPES_F32, np.float64)
    sa = np.zeros((4, NH), np.float64)
    rem = slopes32.copy()
    for a in range(4):
        part = _bf(rem).astype(np.float64)
        sa[a] = part
        rem = rem - part
    k = np.arange(S, dtype=np.float64)
    k_hi, k_lo = np.floor_divide(k, 8), np.mod(k, 8)
    krows_dev = _bf(np.stack([k_hi] * 4 + [k_lo] * 4))
    bl2_dev = np.zeros((NH, 8, 128), np.float64)
    for h in range(NH):
        for a in range(4):
            bl2_dev[h, a, :] = 8.0 * sa[a, h]
            bl2_dev[h, 4 + a, :] = sa[a, h]
    bl2_dev = _bf(bl2_dev)

    qmask_cores = []
    qpos_cores = []
    xown_cores = []
    for core in range(8):
        half = core % 2
        qts = QT_HALF[half]
        qp = np.zeros((128, 8), np.float32)
        for j, t in enumerate(qts):
            qp[:, j] = 128.0 * t + np.arange(128)
        qpos_cores.append(qp)
        qm = np.zeros((128, 512), np.float32)
        for pp in range(2):
            t = qts[4 * pp]
            T = prog_T(pp, 0)
            off = t - (T - 2)
            for j, tt in enumerate(qts[4 * pp:4 * pp + 4]):
                assert tt - (prog_T(pp, j) - 2) == off
            kk = np.arange(256)
            qq = 128 * off + np.arange(128)
            tile = np.where(kk[None, :] <= qq[:, None], 0.0, NEGBIG).astype(np.float32)
            qm[:, 256 * pp:256 * (pp + 1)] = tile
        qmask_cores.append(_bf(qm))
        b = core // 2
        tok_idx = np.concatenate([np.arange(128 * t, 128 * (t + 1)) for t in qts])
        xown_cores.append(np.ascontiguousarray(x_fm[b][:, tok_idx]))

    ident = _bf(np.eye(128, dtype=np.float32))
    onesb = _bf(np.ones((128, 1), np.float32))
    ones1 = np.ones((1, 128), np.float32)

    shared = {
        "wq": wq_dev, "wkv": wkv_dev, "wo": wo_dev,
        "w1": w1_dev, "w3": w3_dev, "w2": w2_dev,
        "a1d": a1_dev, "a3d": a3_dev, "a2d": a2_dev,
        "b1d": b1_dev, "b3d": b3_dev, "b2d": b2_dev,
        "krows": krows_dev, "bl2": bl2_dev,
        "identd": ident, "onesd": onesb, "ones1d": ones1,
    }
    in_maps = []
    for core in range(8):
        m = dict(shared)
        m["x_fm"] = x_fm[core // 2]
        m["x_own"] = xown_cores[core]
        m["qpos"] = qpos_cores[core]
        m["qmask"] = qmask_cores[core]
        in_maps.append(m)
    return in_maps


def _assemble(results):
    out = np.empty((4, S, DM), np.float32)
    for core in range(8):
        half = core % 2
        b = core // 2
        qts = QT_HALF[half]
        tok_idx = np.concatenate([np.arange(128 * t, 128 * (t + 1)) for t in qts])
        yc = np.asarray(results[core]["y"], np.float32).reshape(DM, 1024)
        out[b, tok_idx, :] = yc.T
    return out


_cached = {}


def kernel(**inputs):
    import os
    debug = bool(os.environ.get("KDBG"))
    key = ("nc", debug)
    if key not in _cached:
        _cached[key] = build_program(debug=debug)
    nc = _cached[key]
    in_maps = _prep(inputs)
    res = run_bass_kernel_spmd(nc, in_maps, core_ids=list(range(8)))
    if debug:
        kernel.last_results = res.results
    return _assemble(res.results)


# version 11
# speedup vs baseline: 1.7920x; 1.0148x over previous
import math
import numpy as np
import ml_dtypes

import concourse.bass as bass
import concourse.mybir as mybir
from concourse.bass_utils import run_bass_kernel_spmd

F32 = mybir.dt.float32
BF16 = mybir.dt.bfloat16
AF = mybir.ActivationFunctionType
AX = mybir.AxisListType
OP = mybir.AluOpType

NH = 32
DC = 32
DM = 4096
S = 2048
TOK = 512
NT = 4
NPASS = 2
H2 = 2816
HC = 88
CC = 22
HEAD_SCALE = 1.0 / math.sqrt(128.0)
EPS = 1e-5
NEGBIG = -1.0e9

QT_HALF = [[0, 2, 4, 6, 9, 11, 13, 15], [1, 3, 5, 7, 8, 10, 12, 14]]

_ALIBI_START = 2.0 ** (-(2.0 ** -(math.log2(NH) - 3)))
SLOPES_F32 = [float(np.float32(_ALIBI_START ** (i + 1))) for i in range(NH)]


def prog_T(p, l):
    return (2 * l + 2) if p == 0 else (10 + 2 * l)



class Em:
    COMPUTE = ("pe", "act", "dve")
    QUEUES = ("sy", "gp")

    def __init__(self, nc):
        self.nc = nc
        self.sems = {e: nc.alloc_semaphore(f"sem_{e}") for e in self.COMPUTE}
        self.cnt = {e: 0 for e in self.COMPUTE}
        self.dq = {}
        self.ops = {e: [] for e in self.COMPUTE + self.QUEUES}
        self.buf = {}
        self.last_wait = {}

    def emit(self, e, fn, reads=(), writes=(), dsem=None):
        if e in self.QUEUES:
            assert dsem is not None
            if dsem not in self.sems:
                self.sems[dsem] = self.nc.alloc_semaphore(f"sem_{dsem}")
                self.cnt[dsem] = 0
                self.dq[dsem] = e
            assert self.dq[dsem] == e
            key, inc = dsem, 16
        else:
            assert dsem is None
            key, inc = e, 1

        waits = {}

        def add(tok):
            if tok is None:
                return
            tk, tv = tok
            if tk == e and e == "pe":
                return
            waits[tk] = max(waits.get(tk, 0), tv)

        for b in reads:
            st = self.buf.get(b)
            if st is not None:
                add(st["w"])
        for b in writes:
            st = self.buf.get(b)
            if st is not None:
                add(st["w"])
                for t in st["r"]:
                    add(t)
        lw = self.last_wait.setdefault(e, {})
        waits = {k: v for k, v in waits.items() if lw.get(k, -1) < v}
        for k, v in waits.items():
            lw[k] = v
        self.cnt[key] += inc
        tok = (key, self.cnt[key])
        self.ops[e].append((waits, fn, key, inc))
        for b in reads:
            self.buf.setdefault(b, {"w": None, "r": []})["r"].append(tok)
        for b in writes:
            self.buf[b] = {"w": tok, "r": []}
        return tok

    def carve(self, parent, children):
        st = self.buf.get(parent, {"w": None, "r": []})
        for c in children:
            self.buf[c] = {"w": st["w"], "r": list(st["r"])}

    def merge(self, parent, children):
        toks = []
        for c in children:
            st = self.buf.pop(c, None)
            if st:
                if st["w"]:
                    toks.append(st["w"])
                toks.extend(st["r"])
        self.buf[parent] = {"w": None, "r": toks}

    def coalesce(self, names, tok):
        for b in names:
            self.buf.setdefault(b, {"w": None, "r": []})["w"] = tok

    def finalize(self):
        nc = self.nc
        blk_cm = nc.Block()
        blk = blk_cm.__enter__()
        eng_attr = {"pe": "tensor", "act": "scalar", "dve": "vector",
                    "sy": "sync", "gp": "gpsimd"}
        for ename in ("sy", "gp", "pe", "act", "dve"):
            ops = self.ops[ename]
            fin = [(d, self.cnt[d]) for d, q in self.dq.items() if q == ename]

            def body(eng, _ops=ops, _fin=fin, _sems=self.sems):
                for waits, fn, key, inc in _ops:
                    for se, v in waits.items():
                        eng.wait_ge(_sems[se], v)
                    inst = fn(eng)
                    inst.then_inc(_sems[key], inc)
                for (d, v) in _fin:
                    eng.wait_ge(_sems[d], v)

            getattr(blk, eng_attr[ename])(body)
        blk_cm.__exit__(None, None, None)



def build_program(debug=False):
    nc = bass.Bass("TRN2", target_bir_lowering=False, debug=False)

    def din(name, shape, dt=F32):
        return nc.dram_tensor(name, list(shape), dt, kind="ExternalInput").ap()

    x_fm = din("x_fm", (DM, S))
    x_own = din("x_own", (DM, 1024))
    wq = din("wq", (NH, 128, DM))
    wkv = din("wkv", (2, 128, DM))
    wo = din("wo", (DC, 128, DM))
    w1 = din("w1", (HC, 128, 1152))
    w3 = din("w3", (HC, 128, 1152))
    w2 = din("w2", (DC, 128, H2))
    a1d = din("a1d", (128, 64))
    a3d = din("a3d", (128, 64))
    a2d = din("a2d", (128, 2 * HC))
    b1d = din("b1d", (2, 128 * HC))
    b3d = din("b3d", (2, 128 * HC))
    b2d = din("b2d", (2, DM))
    krows = din("krows", (8, S), BF16)
    bl2 = din("bl2", (NH, 8, 128), BF16)
    qpos = din("qpos", (128, 8))
    qmask = din("qmask", (128, 512), BF16)
    identd = din("identd", (128, 128), BF16)
    onesd = din("onesd", (128, 1), BF16)
    ones1d = din("ones1d", (1, 128))
    y = nc.dram_tensor("y", [DC, 128, 1024], F32, kind="ExternalOutput").ap()
    dbg = {}
    if debug:
        dbg["k"] = nc.dram_tensor("dbg_k", [128, S], F32, kind="ExternalOutput").ap()
        dbg["q"] = nc.dram_tensor("dbg_q", [128, 6144], F32, kind="ExternalOutput").ap()
        dbg["at"] = nc.dram_tensor("dbg_at", [128, 16384], F32, kind="ExternalOutput").ap()
        dbg["x1"] = nc.dram_tensor("dbg_x1", [128, 16384], F32, kind="ExternalOutput").ap()
        dbg["at2"] = nc.dram_tensor("dbg_at2", [128, 16384], F32, kind="ExternalOutput").ap()
        dbg["x12"] = nc.dram_tensor("dbg_x12", [128, 16384], F32, kind="ExternalOutput").ap()
        dbg["q2"] = nc.dram_tensor("dbg_q2", [128, 6144], F32, kind="ExternalOutput").ap()

    sb = nc.alloc_sbuf_tensor
    arena_a = sb("arena_a", [128, 32 * TOK], BF16)
    arena_b = sb("arena_b", [128, 32 * TOK], BF16)
    arena_q = sb("arena_q", [128, 12 * TOK], BF16)
    P_sb = sb("P_sb", [128, S], BF16)
    P2_sb = sb("P2_sb", [128, S], BF16)
    PTS = sb("PTS", [128, S], BF16)
    K_sb = sb("K_sb", [128, S], BF16)
    VT_sb = sb("VT_sb", [128, S], BF16)
    xprime = sb("xprime", [128, 32 * TOK], F32)
    XR = [sb(f"XR{i}", [128, TOK], F32) for i in range(2)]
    XB = [sb(f"XB{i}", [128, TOK], BF16) for i in range(2)]
    WA = [sb(f"WA{i}", [128, 2048], BF16) for i in range(4)]
    WB = [sb(f"WB{i}", [128, 1152], BF16) for i in range(4)]
    KR = sb("KR", [8, S], BF16)
    BL = [sb(f"BL{i}", [8, 128], BF16) for i in range(2)]
    QPOS = sb("QPOS", [128, 8], F32)
    QSQ = [sb(f"QSQ{i}", [128, TOK], BF16) for i in range(2)]
    QM = sb("QM", [128, 512], BF16)
    ID = sb("ID", [128, 128], BF16)
    ONEB = sb("ONEB", [128, 1], BF16)
    ONE1 = sb("ONE1", [1, 128], F32)
    A1 = sb("A1", [128, 64], BF16)
    A3 = sb("A3", [128, 64], BF16)
    A2 = sb("A2", [128, 2 * HC], BF16)
    T1 = sb("T1", [2, TOK], BF16)
    T3 = sb("T3", [2, TOK], BF16)
    T2 = sb("T2", [2, TOK], BF16)
    NB = sb("NB", [128, 4 * NH], F32)
    KN = sb("KN", [1, 1], F32)
    SR2 = sb("SR2", [1, TOK], F32)
    RSP = sb("RSP", [128, 8], F32)
    RS = sb("RS", [128, 1], F32)
    RR = sb("RR", [128, 1], F32)
    GSC = sb("GSC", [128, TOK], BF16)

    ps_big = nc.alloc_psum_tensor("ps_big", [128, 2048], F32)
    P4 = nc.alloc_psum_tensor("P4", [128, 512], F32)
    P5 = nc.alloc_psum_tensor("P5", [128, 512], F32)
    PT0 = nc.alloc_psum_tensor("PT0", [128, 1024], BF16)
    PT1 = nc.alloc_psum_tensor("PT1", [128, 1024], BF16)

    em = Em(nc)
    E = em.emit

    def G(i):
        return ps_big[:, 512 * i:512 * (i + 1)]

    cur = {}

    def XAr(i):
        return cur["XA"][:, TOK * i:TOK * (i + 1)]

    def ABr(i):
        return cur["AB"][:, TOK * i:TOK * (i + 1)]

    def xan(i):
        return f'{cur["xp"]}{i}'

    def abn(i):
        return f'{cur["bp"]}{i}'

    def Qr(i):
        return arena_q[:, TOK * i:TOK * (i + 1)]

    E("sy", lambda e: e.dma_start(out=ID[:], in_=identd), writes=["ID"], dsem="CS")
    E("sy", lambda e: e.dma_start(out=ONEB[:], in_=onesd), writes=["ONEB"], dsem="CS")
    E("sy", lambda e: e.dma_start(out=ONE1[:], in_=ones1d), writes=["ONE1"], dsem="CS")
    E("sy", lambda e: e.dma_start(out=QM[:], in_=qmask), writes=["QM"], dsem="CS")
    E("sy", lambda e: e.dma_start(out=KR[:], in_=krows), writes=["KR"], dsem="CS")
    tk = E("sy", lambda e: e.dma_start(out=QPOS[:], in_=qpos), writes=["QPOS"], dsem="CS")
    em.coalesce(["ID", "ONEB", "ONE1", "QM", "KR", "QPOS"], tk)
    E("gp", lambda e: e.dma_start(out=A1[:], in_=a1d), writes=["LA1"], dsem="LC")
    E("gp", lambda e: e.dma_start(out=A3[:], in_=a3d), writes=["LA3"], dsem="LC")
    tk = E("gp", lambda e: e.dma_start(out=A2[:], in_=a2d), writes=["LA2"], dsem="LC")
    em.coalesce(["LA1", "LA3", "LA2"], tk)

    wa_names = [f"WA{i}" for i in range(4)]
    wb_names = [f"WB{i}" for i in range(4)]

    def load_wa_block(src_ap, slots):
        ncols = src_ap.shape[1]
        half = ncols // 2
        for j, sl in enumerate(slots):
            E("gp",
              lambda e, s=sl, j=j, src=src_ap, h=half:
                  e.dma_start(out=WA[s][:, 0:h], in_=src[:, j * h:(j + 1) * h]),
              writes=[wa_names[sl]], dsem=f"WA{sl}")

    def q_proj_mms(h, p, wslots, part):
        xa = cur["XA"]

        def fn(e, h=h, p=p, ws=wslots, part=part, xa=xa):
            mm = None
            for c in range(8 * part, 8 * part + 8):
                sl = ws[c // 16]
                tile = WA[sl][:, 128 * (c % 16):128 * (c % 16) + 128]
                mm = e.matmul(P4[:], tile, xa[:, TOK * c:TOK * (c + 1)],
                              start=(c == 0), stop=(c == DC - 1))
            return mm
        reads = [wa_names[s] for s in set(wslots)] + [xan(c) for c in range(8 * part, 8 * part + 8)]
        E("pe", fn, reads=reads, writes=(["P4", "P4k"] if part == 0 else ["P4k"]))

    def q_evac(h):
        s = h % 12
        E("act",
          lambda e, s=s: e.activation(Qr(s), P4[:], AF.Copy, bias=0.0, scale=HEAD_SCALE),
          reads=["P4", "P4k"], writes=[f"Q{s}"])

    def negb_a(h):
        s = h % 12
        r = h % 2
        E("act", lambda e, s=s, r=r: e.activation(QSQ[r][:], Qr(s), AF.Square, bias=0.0, scale=1.0),
          reads=[f"Q{s}"], writes=[f"QSQ{r}"])

    def negb_b12(h):
        r = h % 2
        E("pe", lambda e, r=r: e.matmul(P5[0:1, :], ONEB[:], QSQ[r][:], start=True, stop=True),
          reads=[f"QSQ{r}", "ONEB"], writes=["P5", "P5k"])
        E("dve", lambda e: e.tensor_reduce(
            SR2[0:1, 0:4], P5[0:1, 0:512].rearrange("p (g k) -> p g k", k=128),
            axis=AX.X, op=OP.max),
          reads=["P5", "P5k"], writes=["SR2"])
        E("act", lambda e: e.activation(SR2[0:1, 0:4], SR2[0:1, 0:4], AF.Sqrt,
                                        bias=0.0, scale=1.0),
          reads=["SR2"], writes=["SR2"])
        E("dve", lambda e: e.tensor_scalar(SR2[0:1, 0:4], SR2[0:1, 0:4],
                                           KN[0:1, 0:1], -1.0, op0=OP.mult, op1=OP.mult),
          reads=["SR2", "KN"], writes=["SR2"])

    def negb_b3(h, p):
        E("pe", lambda e: e.matmul(P5[:, 0:4], ONE1[:], SR2[0:1, 0:4],
                                   start=True, stop=True),
          reads=["SR2", "ONE1"], writes=["P5", "P5k"])
        E("dve", lambda e, h=h, p=p: e.scalar_tensor_tensor(
            NB[:, 4 * h:4 * h + 4], QPOS[:, 4 * p:4 * p + 4], -SLOPES_F32[h],
            P5[:, 0:4], op0=OP.mult, op1=OP.add),
          reads=["QPOS", "P5", "P5k"], writes=[f"NB{h}"])

    wkv_slots = [0, 1, 2, 3]

    wa_ctr = 0
    for p in range(NPASS):
        if p == 0:
            cur.update(XA=arena_a, xp="A", AB=arena_b, bp="B")
        else:
            cur.update(XA=arena_b, xp="B", AB=arena_a, bp="A")
        for c in range(DC):
            tk = E("gp",
                   lambda e, c=c, p=p, xa=cur["XA"]: e.dma_start(
                       out=xa[:, TOK * c:TOK * (c + 1)],
                       in_=x_own[128 * c:128 * (c + 1), TOK * p:TOK * (p + 1)]),
                   writes=[xan(c)], dsem="XBL")
        em.coalesce([xan(c) for c in range(DC)], tk)

        if p == 0:
            load_wa_block(wkv[0], [0, 1])
            load_wa_block(wkv[1], [2, 3])
            wa_ctr = 2
            for half in range(2):
                for c in range(DC):
                    i = c % 4
                    kb = [abn(2 * i), abn(2 * i + 1)]
                    kvbuf = cur["AB"][:, 1024 * i:1024 * (i + 1)]
                    E("gp",
                      lambda e, c=c, half=half, kvbuf=kvbuf: e.dma_start(
                          out=kvbuf, in_=x_fm[128 * c:128 * (c + 1),
                                              1024 * half:1024 * (half + 1)]),
                      writes=kb, dsem=f"KV{i}")

                    def kv_fn(e, c=c, kvbuf=kvbuf):
                        mm = None
                        kt = WA[c // 16][:, 128 * (c % 16):128 * (c % 16) + 128]
                        vt = WA[2 + c // 16][:, 128 * (c % 16):128 * (c % 16) + 128]
                        for j in range(2):
                            mm = e.matmul(G(j), kt, kvbuf[:, 512 * j:512 * (j + 1)],
                                          start=(c == 0), stop=(c == DC - 1))
                            mm = e.matmul(G(2 + j), vt, kvbuf[:, 512 * j:512 * (j + 1)],
                                          start=(c == 0), stop=(c == DC - 1))
                        return mm
                    E("pe", kv_fn,
                      reads=kb + wa_names,
                      writes=(["G0", "G1", "G2", "G3", "Gk"] if c == 0 else ["Gk"]))
                E("act",
                  lambda e, half=half: e.activation(
                      K_sb[:, 1024 * half:1024 * (half + 1)], ps_big[:, 0:1024],
                      AF.Copy, bias=0.0, scale=1.0),
                  reads=["G0", "G1", "Gk"], writes=["K"])
                E("act",
                  lambda e: e.activation(P_sb[:, 0:1024], ps_big[:, 1024:2048],
                                         AF.Copy, bias=0.0, scale=1.0),
                  reads=["G2", "G3", "Gk"], writes=["P0"])

                def vtr_fn(e, half=half):
                    mm = None
                    for k in range(8):
                        mm = e.transpose(PT0[:, 128 * k:128 * (k + 1)],
                                         P_sb[:, 128 * k:128 * (k + 1)], ID[:])
                    return mm
                E("pe", vtr_fn, reads=["P0", "ID"], writes=["PT0"])
                E("dve",
                  lambda e, half=half: e.tensor_copy(
                      VT_sb[:, 1024 * half:1024 * (half + 1)], PT0[:]),
                  reads=["PT0"], writes=["VT"])

            for c in range(4):
                E("act", lambda e, c=c: e.activation(QSQ[c % 2][:], K_sb[:, 512 * c:512 * (c + 1)],
                                                     AF.Square, bias=0.0, scale=1.0),
                  reads=["K"], writes=[f"QSQ{c % 2}"])
                E("pe", lambda e, c=c: e.matmul(ps_big[0:1, 512 * c:512 * (c + 1)],
                                                ONEB[:], QSQ[c % 2][:], start=True, stop=True),
                  reads=[f"QSQ{c % 2}", "ONEB"], writes=[f"G{c}"])
            E("dve", lambda e: e.reduce_max(KN[0:1, 0:1], ps_big[0:1, :], axis=AX.X),
              reads=["G0", "G1", "G2", "G3"], writes=["KN"])
            E("act", lambda e: e.activation(KN[0:1, 0:1], KN[0:1, 0:1], AF.Sqrt,
                                            bias=0.0, scale=1.0),
              reads=["KN"], writes=["KN"])
            E("dve", lambda e: e.tensor_scalar_mul(KN[0:1, 0:1], KN[0:1, 0:1], 1.0005),
              reads=["KN"], writes=["KN"])
            if debug:
                E("gp", lambda e: e.dma_start(out=dbg["k"], in_=K_sb[:]), reads=["K"], dsem="DBGK")

        nbq = []
        for h in range(12):
            blk_src = wq[h]
            slots = [(2 * wa_ctr) % 4, (2 * wa_ctr + 1) % 4]
            wa_ctr += 1
            load_wa_block(blk_src, slots)
            for part in range(4):
                q_proj_mms(h, p, slots, part)
            if nbq:
                negb_b3(nbq.pop(0), p)
            q_evac(h)
            negb_a(h)
            negb_b12(h)
            nbq.append(h)

        nb12q = []
        if p == 0:
            em.carve("P0", ["P0a", "P1a"])
            em.carve("P1", ["P0b", "P1b"])
        for h in range(NH):
            s = h % 12
            bl = h % 2
            E("sy", lambda e, h=h, bl=bl: e.dma_start(out=BL[bl][:], in_=bl2[h]),
              writes=[f"BL{bl}"], dsem=f"BLD{bl}")

            fill = (1 <= h <= 20)
            if fill:
                fh = h + 11
                fslots = [(2 * wa_ctr) % 4, (2 * wa_ctr + 1) % 4]
                wa_ctr += 1
                load_wa_block(wq[fh], fslots)

            pt_ctr = 0

            def geom(l):
                T = prog_T(p, l)
                L = 128 * T
                nsl = (L + 511) // 512
                return T, L, nsl

            def bank(l, sl):
                return 2 * (l % 2) + sl % 2

            def pbuf(l):
                if p == 0:
                    t = (P_sb, P2_sb)[l % 2]
                    off = 1024 * (l // 2)
                    return (t[:, off:off + 1024], ("P0a", "P0b", "P1a", "P1b")[2 * (l // 2) + (l % 2)])
                return (P_sb[:, 0:2048], "P0") if l % 2 == 0 else (P2_sb[:, 0:2048], "P1")

            def emit_S_slice(l, sl, s=s, bl=bl):
                T, L, nsl = geom(l)
                c0 = 512 * sl
                c1 = min(512 * (sl + 1), L)
                bk = bank(l, sl)

                def s_fn(e, s=s, l=l, c0=c0, c1=c1, bk=bk, bl=bl):
                    e.matmul(G(bk)[:, 0:c1 - c0],
                             Qr(s)[:, 128 * l:128 * (l + 1)],
                             K_sb[:, c0:c1], start=True, stop=False)
                    return e.matmul(G(bk)[:, 0:c1 - c0],
                                    BL[bl][:], KR[0:8, c0:c1], start=False, stop=True)
                E("pe", s_fn, reads=[f"Q{s}", "K", f"BL{bl}"], writes=[f"G{bk}"])
                if sl == nsl - 1:
                    w = c1 - c0
                    E("dve",
                      lambda e, w=w, p=p, bk=bk: e.tensor_tensor(
                          G(bk)[:, w - 256:w], G(bk)[:, w - 256:w],
                          QM[:, 256 * p:256 * (p + 1)], op=OP.add),
                      reads=["QM", f"G{bk}"], writes=[f"G{bk}"])

            def emit_exp(l, sl, h=h):
                T, L, nsl = geom(l)
                pb, pn = pbuf(l)
                c0 = 512 * sl
                c1 = min(512 * (sl + 1), L)
                bk = bank(l, sl)
                ri = 4 * (l % 2) + sl
                E("act",
                  lambda e, c0=c0, c1=c1, h=h, l=l, sl=sl, bk=bk, pb=pb, ri=ri: e.activation(
                      pb[:, c0:c1], G(bk)[:, 0:c1 - c0], AF.Exp,
                      bias=NB[:, 4 * h + l:4 * h + l + 1], scale=1.0,
                      accum_out=RSP[:, ri:ri + 1]),
                  reads=[f"G{bk}", f"NB{h}"], writes=[pn, f"RSP{ri}"])

            def emit_tile_chain(l):
                T, L, nsl = geom(l)
                pb, pn = pbuf(l)
                r0 = 4 * (l % 2)
                E("dve", lambda e, nsl=nsl, r0=r0: e.reduce_sum(RS[:], RSP[:, r0:r0 + nsl], axis=AX.X),
                  reads=[f"RSP{r0 + i}" for i in range(nsl)], writes=["RS"])
                E("dve", lambda e: e.reciprocal(RR[:], RS[:]), reads=["RS"], writes=["RR"])
                E("dve", lambda e, L=L, pb=pb: e.tensor_scalar_mul(pb[:, 0:L], pb[:, 0:L], RR[:]),
                  reads=[pn, "RR"], writes=[pn])

            def emit_tile_full(l):
                T, L, nsl = geom(l)
                for sl in range(nsl):
                    emit_S_slice(l, sl)
                    emit_exp(l, sl)
                emit_tile_chain(l)

            def emit_tile_pair(la, lb):
                Ta, La, nsla = geom(la)
                Tb, Lb, nslb = geom(lb)
                for sl in range(max(nsla, nslb)):
                    if sl < nsla:
                        emit_S_slice(la, sl)
                    if sl < nslb:
                        emit_S_slice(lb, sl)
                    if sl < nsla:
                        emit_exp(la, sl)
                    if sl < nslb:
                        emit_exp(lb, sl)
                emit_tile_chain(la)
                emit_tile_chain(lb)

            def emit_tile_start(l):
                T, L, nsl = geom(l)
                for sl in range(min(2, nsl)):
                    emit_S_slice(l, sl)

            def emit_tile_rest(l):
                T, L, nsl = geom(l)
                for sl in range(min(2, nsl)):
                    emit_exp(l, sl)
                for sl in range(2, nsl):
                    emit_S_slice(l, sl)
                    emit_exp(l, sl)
                emit_tile_chain(l)

            def emit_T(l):
                nonlocal pt_ctr
                T, L, nsl = geom(l)
                pb, pn = pbuf(l)
                for g in range((T + 7) // 8):
                    k0 = 8 * g
                    k1 = min(8 * g + 8, T)
                    pt = pt_ctr % 2
                    pt_ctr += 1

                    def tr_fn(e, k0=k0, k1=k1, pt=pt, pb=pb):
                        mm = None
                        for k in range(k0, k1):
                            mm = e.transpose((PT0 if pt == 0 else PT1)[:, 128 * (k - k0):128 * (k - k0 + 1)],
                                             pb[:, 128 * k:128 * (k + 1)], ID[:])
                        return mm
                    E("pe", tr_fn, reads=[pn, "ID"], writes=[f"PT{pt}"])
                    E("dve",
                      lambda e, k0=k0, k1=k1, pt=pt: e.tensor_copy(
                          PTS[:, 128 * k0:128 * k1],
                          (PT0 if pt == 0 else PT1)[:, 0:128 * (k1 - k0)]),
                      reads=[f"PT{pt}"], writes=["PTS"])

            def emit_AV(l):
                T, L, nsl = geom(l)

                def av_fn(e, T=T, l=l):
                    mm = None
                    for kt in range(T):
                        mm = e.matmul(P5[:, 128 * l:128 * (l + 1)],
                                      VT_sb[:, 128 * kt:128 * (kt + 1)],
                                      PTS[:, 128 * kt:128 * (kt + 1)],
                                      start=(kt == 0), stop=(kt == T - 1))
                    return mm
                E("pe", av_fn, reads=["PTS", "VT"],
                  writes=(["P5", "P5k"] if l == 0 else ["P5k"]))

            emit_tile_pair(0, 1)
            if fill:
                q_proj_mms(fh, p, fslots, 0)
            if nb12q:
                bh = nb12q.pop(0)
                negb_b12(bh)
                nbq.append(bh)
            emit_T(0)
            if nbq:
                negb_b3(nbq.pop(0), p)
            emit_tile_start(2)
            emit_tile_rest(2)
            emit_AV(0)
            emit_T(1)
            emit_tile_start(3)
            emit_tile_rest(3)
            emit_AV(1)
            if fill:
                q_proj_mms(fh, p, fslots, 1)
            emit_T(2)
            if fill:
                q_proj_mms(fh, p, fslots, 2)
            emit_AV(2)
            emit_T(3)
            if fill:
                q_proj_mms(fh, p, fslots, 3)
                q_evac(fh)
            emit_AV(3)
            E("act", lambda e, h=h, ab=cur["AB"]: e.activation(
                ab[:, TOK * h:TOK * (h + 1)], P5[:], AF.Copy, bias=0.0, scale=1.0),
              reads=["P5", "P5k"], writes=[abn(h)])
            if fill:
                negb_a(fh)
                nb12q.append(fh)

        if p == 0:
            em.merge("P0", ["P0a", "P1a"])
            em.merge("P1", ["P0b", "P1b"])
        if debug:
            E("gp", lambda e, p=p: e.dma_start(out=dbg["q" if p == 0 else "q2"], in_=arena_q[:]),
              reads=[f"Q{i}" for i in range(12)], dsem="DBGQ")
            E("gp", lambda e, p=p, ab=cur["AB"]: e.dma_start(
                out=dbg["at" if p == 0 else "at2"], in_=ab[:]),
              reads=[abn(i) for i in range(NH)], dsem="DBGA")

        stq = []
        em.carve("PTS", ["PTc", "PTd", "PTe", "PTf"])
        em.carve("P0", ["PSa", "PSb"])
        em.carve("P1", ["GSa", "GSb", "XR2"])
        XBS = [(XB[0][:], "XB0"), (XB[1][:], "XB1"),
               (PTS[:, 0:512], "PTc"), (PTS[:, 512:1024], "PTd")]
        SQS = [(QSQ[0][:], "QSQ0"), (QSQ[1][:], "QSQ1"),
               (PTS[:, 1024:1536], "PTe"), (PTS[:, 1536:2048], "PTf")]
        G1S = [(XAr(30), xan(30)), (XAr(31), xan(31)),
               (P_sb[:, 0:512], "PSa"), (P_sb[:, 512:1024], "PSb")]
        GSCS = [(GSC[:], "GSC"), (P2_sb[:, 0:512], "GSa"), (P2_sb[:, 512:1024], "GSb")]
        XRS = [(XR[0][:], "XR0"), (XR[1][:], "XR1"),
               (P2_sb[:, 1024:2048].bitcast(F32), "XR2")]

        def stats_a(c):
            xb, xbn = XBS[c % 4]
            sq, sqn = SQS[c % 4]
            E("act", lambda e, c=c, xb=xb: e.activation(xb, xprime[:, TOK * c:TOK * (c + 1)],
                                                        AF.Copy, bias=0.0, scale=1.0),
              reads=[f"X{c}"], writes=[xbn])
            E("act", lambda e, c=c, sq=sq: e.activation(sq, xprime[:, TOK * c:TOK * (c + 1)],
                                                        AF.Square, bias=0.0, scale=1.0),
              reads=[f"X{c}"], writes=[sqn])

        def stats_b(c):
            xb, xbn = XBS[c % 4]
            sq, sqn = SQS[c % 4]
            E("pe", lambda e, c=c, xb=xb: e.matmul(ps_big[0:1, 0:512], ONEB[:], xb,
                                                   start=(c == 0), stop=(c == DC - 1)),
              reads=[xbn, "ONEB"], writes=(["G0", "Gk"] if c == 0 else ["Gk"]))
            E("pe", lambda e, c=c, sq=sq: e.matmul(ps_big[0:1, 512:1024], ONEB[:], sq,
                                                   start=(c == 0), stop=(c == DC - 1)),
              reads=[sqn, "ONEB"], writes=(["G1"] if c == 0 else ["Gk"]))

        def ln_rows_and_apply(out_to_ob):
            r0 = XR[0][0:1, :]
            r1 = XR[1][0:1, :]
            E("act", lambda e, r0=r0: e.activation(r0, ps_big[0:1, 0:512], AF.Copy,
                                                   bias=0.0, scale=1.0 / DM),
              reads=["G0", "Gk"], writes=["XR0"])
            E("act", lambda e, r1=r1: e.activation(r1, ps_big[0:1, 512:1024], AF.Copy,
                                                   bias=0.0, scale=1.0 / DM),
              reads=["G1", "Gk"], writes=["XR1"])
            E("act", lambda e, r0=r0: e.activation(SR2[:], r0, AF.Square, bias=0.0, scale=1.0),
              reads=["XR0"], writes=["SR2"])
            E("dve", lambda e, r1=r1: e.tensor_tensor(r1, r1, SR2[:], op=OP.subtract),
              reads=["XR1", "SR2"], writes=["XR1"])
            E("dve", lambda e, r1=r1: e.tensor_scalar_add(r1, r1, EPS),
              reads=["XR1"], writes=["XR1"])
            E("act", lambda e, r1=r1: e.activation(r1, r1, AF.Sqrt, bias=0.0, scale=1.0),
              reads=["XR1"], writes=["XR1"])
            E("dve", lambda e, r1=r1: e.reciprocal(r1, r1), reads=["XR1"], writes=["XR1"])
            E("dve", lambda e, r0=r0, r1=r1: e.tensor_tensor(r0, r0, r1, op=OP.mult),
              reads=["XR0", "XR1"], writes=["XR0"])
            E("pe", lambda e, r1=r1: e.matmul(ps_big[:, 1024:1536], ONE1[:], r1,
                                              start=True, stop=True),
              reads=["XR1", "ONE1"], writes=["G2"])
            E("pe", lambda e, r0=r0: e.matmul(ps_big[:, 1536:2048], ONE1[:], r0,
                                              start=True, stop=True),
              reads=["XR0", "ONE1"], writes=["G3"])
            for c in range(DC):
                if not out_to_ob:
                    E("dve", lambda e, c=c: e.tensor_tensor(
                        xprime[:, TOK * c:TOK * (c + 1)], xprime[:, TOK * c:TOK * (c + 1)],
                        ps_big[:, 1024:1536], op=OP.mult),
                      reads=[f"X{c}", "G2"], writes=[f"X{c}"])
                    E("dve", lambda e, c=c: e.tensor_tensor(
                        xprime[:, TOK * c:TOK * (c + 1)], xprime[:, TOK * c:TOK * (c + 1)],
                        ps_big[:, 1536:2048], op=OP.subtract),
                      reads=[f"X{c}", "G3"], writes=[f"X{c}"])
                else:
                    xr, xrn = XRS[c % 3]
                    E("dve", lambda e, c=c, xr=xr: e.tensor_tensor(
                        xr, xprime[:, TOK * c:TOK * (c + 1)],
                        ps_big[:, 1024:1536], op=OP.mult),
                      reads=[f"X{c}", "G2"], writes=[xrn])
                    E("dve", lambda e, c=c, xr=xr: e.tensor_tensor(
                        xr, xr, ps_big[:, 1536:2048], op=OP.subtract),
                      reads=[xrn, "G3"], writes=[xrn])
                    E("sy", lambda e, c=c, xr=xr, p=p: e.dma_start(
                        out=y[c][:, TOK * p:TOK * (p + 1)], in_=xr),
                      reads=[xrn], dsem=f"O{xrn}")

        for do in range(DC):
            slots = [(2 * wa_ctr) % 4, (2 * wa_ctr + 1) % 4]
            wa_ctr += 1
            load_wa_block(wo[do], slots)
            pso = [ps_big[:, 1024:1536], ps_big[:, 1536:2048], P4[:], P5[:]][do % 4]
            pn = ["G2", "G3", "P4", "P5"][do % 4]

            def wo_fn(e, slots=slots, pso=pso, ab=cur["AB"]):
                mm = None
                for c in range(DC):
                    sl = slots[c // 16]
                    mm = e.matmul(pso, WA[sl][:, 128 * (c % 16):128 * (c % 16) + 128],
                                  ab[:, TOK * c:TOK * (c + 1)], start=(c == 0), stop=(c == DC - 1))
                return mm
            E("pe", wo_fn,
              reads=[wa_names[s] for s in set(slots)] + [abn(c) for c in range(DC)],
              writes=[pn])
            xr, xrn = XRS[do % 3]
            E("sy", lambda e, do=do, p=p, xr=xr: e.dma_start(
                out=xr, in_=x_own[128 * do:128 * (do + 1), TOK * p:TOK * (p + 1)]),
              writes=[xrn], dsem=f"R{xrn}")
            E("dve", lambda e, do=do, pso=pso, xr=xr: e.tensor_tensor(
                xprime[:, TOK * do:TOK * (do + 1)], xr, pso, op=OP.add),
              reads=[pn, xrn], writes=[f"X{do}"])
            if len(stq) >= 3:
                stats_b(stq.pop(0))
            stats_a(do)
            stq.append(do)

        while stq:
            stats_b(stq.pop(0))
        ln_rows_and_apply(out_to_ob=False)
        if debug:
            E("gp", lambda e, p=p: e.dma_start(out=dbg["x1" if p == 0 else "x12"], in_=xprime[:]),
              reads=[f"X{c}" for c in range(DC)], dsem="DBGX")

        for jc in range(8):
            E("dve", lambda e, jc=jc: e.tensor_tensor(
                XR[0][:], xprime[:, TOK * jc:TOK * (jc + 1)],
                xprime[:, TOK * (8 + jc):TOK * (9 + jc)], op=OP.add),
              reads=[f"X{jc}", f"X{8 + jc}"], writes=["XR0"])
            E("dve", lambda e, jc=jc: e.tensor_tensor(
                XR[0][:], XR[0][:], xprime[:, TOK * (16 + jc):TOK * (17 + jc)], op=OP.add),
              reads=["XR0", f"X{16 + jc}"], writes=["XR0"])
            E("dve", lambda e, jc=jc, xa=cur["XA"]: e.tensor_tensor(
                xa[:, TOK * jc:TOK * (jc + 1)], XR[0][:],
                xprime[:, TOK * (24 + jc):TOK * (25 + jc)], op=OP.add),
              reads=["XR0", f"X{24 + jc}"], writes=[xan(jc)])
        for c in range(DC):
            xb, xbn = XBS[c % 4]
            E("act", lambda e, c=c, xb=xb: e.activation(xb, xprime[:, TOK * c:TOK * (c + 1)],
                                                        AF.Copy, bias=0.0, scale=1.0),
              reads=[f"X{c}"], writes=[xbn])
            E("pe", lambda e, c=c, xb=xb: e.matmul(P4[0:2, :], A1[:, 2 * c:2 * c + 2], xb,
                                            start=(c == 0), stop=(c == DC - 1)),
              reads=[xbn, "LA1"], writes=(["P4", "P4k"] if c == 0 else ["P4k"]))
            E("pe", lambda e, c=c, xb=xb: e.matmul(P5[0:2, :], A3[:, 2 * c:2 * c + 2], xb,
                                            start=(c == 0), stop=(c == DC - 1)),
              reads=[xbn, "LA3"], writes=(["P5", "P5k"] if c == 0 else ["P5k"]))
        E("act", lambda e: e.activation(T1[:], P4[0:2, :], AF.Copy, bias=0.0, scale=0.5),
          reads=["P4", "P4k"], writes=["T1"])
        E("act", lambda e: e.activation(T3[:], P5[0:2, :], AF.Copy, bias=0.0, scale=0.5),
          reads=["P5", "P5k"], writes=["T3"])

        wb_ctr = 0
        t2q = []

        def emit_t2(hc, grd, gap):
            E("pe", lambda e, hc=hc, gap=gap: e.matmul(
                P4[0:2, :], A2[:, 2 * hc:2 * hc + 2], gap,
                start=(hc == 0), stop=(hc == HC - 1)),
              reads=[grd, "LA2"], writes=(["P4", "P4k"] if hc == 0 else ["P4k"]))

        for hc in range(HC):
            m, cc = hc // CC, hc % CC
            s1 = wb_ctr % 4
            s3 = (wb_ctr + 1) % 4
            wb_ctr += 2
            E("gp", lambda e, hc=hc, s1=s1: e.dma_start(out=WB[s1][:], in_=w1[hc]),
              writes=[wb_names[s1]], dsem=f"WB{s1}")
            E("gp", lambda e, hc=hc, s3=s3: e.dma_start(out=WB[s3][:], in_=w3[hc]),
              writes=[wb_names[s3]], dsem=f"WB{s3}")
            gb = 2 * (hc % 2)
            ub = gb + 1

            def gate_fn(e, hc=hc, s1=s1, gb=gb, xa=cur["XA"]):
                for jc in range(8):
                    e.matmul(G(gb), WB[s1][:, 128 * jc:128 * (jc + 1)],
                             xa[:, TOK * jc:TOK * (jc + 1)],
                             start=(jc == 0), stop=False)
                return e.matmul(G(gb), WB[s1][0:2, 1024:1152], T1[:],
                                start=False, stop=True)
            E("pe", gate_fn,
              reads=[wb_names[s1], "T1"] + [xan(j) for j in range(8)],
              writes=[f"G{gb}"])

            def up_fn(e, hc=hc, s3=s3, ub=ub, xa=cur["XA"]):
                for jc in range(8):
                    e.matmul(G(ub), WB[s3][:, 128 * jc:128 * (jc + 1)],
                             xa[:, TOK * jc:TOK * (jc + 1)],
                             start=(jc == 0), stop=False)
                return e.matmul(G(ub), WB[s3][0:2, 1024:1152], T3[:],
                                start=False, stop=True)
            E("pe", up_fn,
              reads=[wb_names[s3], "T3"] + [xan(j) for j in range(8)],
              writes=[f"G{ub}"])
            if t2q:
                emit_t2(*t2q.pop())
            g1, g1n = G1S[hc % 4]
            E("act", lambda e, gb=gb, g1=g1: e.activation(
                g1, G(gb), AF.Silu, bias=0.0, scale=1.0),
              reads=[f"G{gb}"], writes=[g1n])
            gname = xan(8 + cc)
            if m == 0:
                E("dve", lambda e, ub=ub, cc=cc, g1=g1, xa=cur["XA"]: e.tensor_tensor(
                    xa[:, TOK * (8 + cc):TOK * (9 + cc)], g1, G(ub), op=OP.mult),
                  reads=[g1n, f"G{ub}"], writes=[gname])
                t2q.append((hc, gname, XAr(8 + cc)))
            else:
                gs, gsn = GSCS[hc % 3]
                E("dve", lambda e, ub=ub, g1=g1, gs=gs: e.tensor_tensor(
                    gs, g1, G(ub), op=OP.mult),
                  reads=[g1n, f"G{ub}"], writes=[gsn])
                E("dve", lambda e, cc=cc, gs=gs, xa=cur["XA"]: e.tensor_tensor(
                    xa[:, TOK * (8 + cc):TOK * (9 + cc)],
                    xa[:, TOK * (8 + cc):TOK * (9 + cc)], gs, op=OP.add),
                  reads=[gname, gsn], writes=[gname])
                t2q.append((hc, gsn, gs))
        while t2q:
            emit_t2(*t2q.pop())
        E("act", lambda e: e.activation(T2[:], P4[0:2, :], AF.Copy, bias=0.0, scale=0.5),
          reads=["P4", "P4k"], writes=["T2"])

        for do in range(DC):
            slots = [(2 * wa_ctr) % 4, (2 * wa_ctr + 1) % 4]
            wa_ctr += 1
            for j, sl in enumerate(slots):
                E("gp", lambda e, do=do, j=j, sl=sl: e.dma_start(
                    out=WA[sl][:, 0:1408], in_=w2[do][:, 1408 * j:1408 * (j + 1)]),
                  writes=[wa_names[sl]], dsem=f"WA{sl}")
            E("gp", lambda e, do=do, sl=slots[0]: e.dma_start(
                out=WA[sl][0:2, 1408:1536], in_=b2d[:, 128 * do:128 * (do + 1)]),
              writes=[wa_names[slots[0]]], dsem=f"WA{slots[0]}")
            pso = [ps_big[:, 1024:1536], ps_big[:, 1536:2048], P4[:], P5[:]][do % 4]
            pn = ["G2", "G3", "P4", "P5"][do % 4]

            def down_fn(e, slots=slots, pso=pso, do=do, xa=cur["XA"]):
                for cc in range(CC):
                    sl = slots[cc // 11]
                    e.matmul(pso, WA[sl][:, 128 * (cc % 11):128 * (cc % 11) + 128],
                             xa[:, TOK * (8 + cc):TOK * (9 + cc)], start=(cc == 0), stop=False)
                return e.matmul(pso, WA[slots[0]][0:2, 1408:1536], T2[:],
                                start=False, stop=True)
            E("pe", down_fn,
              reads=[wa_names[s] for s in set(slots)] + ["T2"] +
                    [xan(8 + cc) for cc in range(CC)],
              writes=[pn, "P4k" if do % 2 == 0 else "P5k"])
            E("dve", lambda e, do=do, pso=pso: e.tensor_tensor(
                xprime[:, TOK * do:TOK * (do + 1)], xprime[:, TOK * do:TOK * (do + 1)],
                pso, op=OP.add),
              reads=[pn, f"X{do}"], writes=[f"X{do}"])
            if len(stq) >= 3:
                stats_b(stq.pop(0))
            stats_a(do)
            stq.append(do)

        while stq:
            stats_b(stq.pop(0))
        ln_rows_and_apply(out_to_ob=True)
        em.merge("PTS", ["PTc", "PTd", "PTe", "PTf"])
        em.merge("P0", ["PSa", "PSb"])
        em.merge("P1", ["GSa", "GSb", "XR2"])

    em.finalize()
    nc.finalize()
    return nc



def _relayout_blocks(wT, n_out_chunks, K):
    blocks = wT.reshape(K // 128, 128, n_out_chunks, 128)
    blocks = blocks.transpose(2, 1, 0, 3).reshape(n_out_chunks, 128, K)
    return np.ascontiguousarray(blocks)


def _bf(x):
    return np.asarray(x, dtype=ml_dtypes.bfloat16)


def _prep(inputs):
    f32 = np.float32
    X = np.asarray(inputs["L0_attn_hidden_states"], f32)
    Wq = np.asarray(inputs["L0_attn_Wq"], f32)
    Wkv = np.asarray(inputs["L0_attn_Wkv"], f32)
    Wo = np.asarray(inputs["L0_attn_Wo"], f32)
    w1q = np.asarray(inputs["L0_ffn_w1q"], f32)
    a1 = np.asarray(inputs["L0_ffn_a1"], f32)
    b1 = np.asarray(inputs["L0_ffn_b1"], f32)
    w3q = np.asarray(inputs["L0_ffn_w3q"], f32)
    a3 = np.asarray(inputs["L0_ffn_a3"], f32)
    b3 = np.asarray(inputs["L0_ffn_b3"], f32)
    w2q = np.asarray(inputs["L0_ffn_w2q"], f32)
    a2 = np.asarray(inputs["L0_ffn_a2"], f32)
    b2 = np.asarray(inputs["L0_ffn_b2"], f32)

    x_fm = np.ascontiguousarray(X.transpose(0, 2, 1))

    wq_dev = _relayout_blocks(Wq, NH, DM)
    wkv_dev = _relayout_blocks(Wkv, 2, DM)
    wo_dev = _relayout_blocks(Wo, DC, DM)

    w1p = np.zeros((4 * H2, 1024), f32)
    w3p = np.zeros((4 * H2, 1024), f32)
    b1p = np.zeros((4 * H2, 2), f32)
    b3p = np.zeros((4 * H2, 2), f32)
    a2p = np.zeros((2, 4 * H2), f32)
    for m in range(4):
        w1p[m * H2:m * H2 + 2752] = w1q[m * 2752:(m + 1) * 2752]
        w3p[m * H2:m * H2 + 2752] = w3q[m * 2752:(m + 1) * 2752]
        b1p[m * H2:m * H2 + 2752] = b1[m * 2752:(m + 1) * 2752]
        b3p[m * H2:m * H2 + 2752] = b3[m * 2752:(m + 1) * 2752]
        a2p[:, m * H2:m * H2 + 2752] = a2[:, m * 2752:(m + 1) * 2752]
    w1_dev = _relayout_blocks(np.ascontiguousarray(w1p.T), HC, 1024)
    w3_dev = _relayout_blocks(np.ascontiguousarray(w3p.T), HC, 1024)
    w1a = np.zeros((HC, 128, 1152), np.float32)
    w3a = np.zeros((HC, 128, 1152), np.float32)
    w1a[:, :, 0:1024] = w1_dev
    w3a[:, :, 0:1024] = w3_dev
    for hc in range(HC):
        w1a[hc, 0:2, 1024:1152] = b1p.T[:, 128 * hc:128 * (hc + 1)]
        w3a[hc, 0:2, 1024:1152] = b3p.T[:, 128 * hc:128 * (hc + 1)]
    w1_dev, w3_dev = w1a, w3a
    w2p = np.zeros((H2, DM), f32)
    w2p[:2752] = w2q.T
    w2_dev = _relayout_blocks(w2p, DC, H2)

    a1_dev = np.ascontiguousarray(a1.T.reshape(DC, 128, 2).transpose(1, 0, 2).reshape(128, 64))
    a3_dev = np.ascontiguousarray(a3.T.reshape(DC, 128, 2).transpose(1, 0, 2).reshape(128, 64))
    a2_dev = np.ascontiguousarray(a2p.T.reshape(HC, 128, 2).transpose(1, 0, 2).reshape(128, 2 * HC))
    b1_dev = np.ascontiguousarray(b1p.T)
    b3_dev = np.ascontiguousarray(b3p.T)
    b2_dev = np.ascontiguousarray(b2.T)

    slopes32 = np.array(SLOPES_F32, np.float64)
    sa = np.zeros((4, NH), np.float64)
    rem = slopes32.copy()
    for a in range(4):
        part = _bf(rem).astype(np.float64)
        sa[a] = part
        rem = rem - part
    k = np.arange(S, dtype=np.float64)
    k_hi, k_lo = np.floor_divide(k, 8), np.mod(k, 8)
    krows_dev = _bf(np.stack([k_hi] * 4 + [k_lo] * 4))
    bl2_dev = np.zeros((NH, 8, 128), np.float64)
    for h in range(NH):
        for a in range(4):
            bl2_dev[h, a, :] = 8.0 * sa[a, h]
            bl2_dev[h, 4 + a, :] = sa[a, h]
    bl2_dev = _bf(bl2_dev)

    qmask_cores = []
    qpos_cores = []
    xown_cores = []
    for core in range(8):
        half = core % 2
        qts = QT_HALF[half]
        qp = np.zeros((128, 8), np.float32)
        for j, t in enumerate(qts):
            qp[:, j] = 128.0 * t + np.arange(128)
        qpos_cores.append(qp)
        qm = np.zeros((128, 512), np.float32)
        for pp in range(2):
            t = qts[4 * pp]
            T = prog_T(pp, 0)
            off = t - (T - 2)
            for j, tt in enumerate(qts[4 * pp:4 * pp + 4]):
                assert tt - (prog_T(pp, j) - 2) == off
            kk = np.arange(256)
            qq = 128 * off + np.arange(128)
            tile = np.where(kk[None, :] <= qq[:, None], 0.0, NEGBIG).astype(np.float32)
            qm[:, 256 * pp:256 * (pp + 1)] = tile
        qmask_cores.append(_bf(qm))
        b = core // 2
        tok_idx = np.concatenate([np.arange(128 * t, 128 * (t + 1)) for t in qts])
        xown_cores.append(np.ascontiguousarray(x_fm[b][:, tok_idx]))

    ident = _bf(np.eye(128, dtype=np.float32))
    onesb = _bf(np.ones((128, 1), np.float32))
    ones1 = np.ones((1, 128), np.float32)

    shared = {
        "wq": wq_dev, "wkv": wkv_dev, "wo": wo_dev,
        "w1": w1_dev, "w3": w3_dev, "w2": w2_dev,
        "a1d": a1_dev, "a3d": a3_dev, "a2d": a2_dev,
        "b1d": b1_dev, "b3d": b3_dev, "b2d": b2_dev,
        "krows": krows_dev, "bl2": bl2_dev,
        "identd": ident, "onesd": onesb, "ones1d": ones1,
    }
    in_maps = []
    for core in range(8):
        m = dict(shared)
        m["x_fm"] = x_fm[core // 2]
        m["x_own"] = xown_cores[core]
        m["qpos"] = qpos_cores[core]
        m["qmask"] = qmask_cores[core]
        in_maps.append(m)
    return in_maps


def _assemble(results):
    out = np.empty((4, S, DM), np.float32)
    for core in range(8):
        half = core % 2
        b = core // 2
        qts = QT_HALF[half]
        tok_idx = np.concatenate([np.arange(128 * t, 128 * (t + 1)) for t in qts])
        yc = np.asarray(results[core]["y"], np.float32).reshape(DM, 1024)
        out[b, tok_idx, :] = yc.T
    return out


_cached = {}


def kernel(**inputs):
    import os
    debug = bool(os.environ.get("KDBG"))
    key = ("nc", debug)
    if key not in _cached:
        _cached[key] = build_program(debug=debug)
    nc = _cached[key]
    in_maps = _prep(inputs)
    res = run_bass_kernel_spmd(nc, in_maps, core_ids=list(range(8)))
    if debug:
        kernel.last_results = res.results
    return _assemble(res.results)
